# version 16
# baseline (speedup 1.0000x reference)
import numpy as np
import ml_dtypes
import concourse.bass as bass
import concourse.mybir as mybir
from concourse.bass_utils import run_bass_kernel_spmd
from contextlib import ExitStack

F32 = mybir.dt.float32
BF16 = mybir.dt.bfloat16
U32 = mybir.dt.uint32
U8 = mybir.dt.uint8
I32 = mybir.dt.int32
AF = mybir.ActivationFunctionType
ALU = mybir.AluOpType
AXX = mybir.AxisListType.X

T = 2048
H = 1024
D = 64
NB = 2
NT = T // 128
NCH = H // 128
KSEL = 256
NEG = -1e30
N_BISECT = 60

TPI = 2.0 * np.pi
TPI_HI = float(np.float32(TPI))
TPI_LO = float(np.float32(TPI - np.float64(np.float32(TPI))))
PI_F = float(np.float32(np.pi))
HALFPI = float(np.float32(np.pi / 2))


class EM:
    ENGS = ("PE", "ACT", "DVE", "POOL", "SP")

    DMA_SEMS = ("dx0", "dx1", "dxn", "dout", "dpos", "dcon", "dconp",
                "dw0", "dw1", "dwo", "dg0", "dg1", "dtap")

    def __init__(self):
        self.ops = []
        self.last_w = {}
        self.readers = {}
        self.floor = None
        self.dma_idx = {s: 0 for s in self.DMA_SEMS}

    def op(self, eng, fn, r=(), w=(), dma=None, tag=""):
        o = {"eng": eng, "fn": fn, "dma": dma, "deps": [], "inc": False,
             "tag": tag, "i": len(self.ops)}
        if dma is not None:
            self.dma_idx[dma] += 1
            o["dma_i"] = self.dma_idx[dma]
        deps = {}
        if self.floor is not None:
            deps[self.floor["i"]] = self.floor
        for k in r:
            d = self.last_w.get(k)
            if d is not None:
                deps[d["i"]] = d
        for k in w:
            d = self.last_w.get(k)
            if d is not None:
                deps[d["i"]] = d
            for rd in self.readers.get(k, ()):
                deps[rd["i"]] = rd
        for d in deps.values():
            if d is o:
                continue
            if d["dma"] is None and d["eng"] == "PE" and eng == "PE" and o["dma"] is None:
                continue
            d["inc"] = True
            o["deps"].append(d)
        for k in w:
            self.last_w[k] = o
            self.readers[k] = []
        wset = set(w)
        for k in r:
            if k not in wset:
                self.readers.setdefault(k, []).append(o)
        self.ops.append(o)
        return o

    def pe(self, fn, r=(), w=(), tag=""):
        return self.op("PE", fn, r, w, tag=tag)

    def act(self, fn, r=(), w=(), tag=""):
        return self.op("ACT", fn, r, w, tag=tag)

    def dve(self, fn, r=(), w=(), tag=""):
        return self.op("DVE", fn, r, w, tag=tag)

    def pool(self, fn, r=(), w=(), tag=""):
        return self.op("POOL", fn, r, w, tag=tag)

    def dma_sp(self, fn, r=(), w=(), sem="dcon", tag=""):
        return self.op("SP", fn, r, w, dma=sem, tag=tag)

    def dma_pool(self, fn, r=(), w=(), sem="dconp", tag=""):
        return self.op("POOL", fn, r, w, dma=sem, tag=tag)

    def fence_keys(self, keys, eng="POOL"):
        return self.op(eng, lambda e: e.nop(), r=tuple(keys), w=tuple(keys), tag="kfence")

    def barrier(self):
        o = self.op("SP", lambda e: e.nop(), (), (), tag="fence")
        deps = {}
        for d in self.last_w.values():
            deps[d["i"]] = d
        for rl in self.readers.values():
            for d in rl:
                deps[d["i"]] = d
        o["deps"] = []
        for d in deps.values():
            if d is o:
                continue
            d["inc"] = True
            o["deps"].append(d)
        o["inc"] = True
        self.last_w = {k: o for k in self.last_w}
        self.readers = {}
        self.floor = o
        return o

    def finalize_into(self, nc, sems, dma_sems):
        cnt = {e: 0 for e in self.ENGS}
        for o in self.ops:
            if o["dma"] is None and o["inc"]:
                cnt[o["eng"]] += 1
                o["inc_no"] = cnt[o["eng"]]
        for o in self.ops:
            waits = {}
            for d in o["deps"]:
                if d["dma"] is not None:
                    s, v = d["dma"], 16 * d["dma_i"]
                else:
                    s, v = d["eng"], d["inc_no"]
                if waits.get(s, 0) < v:
                    waits[s] = v
            o["waits"] = waits
        allsems = dict(sems)
        allsems.update(dma_sems)

        def emit_for(ename, eng):
            wm = {}
            for o in self.ops:
                if o["eng"] != ename:
                    continue
                for s, v in sorted(o["waits"].items()):
                    if wm.get(s, 0) >= v:
                        continue
                    eng.wait_ge(allsems[s], v)
                    wm[s] = v
                inst = o["fn"](eng)
                if o["dma"] is not None:
                    inst.then_inc(allsems[o["dma"]], 16)
                elif o["inc"]:
                    inst.then_inc(allsems[ename], 1)

        with nc.Block() as block:
            @block.tensor
            def _(e):
                emit_for("PE", e)

            @block.scalar
            def _(e):
                emit_for("ACT", e)

            @block.vector
            def _(e):
                emit_for("DVE", e)

            @block.gpsimd
            def _(e):
                emit_for("POOL", e)

            @block.sync
            def _(e):
                emit_for("SP", e)


def _chunk_major(a, nchunk):
    rows, C = a.shape
    assert rows == nchunk * 128
    return np.ascontiguousarray(
        a.reshape(nchunk, 128, C).transpose(1, 0, 2).reshape(128, nchunk * C))


def prep_host(inp):
    h = {}
    h["wqT"] = _chunk_major(np.ascontiguousarray(np.asarray(inp["L0_attn_wq"]).T), 8)
    h["wkT"] = _chunk_major(np.ascontiguousarray(np.asarray(inp["L0_attn_wk"]).T), 8)
    h["wvT"] = _chunk_major(np.ascontiguousarray(np.asarray(inp["L0_attn_wv"]).T), 8)
    h["woT"] = _chunk_major(np.ascontiguousarray(np.asarray(inp["L0_attn_wo"]).T), 8)
    dw = np.asarray(inp["L0_ffn_dense_wqkv"])
    blocks = []
    for p in range(4):
        blocks.append(np.concatenate(
            [dw[:, 2 * p, 0:64], dw[:, 2 * p + 1, 0:64],
             dw[:, 2 * p, 64:128], dw[:, 2 * p + 1, 64:128],
             dw[:, 2 * p, 128:192], dw[:, 2 * p + 1, 128:192]], axis=1))
    h["dwqkv"] = _chunk_major(np.concatenate(blocks, axis=1), 8)
    mw = np.asarray(inp["L0_ffn_mosa_wqkv"])
    blocks = []
    for hh in range(8):
        blocks.append(np.concatenate(
            [mw[:, hh, 0:64], mw[:, hh, 64:128], mw[:, hh, 128:192]], axis=1))
    h["mwqkv"] = _chunk_major(np.concatenate(blocks, axis=1), 8)
    h["dwo"] = _chunk_major(np.asarray(inp["L0_ffn_dense_wo"]), 4)
    h["mwo"] = _chunk_major(np.asarray(inp["L0_ffn_mosa_wo"]), 4)
    h["router"] = _chunk_major(np.asarray(inp["L0_ffn_mosa_router_w"]), 8)
    return h


def host_consts():
    c = {}
    ident = np.eye(128, dtype=np.float32)
    c["identb"] = ident.astype(ml_dtypes.bfloat16)
    c["identf"] = ident
    c["cmaskb"] = (np.tril(np.ones((128, 128), dtype=np.float32), -1) * NEG).astype(ml_dtypes.bfloat16)
    c["umaskb"] = np.triu(np.ones((128, 128), dtype=np.float32)).astype(ml_dtypes.bfloat16)
    bo = np.zeros((128, 2), dtype=np.float32)
    bo[0:64, 0] = 1.0
    bo[64:128, 1] = 1.0
    c["blockones"] = bo.astype(ml_dtypes.bfloat16)
    c["onesrow"] = np.ones((1, 128), dtype=np.float32)
    j = np.arange(16, dtype=np.float64)
    v = (10000.0 ** (-j / 16.0)) / (2.0 * np.pi)
    hi = v.astype(np.float32)
    lo = (v - hi.astype(np.float64)).astype(np.float32)
    c["invfh"] = np.ascontiguousarray(np.broadcast_to(hi, (128, 16)))
    c["invfl"] = np.ascontiguousarray(np.broadcast_to(lo, (128, 16)))
    c["iota_row"] = np.ascontiguousarray(np.broadcast_to(np.arange(T, dtype=np.float32), (128, T)))
    c["iota1"] = np.ascontiguousarray(np.broadcast_to(np.arange(1, KSEL + 1, dtype=np.float32), (128, KSEL)))
    tok = (np.arange(NT)[None, :] * 128 + np.arange(128)[:, None]).astype(np.float32)
    c["tokidx"] = np.ascontiguousarray(tok)
    return c


def build_nc(en_attn=True, en_dense=True, en_mosa=True, taps=()):
    nc = bass.Bass("TRN2", target_bir_lowering=False)

    def din(name, shape, dtype=F32):
        return nc.dram_tensor(name, list(shape), dtype, kind="ExternalInput")

    x_in = din("x", (NB, T, H))
    d_wqT = din("wqT", (128, 8192))
    d_wkT = din("wkT", (128, 8192))
    d_wvT = din("wvT", (128, 8192))
    d_woT = din("woT", (128, 8192))
    d_dwqkv = din("dwqkv", (128, 8 * 1536))
    d_mwqkv = din("mwqkv", (128, 8 * 1536))
    d_dwo = din("dwo", (128, 4096))
    d_mwo = din("mwo", (128, 4096))
    d_router = din("router", (128, 64))
    d_identb = din("identb", (128, 128), BF16)
    d_identf = din("identf", (128, 128), F32)
    d_cmaskb = din("cmaskb", (128, 128), BF16)
    d_umaskb = din("umaskb", (128, 128), BF16)
    d_blockones = din("blockones", (128, 2), BF16)
    d_onesrow = din("onesrow", (1, 128))
    d_invfh = din("invfh", (128, 16))
    d_invfl = din("invfl", (128, 16))
    d_iota_row = din("iota_row", (128, T))
    d_iota1 = din("iota1", (128, KSEL))
    d_tokidx = din("tokidx", (128, NT))

    out_t = nc.dram_tensor("out", [NB, T, H], F32, kind="ExternalOutput")
    xn_scr = nc.dram_tensor("xn_scr", [NB, T, H], BF16)

    tap_outs = {}
    tap_req = list(taps)
    ctx = ExitStack()
    with ctx:
        def S(name, shape, dtype, stk=None):
            return (stk or ctx).enter_context(nc.sbuf_tensor(name, list(shape), dtype))

        def P(name, shape, dtype):
            return ctx.enter_context(nc.psum_tensor(name, list(shape), dtype))

        def tap(nm, tsb):
            if nm in tap_req:
                tap_outs[nm] = (nc.dram_tensor("dbg_" + nm, [tsb.shape[0], tsb.shape[1]],
                                               tsb.dtype, kind="ExternalOutput"), tsb)

        def tapdump(nm, tsb, rkeys):
            if nm not in tap_req:
                return
            dram = nc.dram_tensor("dbg_" + nm, [tsb.shape[0], tsb.shape[1]],
                                  tsb.dtype, kind="ExternalOutput")
            em.dma_sp(lambda e, dram=dram, tsb=tsb: e.dma_start(dram[:, :], tsb[:]),
                      r=tuple(rkeys), w=(), sem="dtap", tag="tapd:" + nm)

        c_id = S("c_id", (128, 128), BF16)
        c_idf = S("c_idf", (128, 128), F32)
        c_cm = S("c_cm", (128, 128), BF16)
        c_um = S("c_um", (128, 128), BF16)
        c_bo = S("c_bo", (128, 2), BF16)
        c_or = S("c_or", (1, 128), F32)
        c_invfh = S("c_invfh", (128, 16), F32)
        c_invfl = S("c_invfl", (128, 16), F32)
        c_iota1 = S("c_iota1", (128, KSEL), F32)
        c_tok = S("c_tok", (128, NT), F32)
        c_router = S("c_router", (128, 64), BF16)
        z_eps = S("z_eps", (128, 1), F32)
        z_one = S("z_one", (128, 1), F32)

        xt = [S(f"xt{i}", (128, H), F32) for i in range(2)]
        xnt = [S(f"xnt{i}", (128, H), BF16) for i in range(2)]
        junk2 = S("junk2", (128, T), BF16)
        ln_s = S("ln_s", (128, 16), F32)

        r_scores = S("r_scores", (8, T), F32)
        b_lo = [S(f"b_lo{i}", (8, 1), F32) for i in range(2)]
        b_hi = [S(f"b_hi{i}", (8, 1), F32) for i in range(2)]
        b_mid = S("b_mid", (8, 1), F32)
        b_cnt = S("b_cnt", (8, 1), F32)
        b_cond = S("b_cond", (8, 1), U8)
        b_c4 = S("b_c4", (8, 4), F32)

        av_attn = [S(f"av_attn{p}", (128, T), BF16) for p in range(8)]
        av_dense = [S(f"av_dense{p}", (128, T), BF16) for p in range(4)]

        g_tm = S("g_tm", (128, 128), F32)
        sc_tm = S("sc_tm", (128, NT * 9), F32)

        nrm_s = S("nrm_s", (128, 8), F32)
        rb_sb = [S(f"rb_sb{i}", (64, 512), F32) for i in range(2)]
        vtr_a = S("vtr_a", (32, 32), F32)
        vtr_b = S("vtr_b", (32, 32), F32)
        bias_hd = [[S(f"bias_hd{s}_{i}", (128, 1), F32) for i in range(2)] for s in range(2)]
        out_sb = [S(f"out_sb{i}", (128, H), F32) for i in range(2)]
        rts = [S(f"rts{i}", (128, 16), F32) for i in range(3)]
        rti = S("rti", (128, 16), I32)
        rp3 = [S(f"rp3{i}", (128, 16), BF16) for i in range(3)]
        pos2b = S("pos2b", (2, 128), F32)
        cos_tm = S("cos_tm", (128, NT * 16), BF16)
        sin_tm = S("sin_tm", (128, NT * 16), BF16)
        mcos_tm = S("mcos_tm", (128, 2 * 16), BF16)
        msin_tm = S("msin_tm", (128, 2 * 16), BF16)
        qtmb = [S(f"qtmb{i}", (128, 128), BF16) for i in range(2)]

        pSx = P("pSx", (128, 1536), F32)
        pS2 = [pSx[:, i * 512:(i + 1) * 512] for i in range(3)]
        pav = P("pav", (128, 1024), F32)
        ptr = P("ptr", (128, 512), BF16)
        pm1 = P("pm1", (128, 512), F32)
        pm2 = P("pm2", (128, 512), F32)

        sems = {e: ctx.enter_context(nc.semaphore("sem_" + e)) for e in EM.ENGS}
        dma_sems = {s: ctx.enter_context(nc.semaphore("sem_" + s)) for s in EM.DMA_SEMS}

        em = EM()

        tap("r_scores", r_scores)
        tap("g_tm", g_tm)
        tap("sc_tm", sc_tm)

        def ldc(dst, src, key, pool=False):
            f = (lambda e, dd=dst, ss=src: e.dma_start(dd, ss))
            (em.dma_pool if pool else em.dma_sp)(f, r=(), w=(key,), tag="ld:" + key)

        ldc(c_id[:], d_identb[:, :], "c_id")
        ldc(c_idf[:], d_identf[:, :], "c_idf")
        ldc(c_cm[:], d_cmaskb[:, :], "c_cm")
        ldc(c_um[:], d_umaskb[:, :], "c_um")
        ldc(c_bo[:], d_blockones[:, :], "c_bo")
        ldc(c_or[:], d_onesrow[:, :], "c_or")
        ldc(c_invfh[:], d_invfh[:, :], "c_invfh")
        ldc(c_invfl[:], d_invfl[:, :], "c_invfl")
        ldc(c_iota1[:], d_iota1[:, :], "c_iota1")
        ldc(c_tok[:], d_tokidx[:, :], "c_tok")
        ldc(c_router[:], d_router[:, :], "c_router", pool=True)
        em.dve(lambda e: e.memset(z_eps[:], 1e-5), w=("z_eps",))
        em.dve(lambda e: e.memset(z_one[:], 1.0), w=("z_one",))
        em.barrier()

        def rope_tables_tm(cos_tm, sin_tm, pos_col, pos_keys, ntile, kp):
            for ti in range(ntile):
                pap = pos_col(ti)
                em.dve(lambda e, pap=pap: e.tensor_scalar_mul(rts[1][:], c_invfl[:], pap),
                       r=("c_invfl",) + pos_keys, w=(kp + "1",))
                em.dve(lambda e, pap=pap: e.scalar_tensor_tensor(
                    rts[0][:], c_invfh[:], pap, rts[1][:], op0=ALU.mult, op1=ALU.add),
                    r=("c_invfh", kp + "1") + pos_keys, w=(kp + "0",))
                em.dve(lambda e: e.tensor_copy(rti[:], rts[0][:]),
                       r=(kp + "0",), w=(kp + "i",))
                em.dve(lambda e: e.tensor_copy(rts[2][:], rti[:]),
                       r=(kp + "i",), w=(kp + "2",))
                em.dve(lambda e: e.tensor_tensor(rts[1][:], rts[0][:], rts[2][:], op=ALU.subtract),
                       r=(kp + "0", kp + "2"), w=(kp + "1",))
                em.dve(lambda e: e.tensor_scalar_mul(rts[2][:], rts[1][:], TPI_LO),
                       r=(kp + "1",), w=(kp + "2",))
                em.dve(lambda e: e.scalar_tensor_tensor(
                    rts[0][:], rts[1][:], TPI_HI, rts[2][:], op0=ALU.mult, op1=ALU.add),
                    r=(kp + "1", kp + "2"), w=(kp + "0",))
                em.act(lambda e, ti=ti: e.activation(sin_tm[:, ti * 16:(ti + 1) * 16], rts[0][:], AF.Sin),
                       r=(kp + "0",), w=(kp + "sin",))
                em.dve(lambda e: e.tensor_scalar_add(rts[1][:], rts[0][:], HALFPI),
                       r=(kp + "0",), w=(kp + "1",))
                em.dve(lambda e: e.tensor_single_scalar(rts[2][:], rts[1][:], PI_F, op=ALU.is_gt),
                       r=(kp + "1",), w=(kp + "2",))
                em.dve(lambda e: e.tensor_scalar_mul(rts[2][:], rts[2][:], TPI),
                       r=(kp + "2",), w=(kp + "2",))
                em.dve(lambda e: e.tensor_tensor(rts[1][:], rts[1][:], rts[2][:], op=ALU.subtract),
                       r=(kp + "1", kp + "2"), w=(kp + "1",))
                em.act(lambda e, ti=ti: e.activation(cos_tm[:, ti * 16:(ti + 1) * 16], rts[1][:], AF.Sin),
                       r=(kp + "1",), w=(kp + "cos",))

        def rope_apply_tm(tile, tkey, col0, cos_sl, sin_sl, ck, sk):
            x1 = tile[:, col0:col0 + 16]
            x2 = tile[:, col0 + 16:col0 + 32]
            em.dve(lambda e: e.tensor_tensor(rp3[0][:], x1, cos_sl, op=ALU.mult),
                   r=(tkey, ck), w=("rp0",))
            em.dve(lambda e: e.tensor_tensor(rp3[1][:], x2, sin_sl, op=ALU.mult),
                   r=(tkey, sk), w=("rp1",))
            em.dve(lambda e: e.tensor_tensor(rp3[2][:], x1, sin_sl, op=ALU.mult),
                   r=(tkey, sk), w=("rp2",))
            em.dve(lambda e: e.tensor_tensor(x1, rp3[0][:], rp3[1][:], op=ALU.subtract),
                   r=("rp0", "rp1"), w=(tkey,))
            em.dve(lambda e: e.tensor_tensor(rp3[1][:], x2, cos_sl, op=ALU.mult),
                   r=(tkey, ck), w=("rp1",))
            em.dve(lambda e: e.tensor_tensor(x2, rp3[2][:], rp3[1][:], op=ALU.add),
                   r=("rp2", "rp1"), w=(tkey,))

        bg_queue = []

        def drain_bg(k):
            for _ in range(min(k, len(bg_queue))):
                bg_queue.pop(0)()

        rope_done = [False]

        for b in range(NB):
            xn_scr_key = f"xnscr{b}"
            scA = ExitStack()
            xn_fm = S(f"xn_fm{b}", (128, NCH * T), BF16, scA)
            r_mask = S(f"r_mask{b}", (8, T), BF16, scA)
            r_grid = S(f"r_grid{b}", (8, T), F32, scA)
            wbuf = [S(f"wbuf{b}_{i}", (128, 3072), BF16, scA) for i in range(2)]
            q_pair = [S(f"q_pair{b}_{i}", (128, T), BF16, scA) for i in range(2)]
            k_pair = [S(f"k_pair{b}_{i}", (128, T), BF16, scA) for i in range(2)]
            v_pair = [S(f"v_pair{b}_{i}", (128, NT * 192), BF16, scA) for i in range(2)]
            qsq = [S(f"qsq{b}_{i}", (128, T), BF16, scA) for i in range(2)]
            pt3 = [S(f"pt{b}_{i}", (128, 1024), BF16, scA) for i in range(4)]

            for sl_ in range(2):
                em.pool(lambda e, vp=v_pair[sl_]: e.memset(
                    vp[:].rearrange("p (k s) -> p k s", s=192)[:, :, 64:128], 1.0),
                    w=(f"v_ones{sl_}",), tag="vones")

            if en_dense and not rope_done[0]:
                rope_tables_tm(cos_tm, sin_tm, lambda ti: c_tok[:, ti:ti + 1],
                               ("c_tok",), NT, "rtk")
                rope_done[0] = True

            fq = []

            def drain_front(k):
                for _ in range(min(k, len(fq))):
                    fq.pop(0)()

            evq = []
            ev_par = [0]

            def drain_ev(k):
                for _ in range(min(k, len(evq))):
                    evq.pop(0)()

            fr_ctr = [0]

            def fpm():
                fr_ctr[0] += 1
                return (pm1, "pm1") if fr_ctr[0] % 2 == 0 else (pm2, "pm2")

            def front_closures(branch, p, sl):
                cl = []
                wb, wbkey = wbuf[sl], f"wbuf{sl}"
                qp, qk = q_pair[sl], f"q_pair{sl}"
                kp_, kk = k_pair[sl], f"k_pair{sl}"
                vp = v_pair[sl]
                qs, qsk = qsq[sl], f"qsq{sl}"

                if branch == "attn":
                    def add_w():
                        for wi, (dsrc, off) in enumerate(((d_wqT, 0), (d_wkT, 1024), (d_wvT, 2048))):
                            ap = dsrc[:, :].rearrange("p (c e) -> p c e", c=8)[:, :, p * 128:(p + 1) * 128]
                            em.dma_pool(lambda e, ap=ap, off=off, wb=wb: e.dma_start(
                                wb[:, off:off + 1024].rearrange("p (c e) -> p c e", c=8), ap),
                                r=(), w=(wbkey,), sem=f"dw{sl}", tag=f"ldw.a{p}.{wi}")
                    cl.append(add_w)

                    def wq_sl(c, wb=wb):
                        return wb[:, c * 128:(c + 1) * 128]

                    def wk_sl(c, wb=wb):
                        return wb[:, 1024 + c * 128:1024 + (c + 1) * 128]

                    def wv_sl(c, wb=wb):
                        return wb[:, 2048 + c * 128:2048 + (c + 1) * 128]
                else:
                    def add_w():
                        ap = d_dwqkv[:, :].rearrange("p (c e) -> p c e", c=8)[:, :, p * 384:(p + 1) * 384]
                        em.dma_pool(lambda e, ap=ap, wb=wb: e.dma_start(
                            wb[:, 0:3072].rearrange("p (c e) -> p c e", c=8), ap),
                            r=(), w=(wbkey,), sem=f"dw{sl}", tag=f"ldw.d{p}")
                    cl.append(add_w)

                    def wq_sl(c, wb=wb):
                        return wb[:, c * 384:c * 384 + 128]

                    def wk_sl(c, wb=wb):
                        return wb[:, c * 384 + 128:c * 384 + 256]

                    def wv_sl(c, wb=wb):
                        return wb[:, c * 384 + 256:c * 384 + 384]

                if branch == "attn":
                    for ns in range(4):
                        for (dst, dkey, wsl) in ((qp, qk, wq_sl), (kp_, kk, wk_sl)):
                            def projqk(dst=dst, dkey=dkey, wsl=wsl, ns=ns):
                                pm, pmk = fpm()
                                xnk = tuple(f"xnfm.{t4}.{g}" for t4 in range(ns * 4, ns * 4 + 4)
                                            for g in range(2))
                                for c in range(NCH):
                                    em.pe(lambda e, c=c, ns=ns, wsl=wsl, xn_fm=xn_fm, pm=pm: e.matmul(
                                        pm[:, 0:512], wsl(c), xn_fm[:, c * T + ns * 512:c * T + (ns + 1) * 512],
                                        start=(c == 0), stop=(c == NCH - 1)),
                                        r=(wbkey,) + xnk, w=(pmk,), tag="prj")
                                em.act(lambda e, ns=ns, dst=dst, pm=pm: e.activation(
                                    dst[:, ns * 512:(ns + 1) * 512], pm[:, 0:512], AF.Copy),
                                    r=(pmk,), w=(dkey,))
                            cl.append(projqk)
                else:
                    for (dst, dkey, wsl) in ((qp, qk, wq_sl), (kp_, kk, wk_sl)):
                        pend_b = [None]
                        for tt in range(NT):
                            def projtm_a(dst=dst, dkey=dkey, wsl=wsl, tt=tt, pend_b=pend_b):
                                pm, pmk = fpm()
                                qb, qbk = qtmb[tt % 2], f"qtmb{tt % 2}"
                                xnk = (f"xnfm.{tt}.0", f"xnfm.{tt}.1")
                                for c in range(NCH):
                                    em.pe(lambda e, c=c, tt=tt, wsl=wsl, xn_fm=xn_fm, pm=pm: e.matmul(
                                        pm[:, 0:128], xn_fm[:, c * T + tt * 128:c * T + (tt + 1) * 128], wsl(c),
                                        start=(c == 0), stop=(c == NCH - 1)),
                                        r=(wbkey,) + xnk, w=(pmk,), tag="prjtm")
                                em.act(lambda e, qb=qb, pm=pm: e.activation(qb[:], pm[:, 0:128], AF.Copy),
                                       r=(pmk,), w=(qbk,))
                                for hh in range(2):
                                    rope_apply_tm(qb, qbk, hh * 64,
                                                  cos_tm[:, tt * 16:(tt + 1) * 16],
                                                  sin_tm[:, tt * 16:(tt + 1) * 16], "rtkcos", "rtksin")

                                def projtm_b(qb=qb, qbk=qbk, tt=tt, dst=dst, dkey=dkey):
                                    em.pe(lambda e, qb=qb, tt=tt: e.transpose(
                                        ptr[:, (tt % 4) * 128:(tt % 4 + 1) * 128], qb[:], c_id[:]),
                                        r=(qbk, "c_id"), w=("ptr",), tag="trQ")
                                    if tt % 4 == 3:
                                        g = tt // 4
                                        em.act(lambda e, g=g, dst=dst: e.activation(
                                            dst[:, g * 512:(g + 1) * 512], ptr[:, 0:512], AF.Copy),
                                            r=("ptr",), w=(dkey,))
                                pend_b[0] = projtm_b

                            def projtm_ab(tt=tt, pend_b=pend_b, projtm_a=projtm_a):
                                bprev = pend_b[0]
                                projtm_a()
                                if bprev is not None:
                                    bprev()
                            cl.append(projtm_ab)

                        def projtm_flush(pend_b=pend_b):
                            if pend_b[0] is not None:
                                pend_b[0]()
                                pend_b[0] = None
                        cl.append(projtm_flush)
                for tt in range(NT):
                    def projv(tt=tt, wsl=wv_sl, vp=vp):
                        pm, pmk = fpm()
                        xnk = (f"xnfm.{tt}.0", f"xnfm.{tt}.1")
                        for c in range(NCH):
                            em.pe(lambda e, c=c, tt=tt, wsl=wsl, xn_fm=xn_fm, pm=pm: e.matmul(
                                pm[:, 0:128], xn_fm[:, c * T + tt * 128:c * T + (tt + 1) * 128], wsl(c),
                                start=(c == 0), stop=(c == NCH - 1)),
                                r=(wbkey,) + xnk, w=(pmk,), tag="prj.v")
                        em.dve(lambda e, tt=tt, vp=vp, pm=pm: e.tensor_copy(
                            vp[:, tt * 192:(tt + 1) * 192].rearrange(
                                "p (g c) -> p g c", c=64)[:, 0:3:2, :],
                            pm[:, 0:128].rearrange("p (g c) -> p g c", c=64)),
                            r=(pmk,), w=(f"v_pair{sl}.{tt}",))
                    cl.append(projv)

                def mk_msq(src_t, skey):
                    def msq(src_t=src_t, skey=skey):
                        em.act(lambda e, src_t=src_t, qs=qs: e.activation(qs[:], src_t[:], AF.Square),
                               r=(skey,), w=(qsk,))
                    return msq

                def mk_mred(col):
                    def mred(col=col):
                        for ns in range(4):
                            pm, pmk = fpm()
                            em.pe(lambda e, ns=ns, qs=qs, pm=pm: e.matmul(
                                pm[0:2, 0:512], c_bo[:], qs[:, ns * 512:(ns + 1) * 512],
                                start=True, stop=True),
                                r=("c_bo", qsk), w=(pmk,))
                            em.dve(lambda e, ns=ns, col=col, pm=pm: e.tensor_reduce(
                                nrm_s[0:2, col + ns:col + ns + 1], pm[0:2, 0:512], axis=AXX, op=ALU.max),
                                r=(pmk,), w=(f"nrm{col + ns}",))
                    return mred

                def mnorm_fin():
                    em.dve(lambda e: e.tensor_reduce(vtr_a[0:2, 2:3], nrm_s[0:2, 0:4], axis=AXX, op=ALU.max),
                           r=("nrm0", "nrm1", "nrm2", "nrm3"), w=("nq",))
                    em.dve(lambda e: e.tensor_reduce(vtr_a[0:2, 3:4], nrm_s[0:2, 4:8], axis=AXX, op=ALU.max),
                           r=("nrm4", "nrm5", "nrm6", "nrm7"), w=("nk",))
                    em.act(lambda e: e.activation(vtr_a[0:2, 4:5], vtr_a[0:2, 2:3], AF.Ln),
                           r=("nq",), w=("lnq",))
                    em.act(lambda e: e.activation(vtr_a[0:2, 5:6], vtr_a[0:2, 3:4], AF.Ln),
                           r=("nk",), w=("lnk",))
                    em.dve(lambda e: e.tensor_tensor(vtr_a[0:2, 6:7], vtr_a[0:2, 4:5], vtr_a[0:2, 5:6], op=ALU.add),
                           r=("lnq", "lnk"), w=("lnsum",))
                    em.act(lambda e: e.activation(vtr_a[0:2, 0:1], vtr_a[0:2, 6:7], AF.Exp, bias=0.0, scale=0.5),
                           r=("lnsum",), w=("vtra",))
                    em.dve(lambda e: e.tensor_scalar_mul(vtr_a[0:2, 0:1], vtr_a[0:2, 0:1], -0.125),
                           r=("vtra",), w=("vtra",))
                    em.dve(lambda e: e.transpose(vtr_b[:], vtr_a[:]), r=("vtra",), w=("vtrb",))
                    em.dve(lambda e: e.tensor_reduce(vtr_b[0:1, 2:3], vtr_b[0:1, 0:2],
                                                     axis=AXX, op=ALU.min),
                           r=("vtrb",), w=("vtrb",))

                def mnorm_bc():
                    em.pe(lambda e: e.matmul(pm1[:, 0:1], c_or[:], vtr_b[0:1, 2:3],
                                             start=True, stop=True),
                          r=("vtrb", "c_or"), w=("pm1",), tag="bcM")
                    em.act(lambda e: e.activation(bias_hd[sl][0][:], pm1[:, 0:1], AF.Copy),
                           r=("pm1",), w=(f"bias{sl}.0",))
                cl.append(mk_msq(qp, qk))
                cl.append(mk_mred(0))
                cl.append(mk_msq(kp_, kk))
                cl.append(mk_mred(4))
                cl.append(mnorm_fin)
                cl.append(mnorm_bc)
                return cl

            seq = []
            if en_attn:
                seq += [("attn", p, av_attn[p], f"ava{p}") for p in range(8)]
            if en_dense:
                seq += [("dense", p, av_dense[p], f"avd{p}") for p in range(4)]
            if seq:
                fq.extend(front_closures(seq[0][0], seq[0][1], 0))

            for tt in range(NT):
                xb, xkey = xt[tt % 2], f"xt{tt % 2}"
                xnb, xnkey = xnt[tt % 2], f"xnt{tt % 2}"
                src = x_in[b, tt * 128:(tt + 1) * 128, :]
                em.dma_sp(lambda e, d=xb, s=src: e.dma_start(d[:], s), r=(), w=(xkey,),
                          sem=f"dx{tt % 2}", tag=f"ldx{b}.{tt}")
                o = (tt % 2) * 8
                pk = str(tt % 2)
                em.dve(lambda e, xb=xb, o=o: e.tensor_reduce(ln_s[:, o:o + 1], xb[:], axis=AXX, op=ALU.add),
                       r=(xkey,), w=("ln0" + pk,))
                em.act(lambda e, xb=xb, o=o, tt=tt: e.activation(
                    junk2[:, (tt % 2) * H:(tt % 2) * H + H], xb[:], AF.Square, accum_out=ln_s[:, o + 1:o + 2]),
                    r=(xkey,), w=("ln1" + pk, "junk2." + pk))
                em.dve(lambda e, o=o: e.tensor_scalar_mul(ln_s[:, o + 2:o + 3], ln_s[:, o:o + 1], 1.0 / H),
                       r=("ln0" + pk,), w=("ln2" + pk,))
                em.dve(lambda e, o=o: e.tensor_tensor(ln_s[:, o + 3:o + 4], ln_s[:, o + 2:o + 3],
                                                      ln_s[:, o + 2:o + 3], op=ALU.mult),
                       r=("ln2" + pk,), w=("ln3" + pk,))
                em.dve(lambda e, o=o: e.scalar_tensor_tensor(ln_s[:, o + 4:o + 5], ln_s[:, o + 1:o + 2],
                                                             1.0 / H, ln_s[:, o + 3:o + 4],
                                                             op0=ALU.mult, op1=ALU.subtract),
                       r=("ln1" + pk, "ln3" + pk), w=("ln4" + pk,))
                em.act(lambda e, o=o: e.activation(ln_s[:, o + 5:o + 6], ln_s[:, o + 4:o + 5],
                                                   AF.Ln, bias=z_eps[:], scale=1.0),
                       r=("ln4" + pk, "z_eps"), w=("ln5" + pk,))
                em.act(lambda e, o=o: e.activation(ln_s[:, o + 6:o + 7], ln_s[:, o + 5:o + 6],
                                                   AF.Exp, bias=0.0, scale=-0.5),
                       r=("ln5" + pk,), w=("ln6" + pk,))
                em.dve(lambda e, xb=xb, xnb=xnb, o=o: e.scalar_tensor_tensor(
                    xnb[:], xb[:], ln_s[:, o + 2:o + 3], ln_s[:, o + 6:o + 7].to_broadcast([128, H]),
                    op0=ALU.subtract, op1=ALU.mult),
                    r=(xkey, "ln2" + pk, "ln6" + pk), w=(xnkey,))
                em.dma_sp(lambda e, xnb=xnb, d=xn_scr[b, tt * 128:(tt + 1) * 128, :]: e.dma_start(d, xnb[:]),
                          r=(xnkey,), w=(xn_scr_key + f".{tt}",), sem="dxn", tag=f"stxn{b}.{tt}")
                for g in range(2):
                    for c4 in range(4):
                        c = g * 4 + c4
                        em.pe(lambda e, xnb=xnb, c=c, c4=c4: e.transpose(
                            ptr[:, c4 * 128:(c4 + 1) * 128], xnb[:, c * 128:(c + 1) * 128], c_id[:]),
                            r=(xnkey, "c_id"), w=("ptr",))
                    em.dve(lambda e, g=g, tt=tt, xn_fm=xn_fm: e.tensor_copy(
                        xn_fm[:].rearrange("p (c t) -> p c t", c=NCH)[:, g * 4:(g + 1) * 4, tt * 128:(tt + 1) * 128],
                        ptr[:].rearrange("p (c t) -> p c t", c=4)),
                        r=("ptr",), w=(f"xnfm.{tt}.{g}",))
                if tt == 0:
                    drain_front(1)
                elif tt % 4 == 3:
                    drain_front(2)
            xn_keys = tuple(f"xnfm.{tt}.{g}" for tt in range(NT) for g in range(2))

            if en_mosa:
                for ns in range(4):
                    for c in range(NCH):
                        em.pe(lambda e, ns=ns, c=c, xn_fm=xn_fm: e.matmul(
                            pm1[0:8, 0:512], c_router[:, c * 8:(c + 1) * 8],
                            xn_fm[:, c * T + ns * 512:c * T + (ns + 1) * 512],
                            start=(c == 0), stop=(c == NCH - 1)),
                            r=("c_router",) + xn_keys, w=("pm1",), tag="router")
                    em.act(lambda e, ns=ns: e.activation(
                        r_scores[:, ns * 512:(ns + 1) * 512], pm1[0:8, 0:512], AF.Copy),
                        r=("pm1",), w=("r_scores",), tag="rsc")

                def bis_init():
                    em.dve(lambda e: e.tensor_reduce(b_hi[0][:], r_scores[:], axis=AXX, op=ALU.max),
                           r=("r_scores",), w=("b_hi0",))
                    em.dve(lambda e: e.tensor_reduce(b_lo[0][:], r_scores[:], axis=AXX, op=ALU.min),
                           r=("b_lo0", "r_scores"), w=("b_lo0",))
                    em.dve(lambda e: e.tensor_scalar_add(b_lo[0][:], b_lo[0][:], -1.0),
                           r=("b_lo0",), w=("b_lo0",))
                    em.dve(lambda e: e.tensor_scalar(b_mid[:], b_lo[0][:], b_hi[0][:], 0.5,
                                                     op0=ALU.add, op1=ALU.mult),
                           r=("b_lo0", "b_hi0"), w=("b_mid",))
                bg_queue.append(bis_init)

                def mk_iter(i):
                    cur, nxt = i % 2, (i + 1) % 2
                    cls = []
                    for cc in range(4):
                        cls.append(lambda cc=cc: em.dve(lambda e, cc=cc: e.tensor_scalar(
                            junk2[0:8, cc * 512:(cc + 1) * 512], r_scores[:, cc * 512:(cc + 1) * 512],
                            b_mid[:], 0.0, op0=ALU.is_gt, op1=ALU.add,
                            accum_out=b_c4[:, cc:cc + 1]),
                            r=("r_scores", "b_mid"), w=("junk2.0", "junk2.1", "b_c4")))

                    def tail():
                        em.dve(lambda e: e.tensor_reduce(b_cnt[:], b_c4[:], axis=AXX, op=ALU.add),
                               r=("b_c4",), w=("b_cnt",))
                        em.dve(lambda e: e.tensor_single_scalar(b_cond[:], b_cnt[:], float(KSEL), op=ALU.is_ge),
                               r=("b_cnt",), w=("b_cond",))
                        em.dve(lambda e: e.select(b_lo[nxt][:], b_cond[:], b_mid[:], b_lo[cur][:]),
                               r=("b_cond", "b_mid", f"b_lo{cur}"), w=(f"b_lo{nxt}",))
                        em.dve(lambda e: e.select(b_hi[nxt][:], b_cond[:], b_hi[cur][:], b_mid[:]),
                               r=("b_cond", "b_mid", f"b_hi{cur}"), w=(f"b_hi{nxt}",))
                        em.dve(lambda e: e.tensor_scalar(b_mid[:], b_lo[nxt][:], b_hi[nxt][:], 0.5,
                                                         op0=ALU.add, op1=ALU.mult),
                               r=(f"b_lo{nxt}", f"b_hi{nxt}"), w=("b_mid",))
                    cls.append(tail)
                    return cls
                for i in range(N_BISECT):
                    bg_queue.extend(mk_iter(i))
                lo_fin = f"b_lo{N_BISECT % 2}"

                def bis_post(r_mask=r_mask, r_grid=r_grid):
                    em.dve(lambda e: e.tensor_single_scalar(r_mask[:], r_scores[:],
                                                            b_lo[N_BISECT % 2][:], op=ALU.is_gt),
                           r=("r_scores", lo_fin), w=("r_mask",))
                    em.dve(lambda e: e.tensor_tensor_scan(r_grid[:], r_mask[:],
                                                          z_one[0:8, :].to_broadcast([8, T]),
                                                          0.0, op0=ALU.add, op1=ALU.mult),
                           r=("r_mask", "z_one"), w=("r_grid",))
                    em.dve(lambda e: e.scalar_tensor_tensor(r_grid[:], r_grid[:], 1.0, r_mask[:],
                                                            op0=ALU.mult, op1=ALU.mult),
                           r=("r_grid", "r_mask"), w=("r_grid",))
                bg_queue.append(bis_post)


            def emit_attention(branch, p, sl, av_tile, avkey):
                qp, qk = q_pair[sl], f"q_pair{sl}"
                kp_, kk = k_pair[sl], f"k_pair{sl}"
                vp = v_pair[sl]
                sct = [0]
                for w in range(4):
                    w0 = 512 * w
                    w1 = w0 + 512
                    kts = [kt for kt in range(NT) if kt * 128 < w1]
                    klast = kts[-1]

                    def emit_S_pair(idx, kt, w0=w0, w1=w1):
                        qstart = max(kt * 128, w0)
                        cw = w1 - qstart
                        slab, slk = pt3[idx % 4], f"pt{idx % 4}"
                        pss = []
                        for hb in (0, 64):
                            bi = sct[0] % 3
                            ps, psk = pS2[bi], f"pS{bi}"
                            sct[0] += 1
                            pss.append((hb, bi, ps, psk))
                            em.pe(lambda e, ps=ps, cw=cw, kt=kt, hb=hb, qa=qstart: e.matmul(
                                ps[:, 0:cw],
                                kp_[hb:hb + 64, kt * 128:(kt + 1) * 128],
                                qp[hb:hb + 64, qa:qa + cw],
                                start=True, stop=True, tile_position=(hb, 0)),
                                r=(kk, qk), w=(psk,), tag="S")
                        biA, biB = pss[0][1], pss[1][1]
                        if biB == biA + 1:
                            pv = pSx[:, biA * 512:biA * 512 + 1024].rearrange(
                                "p (two c) -> p two c", two=2)[:, :, 0:cw]
                            sv = slab[:].rearrange(
                                "p (two c) -> p two c", two=2)[:, :, 0:cw]
                            em.act(lambda e, sv=sv, pv=pv: e.activation(
                                sv, pv, AF.Exp, bias=bias_hd[sl][0][:], scale=0.125),
                                r=(f"pS{biA}", f"pS{biB}", f"bias{sl}.0"),
                                w=(slk + "a", slk + "b"), tag="exp")
                        else:
                            for hb, bi, ps, psk in pss:
                                so = (hb // 64) * 512
                                em.act(lambda e, slab=slab, ps=ps, cw=cw, so=so: e.activation(
                                    slab[:, so:so + cw], ps[:, 0:cw], AF.Exp,
                                    bias=bias_hd[sl][0][:], scale=0.125),
                                    r=(psk, f"bias{sl}.0"),
                                    w=(slk + ("a" if hb == 0 else "b"),), tag="exp")
                        if kt * 128 >= w0:
                            for hb in (0, 64):
                                so = (hb // 64) * 512
                                sk2 = slk + ("a" if hb == 0 else "b")
                                em.dve(lambda e, slab=slab, so=so: e.tensor_tensor(
                                    slab[:, so:so + 128], slab[:, so:so + 128],
                                    c_um[:], op=ALU.mult),
                                    r=(sk2, "c_um"), w=(sk2,), tag="dmask")

                    def emit_AV_pair(idx, kt, w0=w0, klast=klast):
                        qstart = max(kt * 128, w0)
                        cw = 512 - (qstart - w0)
                        co = qstart - w0
                        slab, slk = pt3[idx % 4], f"pt{idx % 4}"
                        for hb in (0, 64):
                            so = (hb // 64) * 512
                            vsl = vp[:, kt * 192 + hb:kt * 192 + hb + 128]
                            em.pe(lambda e, vsl=vsl, slab=slab, kt=kt, cw=cw, co=co,
                                  so=so, hb=hb: e.matmul(
                                pav[0:128, (hb // 64) * 512 + co:(hb // 64) * 512 + co + cw],
                                vsl, slab[:, so:so + cw],
                                start=(kt == 0), stop=(kt == klast)),
                                r=(slk + ("a" if hb == 0 else "b"),
                                   f"v_pair{sl}.{kt}", f"v_ones{sl}"),
                                w=(f"pavb{hb // 64}",), tag="AV")

                    def push_evac(hb, w0=w0):
                        pk = f"pavb{hb // 64}"
                        co = (hb // 64) * 512
                        pr = ev_par[0] % 2
                        ev_par[0] += 1
                        rbb, rbk = rb_sb[pr], f"rbsb{pr}"
                        lb = 64 - hb
                        evq.append(lambda: em.act(lambda e: e.activation(
                            rbb[:, 0:512], pav[lb:lb + 64, co:co + 512], AF.Ln),
                            r=(pk,), w=(rbk,), tag="lln"))
                        evq.append(lambda: em.act(lambda e: e.activation(
                            rbb[:, 0:512], rbb[:, 0:512], AF.Exp, bias=0.0, scale=-1.0),
                            r=(rbk,), w=(rbk,), tag="lexp"))

                        def ev_rest():
                            em.dve(lambda e: e.tensor_tensor(
                                av_tile[hb:hb + 64, w0:w0 + 512],
                                pav[hb:hb + 64, co:co + 512], rbb[:, 0:512], op=ALU.mult),
                                r=(pk, rbk), w=(avkey,), tag="avn")
                        evq.append(ev_rest)

                    SK = 3
                    for idx, kt in enumerate(kts):
                        emit_S_pair(idx, kt)
                        drain_ev(1)
                        drain_front(1)
                        drain_bg(1)
                        if idx >= SK:
                            emit_AV_pair(idx - SK, kts[idx - SK])
                            drain_ev(1)
                            drain_front(1)
                    for idx in range(max(0, len(kts) - SK), len(kts)):
                        emit_AV_pair(idx, kts[idx])
                        drain_ev(1)
                        drain_front(1)
                        drain_bg(1)
                    for hb in (0, 64):
                        push_evac(hb)
                    while evq:
                        drain_ev(1)
                        drain_front(1)

            for i_, (br, p, avt, avk) in enumerate(seq):
                sl = i_ % 2
                if i_ == 0:
                    drain_front(10 ** 9)
                if i_ + 1 < len(seq):
                    fq.extend(front_closures(seq[i_ + 1][0], seq[i_ + 1][1], (i_ + 1) % 2))
                emit_attention(br, p, sl, avt, avk)
                drain_front(10 ** 9)
            drain_bg(10 ** 9)

            if en_mosa:
                for tt in range(NT):
                    em.pe(lambda e, tt=tt, r_grid=r_grid: e.transpose(
                        pav[:, tt * 8:tt * 8 + 8], r_grid[:, tt * 128:(tt + 1) * 128], c_idf[0:8, 0:8]),
                        r=("r_grid", "c_idf"), w=("pavb0",), tag="trG")
                em.dve(lambda e: e.tensor_copy(g_tm[:], pav[:, 0:128]),
                       r=("pavb0",), w=("g_tm",))
                for tt in range(NT):
                    em.pe(lambda e, tt=tt: e.transpose(
                        pm1[:, tt * 8:tt * 8 + 8], r_scores[:, tt * 128:(tt + 1) * 128], c_idf[0:8, 0:8]),
                        r=("r_scores", "c_idf"), w=("pm1",), tag="trS")
                em.dve(lambda e: e.tensor_copy(
                    sc_tm[:].rearrange("p (k n) -> p k n", n=9)[:, :, 1:9],
                    pm1[:, 0:128].rearrange("p (k n) -> p k n", n=8)),
                    r=("pm1",), w=("sc_tm",))
                em.pool(lambda e: e.tensor_copy(
                    sc_tm[:].rearrange("p (k n) -> p k n", n=9)[:, :, 0:1],
                    c_tok[:].rearrange("p (k one) -> p k one", one=1)),
                    r=("c_tok",), w=("sc_tm",))

            em.barrier()
            scA.close()

            scB = ExitStack()
            c_iotar = S(f"c_iotar{b}", (128, T), F32, scB)
            av_mosa = [S(f"av_mosa{b}_{p}", (128, T), BF16, scB) for p in range(4)]
            wo_all = S(f"wo_all{b}", (128, 8192), BF16, scB)
            dwo_all = S(f"dwo_all{b}", (128, 4096), BF16, scB)
            mwo_all = S(f"mwo_all{b}", (128, 4096), BF16, scB)
            em.dma_sp(lambda e, dd=c_iotar[:], ss=d_iota_row[:, :]: e.dma_start(dd, ss),
                      r=(), w=("c_iotar",), sem="dwo", tag="ld:c_iotar")
            for (dst_, src_, k_) in ((wo_all, d_woT, "wo_all"), (dwo_all, d_dwo, "dwo_all"),
                                     (mwo_all, d_mwo, "mwo_all")):
                em.dma_pool(lambda e, dd=dst_[:], ss=src_[:, :]: e.dma_start(dd, ss),
                            r=(), w=(k_,), sem="dwo", tag="ld:" + k_)
            em.fence_keys(("c_iotar", "wo_all", "dwo_all", "mwo_all"))
            if en_mosa:
                wbufm = [S(f"wbm{b}_{i}", (128, 1536), BF16, scB) for i in range(2)]
                oh_t = [S(f"oh_t{b}_{i}", (128, KSEL), F32, scB) for i in range(2)]
                idxsc8 = S(f"idxsc8{b}", (128, 32), F32, scB)
                cmT = S(f"cmT{b}", (2, 256), F32, scB)
                idx_u = S(f"idx_u{b}", (128, 2), U32, scB)
                gate_c = S(f"gate_c{b}", (128, 2), F32, scB)
                gate_e = S(f"gate_e{b}", (128, 2), F32, scB)
                xg = [S(f"xg{b}_{i}", (128, H), BF16, scB) for i in range(2)]
                xg_fm = S(f"xg_fm{b}", (128, NCH * KSEL), BF16, scB)
                mq8 = S(f"mq8{b}", (64, 8 * KSEL), BF16, scB)
                mk8 = S(f"mk8{b}", (64, 8 * KSEL), BF16, scB)
                mv8 = S(f"mv8{b}", (128, 8 * 130), BF16, scB)
                msq = S(f"msq{b}", (64, KSEL), BF16, scB)
                mpt = S(f"mpt{b}", (128, 3 * 128), BF16, scB)
                mavtm = S(f"mavtm{b}", (128, 2 * D), BF16, scB)
                bias_m = S(f"bias_m{b}", (128, 1), F32, scB)
                oh_rt = [S(f"oh_rt{b}_{i}", (128, 512), BF16, scB) for i in range(2)]

                em.pool(lambda e, mv8=mv8: e.memset(
                    mv8[:].rearrange("p (h s) -> p h s", s=130)[:, :, 64:130:65], 1.0),
                    w=("mv_ones",), tag="mvones")

                for hh in range(8):
                    wb, wbkey = wbufm[hh % 2], f"wbm{hh % 2}"
                    ap = d_mwqkv[:, :].rearrange("p (c e) -> p c e", c=8)[:, :, hh * 192:(hh + 1) * 192]
                    em.dma_pool(lambda e, ap=ap, wb=wb: e.dma_start(
                        wb[:, 0:1536].rearrange("p (c e) -> p c e", c=8), ap),
                        r=(), w=(wbkey,), sem=f"dw{hh % 2}", tag=f"ldmw{hh}")
                    for tt in range(NT):
                        oh, ohk = oh_t[tt % 2], f"oh{tt % 2}"
                        em.dve(lambda e, oh=oh, tt=tt, hh=hh: e.tensor_single_scalar(
                            oh[:], c_iota1[:], g_tm[:, tt * 8 + hh:tt * 8 + hh + 1], op=ALU.is_equal),
                            r=("c_iota1", "g_tm"), w=(ohk,))
                        em.pe(lambda e, oh=oh, tt=tt, hh=hh: e.matmul(
                            pm1[0:2, 0:256],
                            sc_tm[:, tt * 9:tt * 9 + hh + 2:hh + 1],
                            oh[:, 0:256],
                            start=(tt == 0), stop=(tt == NT - 1)),
                            r=(ohk, "sc_tm"), w=("pm1",), tag="cmm")
                    em.act(lambda e: e.activation(cmT[:, 0:256], pm1[0:2, 0:256], AF.Copy),
                           r=("pm1",), w=("cmT",))
                    for half in range(2):
                        em.pe(lambda e, half=half: e.transpose(
                            pm2[0:128, half * 2:half * 2 + 2],
                            cmT[:, half * 128:(half + 1) * 128], c_idf[0:2, 0:2]),
                            r=("cmT", "c_idf"), w=("pm2",), tag="cmtr")
                    em.act(lambda e, hh=hh, idxsc8=idxsc8: e.activation(
                        idxsc8[:, hh * 4:hh * 4 + 4], pm2[0:128, 0:4], AF.Copy),
                        r=("pm2",), w=(f"idxsc.{hh}",))
                    em.dve(lambda e, hh=hh, idx_u=idx_u, idxsc8=idxsc8: e.tensor_copy(
                        idx_u[:], idxsc8[:, hh * 4:hh * 4 + 4:2]),
                        r=(f"idxsc.{hh}",), w=("idx_u",))
                    for rs in range(2):
                        em.dma_pool(lambda e, rs=rs, b=b, xg=xg, idx_u=idx_u: e.indirect_dma_start(
                            xg[rs][:], None,
                            xn_scr[:, :, :].rearrange("b t h -> (b t) h"),
                            bass.IndirectOffsetOnAxis(ap=idx_u[:, rs:rs + 1], axis=0),
                            element_offset=b * T * H),
                            r=("idx_u",) + tuple(xn_scr_key + f".{tt}" for tt in range(NT)),
                            w=(f"xg{rs}",), sem=f"dg{rs}", tag=f"gather{hh}.{rs}")
                    for rs in range(2):
                        for cg in range(2):
                            for c4 in range(4):
                                c = cg * 4 + c4
                                em.pe(lambda e, rs=rs, c=c, c4=c4, xg=xg: e.transpose(
                                    ptr[:, c4 * 128:(c4 + 1) * 128], xg[rs][:, c * 128:(c + 1) * 128], c_id[:]),
                                    r=(f"xg{rs}", "c_id"), w=("ptr",), tag="trXg")
                            em.dve(lambda e, rs=rs, cg=cg, xg_fm=xg_fm: e.tensor_copy(
                                xg_fm[:].rearrange("p (c r) -> p c r", c=NCH)[:, cg * 4:(cg + 1) * 4, rs * 128:(rs + 1) * 128],
                                ptr[:].rearrange("p (c t) -> p c t", c=4)),
                                r=("ptr",), w=("xg_fm",))
                    rope_tables_tm(mcos_tm, msin_tm,
                                   lambda ti, hh=hh, idxsc8=idxsc8: idxsc8[:, hh * 4 + ti * 2:hh * 4 + ti * 2 + 1],
                                   (f"idxsc.{hh}",), 2, "mrtk")
                    for (dst8, dk8, woff, ck, sk) in (
                            (mq8, f"mq.{hh}", 0, "mrtkcos", "mrtksin"),
                            (mk8, f"mk.{hh}", 64, "mrtkcos", "mrtksin")):
                        for rs in range(2):
                            qb, qbk = qtmb[rs % 2], f"qtmb{rs % 2}"
                            for c in range(NCH):
                                em.pe(lambda e, c=c, rs=rs, wb=wb, xg_fm=xg_fm, woff=woff: e.matmul(
                                    pm1[:, 0:64], xg_fm[:, c * KSEL + rs * 128:c * KSEL + (rs + 1) * 128],
                                    wb[:, c * 192 + woff:c * 192 + woff + 64],
                                    start=(c == 0), stop=(c == NCH - 1)),
                                    r=(wbkey, "xg_fm"), w=("pm1",), tag="mqk")
                            em.act(lambda e, qb=qb: e.activation(qb[:, 0:64], pm1[:, 0:64], AF.Copy),
                                   r=("pm1",), w=(qbk,))
                            rope_apply_tm(qb, qbk, 0,
                                          mcos_tm[:, rs * 16:(rs + 1) * 16],
                                          msin_tm[:, rs * 16:(rs + 1) * 16], ck, sk)
                            em.pe(lambda e, qb=qb, rs=rs: e.transpose(
                                ptr[0:64, rs * 128:(rs + 1) * 128], qb[:, 0:64], c_id[:]),
                                r=(qbk, "c_id"), w=("ptr",), tag="trmQ")
                        em.act(lambda e, dst8=dst8, hh=hh: e.activation(
                            dst8[:, hh * KSEL:(hh + 1) * KSEL], ptr[0:64, 0:KSEL], AF.Copy),
                            r=("ptr",), w=(dk8,))
                    for rs in range(2):
                        for c in range(NCH):
                            em.pe(lambda e, c=c, rs=rs, wb=wb, xg_fm=xg_fm: e.matmul(
                                pm1[:, 0:64], xg_fm[:, c * KSEL + rs * 128:c * KSEL + (rs + 1) * 128],
                                wb[:, c * 192 + 128:c * 192 + 192],
                                start=(c == 0), stop=(c == NCH - 1)),
                                r=(wbkey, "xg_fm"), w=("pm1",), tag="mv")
                        em.act(lambda e, rs=rs, hh=hh, mv8=mv8: e.activation(
                            mv8[:, hh * 130 + rs * 65:hh * 130 + rs * 65 + 64], pm1[:, 0:64], AF.Copy),
                            r=("pm1",), w=(f"mv.{hh}.{rs}",))

                for hh in range(8):
                    mqv = mq8[:, hh * KSEL:(hh + 1) * KSEL]
                    mkv = mk8[:, hh * KSEL:(hh + 1) * KSEL]
                    em.act(lambda e, mqv=mqv, msq=msq: e.activation(msq[:], mqv, AF.Square),
                           r=(f"mq.{hh}",), w=("msq",))
                    em.pe(lambda e, msq=msq: e.matmul(pm1[0:1, 0:KSEL], c_bo[0:64, 0:1], msq[:],
                                                      start=True, stop=True),
                          r=("c_bo", "msq"), w=("pm1",), tag="mnq")
                    em.dve(lambda e: e.tensor_reduce(nrm_s[0:1, 0:1], pm1[0:1, 0:KSEL], axis=AXX, op=ALU.max),
                           r=("pm1",), w=("nq",))
                    em.act(lambda e, mkv=mkv, msq=msq: e.activation(msq[:], mkv, AF.Square),
                           r=(f"mk.{hh}",), w=("msq",))
                    em.pe(lambda e, msq=msq: e.matmul(pm1[0:1, 0:KSEL], c_bo[0:64, 0:1], msq[:],
                                                      start=True, stop=True),
                          r=("c_bo", "msq"), w=("pm1",), tag="mnk")
                    em.dve(lambda e: e.tensor_reduce(nrm_s[0:1, 4:5], pm1[0:1, 0:KSEL], axis=AXX, op=ALU.max),
                           r=("pm1",), w=("nk",))
                    em.act(lambda e: e.activation(nrm_s[0:1, 1:2], nrm_s[0:1, 0:1], AF.Ln),
                           r=("nq",), w=("lnq",))
                    em.act(lambda e: e.activation(nrm_s[0:1, 5:6], nrm_s[0:1, 4:5], AF.Ln),
                           r=("nk",), w=("lnk",))
                    em.dve(lambda e: e.tensor_tensor(nrm_s[0:1, 2:3], nrm_s[0:1, 1:2], nrm_s[0:1, 5:6], op=ALU.add),
                           r=("lnq", "lnk"), w=("lnsum",))
                    em.act(lambda e: e.activation(nrm_s[0:1, 3:4], nrm_s[0:1, 2:3], AF.Exp, bias=0.0, scale=0.5),
                           r=("lnsum",), w=("Mt",))
                    em.dve(lambda e: e.tensor_scalar_mul(nrm_s[0:1, 3:4], nrm_s[0:1, 3:4], -0.125),
                           r=("Mt",), w=("Mt",))
                    em.pe(lambda e: e.matmul(pm1[:, 0:1], c_or[:], nrm_s[0:1, 3:4], start=True, stop=True),
                          r=("Mt", "c_or"), w=("pm1",), tag="mbc")
                    em.act(lambda e, bias_m=bias_m: e.activation(bias_m[:], pm1[:, 0:1], AF.Copy),
                           r=("pm1",), w=("bias_m",))

                    tiles = [(0, 0), (0, 1), (1, 1)]
                    for ti, (kt, qt) in enumerate(tiles):
                        em.pe(lambda e, ti=ti, kt=kt, qt=qt, hh=hh, mq8=mq8, mk8=mk8: e.matmul(
                            pS2[0][:, ti * 128:(ti + 1) * 128],
                            mk8[:, hh * KSEL + kt * 128:hh * KSEL + (kt + 1) * 128],
                            mq8[:, hh * KSEL + qt * 128:hh * KSEL + (qt + 1) * 128],
                            start=True, stop=(kt != qt)),
                            r=(f"mk.{hh}", f"mq.{hh}"), w=("pS0",), tag="mS")
                        if kt == qt:
                            em.pe(lambda e, ti=ti: e.matmul(
                                pS2[0][:, ti * 128:(ti + 1) * 128], c_id[:], c_cm[:],
                                start=False, stop=True),
                                r=("c_id", "c_cm"), w=("pS0",), tag="mmask")
                    em.act(lambda e, mpt=mpt, bias_m=bias_m: e.activation(
                        mpt[:, 0:384], pS2[0][:, 0:384], AF.Exp, bias=bias_m[:], scale=0.125),
                        r=("pS0", "bias_m"), w=("mpt",), tag="mexp")
                    for ti, (kt, qt) in enumerate(tiles):
                        em.pe(lambda e, ti=ti, kt=kt, qt=qt, hh=hh, mpt=mpt, mv8=mv8: e.matmul(
                            pav[:, qt * 65:qt * 65 + 65], mpt[:, ti * 128:(ti + 1) * 128],
                            mv8[:, hh * 130 + kt * 65:hh * 130 + kt * 65 + 65],
                            start=(kt == 0), stop=(kt == qt)),
                            r=("mpt", f"mv.{hh}.{kt}", "mv_ones"), w=("pavb0",), tag="mAV")
                    em.act(lambda e, hh=hh, gate_e=gate_e, idxsc8=idxsc8: e.activation(
                        gate_e[:], idxsc8[:, hh * 4 + 1:hh * 4 + 4:2], AF.Exp, bias=0.0, scale=-1.0),
                        r=(f"idxsc.{hh}",), w=("gate_e",))
                    em.dve(lambda e, gate_e=gate_e: e.tensor_scalar_add(gate_e[:], gate_e[:], 1.0),
                           r=("gate_e",), w=("gate_e",))
                    em.dve(lambda e, gate_c=gate_c, gate_e=gate_e: e.reciprocal(gate_c[:], gate_e[:]),
                           r=("gate_e",), w=("gate_c",))
                    em.dve(lambda e: e.reciprocal(
                        nrm_s[:, 0:2], pav[:, 64:130:65]),
                        r=("pavb0",), w=("mrcp",))
                    for rs in range(2):
                        em.dve(lambda e, rs=rs, mavtm=mavtm, gate_c=gate_c: e.tensor_scalar(
                            mavtm[:, rs * 64:(rs + 1) * 64], pav[:, rs * 65:rs * 65 + 64],
                            nrm_s[:, rs:rs + 1], gate_c[:, rs:rs + 1], op0=ALU.mult, op1=ALU.mult),
                            r=("pavb0", "mrcp", "gate_c"), w=("mavtm",))
                    hb2 = (hh % 2) * 64
                    for ns in range(4):
                        for rs in range(2):
                            em.dve(lambda e, rs=rs, ns=ns, hh=hh, oh_rt=oh_rt, idxsc8=idxsc8: e.tensor_single_scalar(
                                oh_rt[rs][:], c_iotar[:, ns * 512:(ns + 1) * 512],
                                idxsc8[:, hh * 4 + rs * 2:hh * 4 + rs * 2 + 1], op=ALU.is_equal),
                                r=("c_iotar", f"idxsc.{hh}"), w=(f"ohrt{rs}",))
                        em.pe(lambda e, ns=ns, mavtm=mavtm, oh_rt=oh_rt: e.matmul(
                            pm1[0:64, 0:512], mavtm[:, 0:64], oh_rt[0][:],
                            start=True, stop=False),
                            r=("mavtm", "ohrt0"), w=("pm1",), tag="msc")
                        em.pe(lambda e, ns=ns, mavtm=mavtm, oh_rt=oh_rt: e.matmul(
                            pm1[0:64, 0:512], mavtm[:, 64:128], oh_rt[1][:],
                            start=False, stop=True),
                            r=("mavtm", "ohrt1"), w=("pm1",), tag="msc2")
                        em.act(lambda e, ns=ns, hh=hh, hb2=hb2: e.activation(
                            av_mosa[hh // 2][hb2:hb2 + 64, ns * 512:(ns + 1) * 512],
                            pm1[0:64, 0:512], AF.Copy),
                            r=("pm1",), w=(f"avm{hh // 2}",))

            srcs = []
            if en_attn:
                srcs += [(av_attn[p], wo_all, p, (f"ava{p}",), "wo_all") for p in range(8)]
            if en_dense:
                srcs += [(av_dense[p], dwo_all, p, (f"avd{p}",), "dwo_all") for p in range(4)]
            if en_mosa:
                srcs += [(av_mosa[p], mwo_all, p, (f"avm{p}",), "mwo_all") for p in range(4)]
            for tt in range(NT):
                xb, xkey = xt[tt % 2], f"xt{tt % 2}"
                src = x_in[b, tt * 128:(tt + 1) * 128, :]
                em.dma_sp(lambda e, d=xb, s=src: e.dma_start(d[:], s), r=(), w=(xkey,),
                          sem=f"dx{tt % 2}", tag=f"rldx{b}.{tt}")
                ob, obkey = out_sb[tt % 2], f"osb{tt % 2}"
                for ns in range(2):
                    pb, pbk = pS2[ns], f"pS{ns}"
                    if srcs:
                        for ki, (av, wsb, p, avkeys, wkey) in enumerate(srcs):
                            em.pe(lambda e, pb=pb, av=av, wsb=wsb, p=p, tt=tt, ns=ns,
                                  ki=ki, nsr=len(srcs): e.matmul(
                                pb[:, 0:512], av[:, tt * 128:(tt + 1) * 128],
                                wsb[:, p * 1024 + ns * 512:p * 1024 + (ns + 1) * 512],
                                start=(ki == 0), stop=(ki == nsr - 1)),
                                r=avkeys + (wkey,), w=(pbk,), tag="oprj")
                        em.dve(lambda e, pb=pb, ob=ob, ns=ns, xb=xb: e.tensor_tensor(
                            ob[:, ns * 512:(ns + 1) * 512], pb[:, 0:512],
                            xb[:, ns * 512:(ns + 1) * 512], op=ALU.add),
                            r=(pbk, xkey), w=(obkey,), tag="oadd")
                    else:
                        em.dve(lambda e, ob=ob, ns=ns, xb=xb: e.tensor_copy(
                            ob[:, ns * 512:(ns + 1) * 512], xb[:, ns * 512:(ns + 1) * 512]),
                            r=(xkey,), w=(obkey,), tag="ocp")
                em.dma_sp(lambda e, ob=ob, d=out_t[b, tt * 128:(tt + 1) * 128, :]: e.dma_start(d, ob[:]),
                          r=(obkey,), w=(), sem="dout", tag=f"stout{b}.{tt}")
            em.barrier()
            scB.close()

        if tap_outs:
            for nm, (dram, sb) in tap_outs.items():
                em.dma_sp(lambda e, dram=dram, sb=sb: e.dma_start(dram[:, :], sb[:]),
                          r=(), w=(), sem="dtap", tag="tap:" + nm)
        em.barrier()
        em.finalize_into(nc, sems, dma_sems)
    return nc


_CACHE = {}


def kernel(**inputs):
    x = np.asarray(inputs["L0_attn_x"])
    host = prep_host(inputs)
    consts = host_consts()
    if "full" not in _CACHE:
        _CACHE["full"] = build_nc()
    nc = _CACHE["full"]
    in_maps = []
    for core in range(8):
        m = {"x": np.ascontiguousarray(x[core * NB:(core + 1) * NB])}
        m.update(host)
        m.update(consts)
        in_maps.append(m)
    res = run_bass_kernel_spmd(nc, in_maps, core_ids=list(range(8)))
    out = np.concatenate([np.asarray(r["out"]) for r in res.results], axis=0)
    return out.astype(np.float32)


# version 18
# speedup vs baseline: 1.0546x; 1.0546x over previous
import numpy as np
import ml_dtypes
import concourse.bass as bass
import concourse.mybir as mybir
from concourse.bass_utils import run_bass_kernel_spmd
from contextlib import ExitStack

F32 = mybir.dt.float32
BF16 = mybir.dt.bfloat16
U32 = mybir.dt.uint32
U8 = mybir.dt.uint8
I32 = mybir.dt.int32
AF = mybir.ActivationFunctionType
ALU = mybir.AluOpType
AXX = mybir.AxisListType.X

T = 2048
H = 1024
D = 64
NB = 2
NT = T // 128
NCH = H // 128
KSEL = 256
NEG = -1e30
N_BISECT = 60

TPI = 2.0 * np.pi
TPI_HI = float(np.float32(TPI))
TPI_LO = float(np.float32(TPI - np.float64(np.float32(TPI))))
PI_F = float(np.float32(np.pi))
HALFPI = float(np.float32(np.pi / 2))


class EM:
    ENGS = ("PE", "ACT", "DVE", "POOL", "SP")

    DMA_SEMS = ("dx0", "dx1", "dxn", "dout", "dpos", "dcon", "dconp",
                "dw0", "dw1", "dwo", "dg0", "dg1", "dtap")

    def __init__(self):
        self.ops = []
        self.last_w = {}
        self.readers = {}
        self.floor = None
        self.dma_idx = {s: 0 for s in self.DMA_SEMS}

    def op(self, eng, fn, r=(), w=(), dma=None, tag=""):
        o = {"eng": eng, "fn": fn, "dma": dma, "deps": [], "inc": False,
             "tag": tag, "i": len(self.ops)}
        if dma is not None:
            self.dma_idx[dma] += 1
            o["dma_i"] = self.dma_idx[dma]
        deps = {}
        if self.floor is not None:
            deps[self.floor["i"]] = self.floor
        for k in r:
            d = self.last_w.get(k)
            if d is not None:
                deps[d["i"]] = d
        for k in w:
            d = self.last_w.get(k)
            if d is not None:
                deps[d["i"]] = d
            for rd in self.readers.get(k, ()):
                deps[rd["i"]] = rd
        for d in deps.values():
            if d is o:
                continue
            if d["dma"] is None and d["eng"] == "PE" and eng == "PE" and o["dma"] is None:
                continue
            d["inc"] = True
            o["deps"].append(d)
        for k in w:
            self.last_w[k] = o
            self.readers[k] = []
        wset = set(w)
        for k in r:
            if k not in wset:
                self.readers.setdefault(k, []).append(o)
        self.ops.append(o)
        return o

    def pe(self, fn, r=(), w=(), tag=""):
        return self.op("PE", fn, r, w, tag=tag)

    def act(self, fn, r=(), w=(), tag=""):
        return self.op("ACT", fn, r, w, tag=tag)

    def dve(self, fn, r=(), w=(), tag=""):
        return self.op("DVE", fn, r, w, tag=tag)

    def pool(self, fn, r=(), w=(), tag=""):
        return self.op("POOL", fn, r, w, tag=tag)

    def dma_sp(self, fn, r=(), w=(), sem="dcon", tag=""):
        return self.op("SP", fn, r, w, dma=sem, tag=tag)

    def dma_pool(self, fn, r=(), w=(), sem="dconp", tag=""):
        return self.op("POOL", fn, r, w, dma=sem, tag=tag)

    def fence_keys(self, keys, eng="POOL"):
        return self.op(eng, lambda e: e.nop(), r=tuple(keys), w=tuple(keys), tag="kfence")

    def barrier(self):
        o = self.op("SP", lambda e: e.nop(), (), (), tag="fence")
        deps = {}
        for d in self.last_w.values():
            deps[d["i"]] = d
        for rl in self.readers.values():
            for d in rl:
                deps[d["i"]] = d
        o["deps"] = []
        for d in deps.values():
            if d is o:
                continue
            d["inc"] = True
            o["deps"].append(d)
        o["inc"] = True
        self.last_w = {k: o for k in self.last_w}
        self.readers = {}
        self.floor = o
        return o

    def finalize_into(self, nc, sems, dma_sems):
        cnt = {e: 0 for e in self.ENGS}
        for o in self.ops:
            if o["dma"] is None and o["inc"]:
                cnt[o["eng"]] += 1
                o["inc_no"] = cnt[o["eng"]]
        for o in self.ops:
            waits = {}
            for d in o["deps"]:
                if d["dma"] is not None:
                    s, v = d["dma"], 16 * d["dma_i"]
                else:
                    s, v = d["eng"], d["inc_no"]
                if waits.get(s, 0) < v:
                    waits[s] = v
            o["waits"] = waits
        allsems = dict(sems)
        allsems.update(dma_sems)

        def emit_for(ename, eng):
            wm = {}
            for o in self.ops:
                if o["eng"] != ename:
                    continue
                for s, v in sorted(o["waits"].items()):
                    if wm.get(s, 0) >= v:
                        continue
                    eng.wait_ge(allsems[s], v)
                    wm[s] = v
                inst = o["fn"](eng)
                if o["dma"] is not None:
                    inst.then_inc(allsems[o["dma"]], 16)
                elif o["inc"]:
                    inst.then_inc(allsems[ename], 1)

        with nc.Block() as block:
            @block.tensor
            def _(e):
                emit_for("PE", e)

            @block.scalar
            def _(e):
                emit_for("ACT", e)

            @block.vector
            def _(e):
                emit_for("DVE", e)

            @block.gpsimd
            def _(e):
                emit_for("POOL", e)

            @block.sync
            def _(e):
                emit_for("SP", e)


def _chunk_major(a, nchunk):
    rows, C = a.shape
    assert rows == nchunk * 128
    return np.ascontiguousarray(
        a.reshape(nchunk, 128, C).transpose(1, 0, 2).reshape(128, nchunk * C))


def prep_host(inp):
    h = {}
    h["wqT"] = _chunk_major(np.ascontiguousarray(np.asarray(inp["L0_attn_wq"]).T), 8)
    h["wkT"] = _chunk_major(np.ascontiguousarray(np.asarray(inp["L0_attn_wk"]).T), 8)
    h["wvT"] = _chunk_major(np.ascontiguousarray(np.asarray(inp["L0_attn_wv"]).T), 8)
    h["woT"] = _chunk_major(np.ascontiguousarray(np.asarray(inp["L0_attn_wo"]).T), 8)
    dw = np.asarray(inp["L0_ffn_dense_wqkv"])
    blocks = []
    for p in range(4):
        blocks.append(np.concatenate(
            [dw[:, 2 * p, 0:64], dw[:, 2 * p + 1, 0:64],
             dw[:, 2 * p, 64:128], dw[:, 2 * p + 1, 64:128],
             dw[:, 2 * p, 128:192], dw[:, 2 * p + 1, 128:192]], axis=1))
    h["dwqkv"] = _chunk_major(np.concatenate(blocks, axis=1), 8)
    mw = np.asarray(inp["L0_ffn_mosa_wqkv"])
    blocks = []
    for hh in range(8):
        blocks.append(np.concatenate(
            [mw[:, hh, 0:64], mw[:, hh, 64:128], mw[:, hh, 128:192]], axis=1))
    h["mwqkv"] = _chunk_major(np.concatenate(blocks, axis=1), 8)
    h["dwo"] = _chunk_major(np.asarray(inp["L0_ffn_dense_wo"]), 4)
    h["mwo"] = _chunk_major(np.asarray(inp["L0_ffn_mosa_wo"]), 4)
    h["router"] = _chunk_major(np.asarray(inp["L0_ffn_mosa_router_w"]), 8)
    return h


def host_consts():
    c = {}
    ident = np.eye(128, dtype=np.float32)
    c["identb"] = ident.astype(ml_dtypes.bfloat16)
    c["identf"] = ident
    c["cmaskb"] = (np.tril(np.ones((128, 128), dtype=np.float32), -1) * NEG).astype(ml_dtypes.bfloat16)
    c["umaskb"] = np.triu(np.ones((128, 128), dtype=np.float32)).astype(ml_dtypes.bfloat16)
    bo = np.zeros((128, 2), dtype=np.float32)
    bo[0:64, 0] = 1.0
    bo[64:128, 1] = 1.0
    c["blockones"] = bo.astype(ml_dtypes.bfloat16)
    c["onesrow"] = np.ones((1, 128), dtype=np.float32)
    j = np.arange(16, dtype=np.float64)
    v = (10000.0 ** (-j / 16.0)) / (2.0 * np.pi)
    hi = v.astype(np.float32)
    lo = (v - hi.astype(np.float64)).astype(np.float32)
    c["invfh"] = np.ascontiguousarray(np.broadcast_to(hi, (128, 16)))
    c["invfl"] = np.ascontiguousarray(np.broadcast_to(lo, (128, 16)))
    c["iota_row"] = np.ascontiguousarray(np.broadcast_to(np.arange(T, dtype=np.float32), (128, T)))
    c["iota1"] = np.ascontiguousarray(np.broadcast_to(np.arange(1, KSEL + 1, dtype=np.float32), (128, KSEL)))
    tok = (np.arange(NT)[None, :] * 128 + np.arange(128)[:, None]).astype(np.float32)
    c["tokidx"] = np.ascontiguousarray(tok)
    return c


def build_nc(en_attn=True, en_dense=True, en_mosa=True, taps=()):
    nc = bass.Bass("TRN2", target_bir_lowering=False)

    def din(name, shape, dtype=F32):
        return nc.dram_tensor(name, list(shape), dtype, kind="ExternalInput")

    x_in = din("x", (NB, T, H))
    d_wqT = din("wqT", (128, 8192))
    d_wkT = din("wkT", (128, 8192))
    d_wvT = din("wvT", (128, 8192))
    d_woT = din("woT", (128, 8192))
    d_dwqkv = din("dwqkv", (128, 8 * 1536))
    d_mwqkv = din("mwqkv", (128, 8 * 1536))
    d_dwo = din("dwo", (128, 4096))
    d_mwo = din("mwo", (128, 4096))
    d_router = din("router", (128, 64))
    d_identb = din("identb", (128, 128), BF16)
    d_identf = din("identf", (128, 128), F32)
    d_cmaskb = din("cmaskb", (128, 128), BF16)
    d_umaskb = din("umaskb", (128, 128), BF16)
    d_blockones = din("blockones", (128, 2), BF16)
    d_onesrow = din("onesrow", (1, 128))
    d_invfh = din("invfh", (128, 16))
    d_invfl = din("invfl", (128, 16))
    d_iota_row = din("iota_row", (128, T))
    d_iota1 = din("iota1", (128, KSEL))
    d_tokidx = din("tokidx", (128, NT))

    out_t = nc.dram_tensor("out", [NB, T, H], F32, kind="ExternalOutput")
    xn_scr = nc.dram_tensor("xn_scr", [NB, T, H], BF16)

    tap_outs = {}
    tap_req = list(taps)
    ctx = ExitStack()
    with ctx:
        def S(name, shape, dtype, stk=None):
            return (stk or ctx).enter_context(nc.sbuf_tensor(name, list(shape), dtype))

        def P(name, shape, dtype):
            return ctx.enter_context(nc.psum_tensor(name, list(shape), dtype))

        def tap(nm, tsb):
            if nm in tap_req:
                tap_outs[nm] = (nc.dram_tensor("dbg_" + nm, [tsb.shape[0], tsb.shape[1]],
                                               tsb.dtype, kind="ExternalOutput"), tsb)

        def tapdump(nm, tsb, rkeys):
            if nm not in tap_req:
                return
            dram = nc.dram_tensor("dbg_" + nm, [tsb.shape[0], tsb.shape[1]],
                                  tsb.dtype, kind="ExternalOutput")
            em.dma_sp(lambda e, dram=dram, tsb=tsb: e.dma_start(dram[:, :], tsb[:]),
                      r=tuple(rkeys), w=(), sem="dtap", tag="tapd:" + nm)

        c_id = S("c_id", (128, 128), BF16)
        c_idf = S("c_idf", (128, 128), F32)
        c_cm = S("c_cm", (128, 128), BF16)
        c_um = S("c_um", (128, 128), BF16)
        c_bo = S("c_bo", (128, 2), BF16)
        c_or = S("c_or", (1, 128), F32)
        c_invfh = S("c_invfh", (128, 16), F32)
        c_invfl = S("c_invfl", (128, 16), F32)
        c_iota1 = S("c_iota1", (128, KSEL), F32)
        c_tok = S("c_tok", (128, NT), F32)
        c_router = S("c_router", (128, 64), BF16)
        z_eps = S("z_eps", (128, 1), F32)
        z_one = S("z_one", (128, 1), F32)

        xt = [S(f"xt{i}", (128, H), F32) for i in range(2)]
        xnt = [S(f"xnt{i}", (128, H), BF16) for i in range(2)]
        junk2 = S("junk2", (128, T), BF16)
        ln_s = S("ln_s", (128, 16), F32)

        r_scores = S("r_scores", (8, T), F32)
        b_lo = [S(f"b_lo{i}", (8, 1), F32) for i in range(2)]
        b_hi = [S(f"b_hi{i}", (8, 1), F32) for i in range(2)]
        b_mid = S("b_mid", (8, 1), F32)
        b_cnt = S("b_cnt", (8, 1), F32)
        b_cond = S("b_cond", (8, 1), U8)
        b_c4 = S("b_c4", (8, 4), F32)

        av_attn = [S(f"av_attn{p}", (128, T), BF16) for p in range(8)]
        av_dense = [S(f"av_dense{p}", (128, T), BF16) for p in range(4)]

        g_tm = S("g_tm", (128, 128), F32)
        sc_tm = S("sc_tm", (128, NT * 9), F32)

        nrm_s = S("nrm_s", (128, 8), F32)
        rb_sb = [S(f"rb_sb{i}", (64, 512), F32) for i in range(2)]
        vtr_a = S("vtr_a", (32, 32), F32)
        vtr_b = S("vtr_b", (32, 32), F32)
        bias_hd = [[S(f"bias_hd{s}_{i}", (128, 1), F32) for i in range(2)] for s in range(2)]
        out_sb = [S(f"out_sb{i}", (128, H), F32) for i in range(2)]
        rts = [S(f"rts{i}", (128, 16), F32) for i in range(3)]
        rti = S("rti", (128, 16), I32)
        rp3 = [S(f"rp3{i}", (128, 16), BF16) for i in range(3)]
        pos2b = S("pos2b", (2, 128), F32)
        cos_tm = S("cos_tm", (128, NT * 16), BF16)
        sin_tm = S("sin_tm", (128, NT * 16), BF16)
        mcos_tm = S("mcos_tm", (128, 2 * 16), BF16)
        msin_tm = S("msin_tm", (128, 2 * 16), BF16)
        qtmb = [S(f"qtmb{i}", (128, 128), BF16) for i in range(2)]

        pSx = P("pSx", (128, 1536), F32)
        pS2 = [pSx[:, i * 512:(i + 1) * 512] for i in range(3)]
        pav = P("pav", (128, 1024), F32)
        ptr = P("ptr", (128, 512), BF16)
        pm1 = P("pm1", (128, 512), F32)
        pm2 = P("pm2", (128, 512), F32)

        sems = {e: ctx.enter_context(nc.semaphore("sem_" + e)) for e in EM.ENGS}
        dma_sems = {s: ctx.enter_context(nc.semaphore("sem_" + s)) for s in EM.DMA_SEMS}

        em = EM()

        tap("r_scores", r_scores)
        tap("g_tm", g_tm)
        tap("sc_tm", sc_tm)

        def ldc(dst, src, key, pool=False):
            f = (lambda e, dd=dst, ss=src: e.dma_start(dd, ss))
            (em.dma_pool if pool else em.dma_sp)(f, r=(), w=(key,), tag="ld:" + key)

        ldc(c_id[:], d_identb[:, :], "c_id")
        ldc(c_idf[:], d_identf[:, :], "c_idf")
        ldc(c_cm[:], d_cmaskb[:, :], "c_cm")
        ldc(c_um[:], d_umaskb[:, :], "c_um")
        ldc(c_bo[:], d_blockones[:, :], "c_bo")
        ldc(c_or[:], d_onesrow[:, :], "c_or")
        ldc(c_invfh[:], d_invfh[:, :], "c_invfh")
        ldc(c_invfl[:], d_invfl[:, :], "c_invfl")
        ldc(c_iota1[:], d_iota1[:, :], "c_iota1")
        ldc(c_tok[:], d_tokidx[:, :], "c_tok")
        ldc(c_router[:], d_router[:, :], "c_router", pool=True)
        em.dve(lambda e: e.memset(z_eps[:], 1e-5), w=("z_eps",))
        em.dve(lambda e: e.memset(z_one[:], 1.0), w=("z_one",))
        em.barrier()

        def rope_tables_tm(cos_tm, sin_tm, pos_col, pos_keys, ntile, kp):
            for ti in range(ntile):
                pap = pos_col(ti)
                em.dve(lambda e, pap=pap: e.tensor_scalar_mul(rts[1][:], c_invfl[:], pap),
                       r=("c_invfl",) + pos_keys, w=(kp + "1",))
                em.dve(lambda e, pap=pap: e.scalar_tensor_tensor(
                    rts[0][:], c_invfh[:], pap, rts[1][:], op0=ALU.mult, op1=ALU.add),
                    r=("c_invfh", kp + "1") + pos_keys, w=(kp + "0",))
                em.dve(lambda e: e.tensor_copy(rti[:], rts[0][:]),
                       r=(kp + "0",), w=(kp + "i",))
                em.dve(lambda e: e.tensor_copy(rts[2][:], rti[:]),
                       r=(kp + "i",), w=(kp + "2",))
                em.dve(lambda e: e.tensor_tensor(rts[1][:], rts[0][:], rts[2][:], op=ALU.subtract),
                       r=(kp + "0", kp + "2"), w=(kp + "1",))
                em.dve(lambda e: e.tensor_scalar_mul(rts[2][:], rts[1][:], TPI_LO),
                       r=(kp + "1",), w=(kp + "2",))
                em.dve(lambda e: e.scalar_tensor_tensor(
                    rts[0][:], rts[1][:], TPI_HI, rts[2][:], op0=ALU.mult, op1=ALU.add),
                    r=(kp + "1", kp + "2"), w=(kp + "0",))
                em.act(lambda e, ti=ti: e.activation(sin_tm[:, ti * 16:(ti + 1) * 16], rts[0][:], AF.Sin),
                       r=(kp + "0",), w=(kp + "sin",))
                em.dve(lambda e: e.tensor_scalar_add(rts[1][:], rts[0][:], HALFPI),
                       r=(kp + "0",), w=(kp + "1",))
                em.dve(lambda e: e.tensor_single_scalar(rts[2][:], rts[1][:], PI_F, op=ALU.is_gt),
                       r=(kp + "1",), w=(kp + "2",))
                em.dve(lambda e: e.tensor_scalar_mul(rts[2][:], rts[2][:], TPI),
                       r=(kp + "2",), w=(kp + "2",))
                em.dve(lambda e: e.tensor_tensor(rts[1][:], rts[1][:], rts[2][:], op=ALU.subtract),
                       r=(kp + "1", kp + "2"), w=(kp + "1",))
                em.act(lambda e, ti=ti: e.activation(cos_tm[:, ti * 16:(ti + 1) * 16], rts[1][:], AF.Sin),
                       r=(kp + "1",), w=(kp + "cos",))

        def rope_apply_tm(tile, tkey, col0, cos_sl, sin_sl, ck, sk):
            x1 = tile[:, col0:col0 + 16]
            x2 = tile[:, col0 + 16:col0 + 32]
            em.dve(lambda e: e.tensor_tensor(rp3[0][:], x1, cos_sl, op=ALU.mult),
                   r=(tkey, ck), w=("rp0",))
            em.dve(lambda e: e.tensor_tensor(rp3[1][:], x2, sin_sl, op=ALU.mult),
                   r=(tkey, sk), w=("rp1",))
            em.dve(lambda e: e.tensor_tensor(rp3[2][:], x1, sin_sl, op=ALU.mult),
                   r=(tkey, sk), w=("rp2",))
            em.dve(lambda e: e.tensor_tensor(x1, rp3[0][:], rp3[1][:], op=ALU.subtract),
                   r=("rp0", "rp1"), w=(tkey,))
            em.dve(lambda e: e.tensor_tensor(rp3[1][:], x2, cos_sl, op=ALU.mult),
                   r=(tkey, ck), w=("rp1",))
            em.dve(lambda e: e.tensor_tensor(x2, rp3[2][:], rp3[1][:], op=ALU.add),
                   r=("rp2", "rp1"), w=(tkey,))

        bg_queue = []

        def drain_bg(k):
            for _ in range(min(k, len(bg_queue))):
                bg_queue.pop(0)()

        rope_done = [False]

        for b in range(NB):
            xn_scr_key = f"xnscr{b}"
            scA = ExitStack()
            xn_fm = S(f"xn_fm{b}", (128, NCH * T), BF16, scA)
            r_mask = S(f"r_mask{b}", (8, T), BF16, scA)
            r_grid = S(f"r_grid{b}", (8, T), F32, scA)
            wbuf = [S(f"wbuf{b}_{i}", (128, 3072), BF16, scA) for i in range(2)]
            q_pair = [S(f"q_pair{b}_{i}", (128, T), BF16, scA) for i in range(2)]
            k_pair = [S(f"k_pair{b}_{i}", (128, T), BF16, scA) for i in range(2)]
            v_pair = [S(f"v_pair{b}_{i}", (128, NT * 192), BF16, scA) for i in range(2)]
            qsq = [S(f"qsq{b}_{i}", (128, T), BF16, scA) for i in range(2)]
            pt3 = [S(f"pt{b}_{i}", (128, 1024), BF16, scA) for i in range(4)]

            for sl_ in range(2):
                em.pool(lambda e, vp=v_pair[sl_]: e.memset(
                    vp[:].rearrange("p (k s) -> p k s", s=192)[:, :, 64:128], 1.0),
                    w=(f"v_ones{sl_}",), tag="vones")

            if en_dense and not rope_done[0]:
                rope_tables_tm(cos_tm, sin_tm, lambda ti: c_tok[:, ti:ti + 1],
                               ("c_tok",), NT, "rtk")
                rope_done[0] = True

            fq = []

            def drain_front(k):
                for _ in range(min(k, len(fq))):
                    fq.pop(0)()

            evq = []
            ev_par = [0]

            def drain_ev(k):
                for _ in range(min(k, len(evq))):
                    evq.pop(0)()

            fr_ctr = [0]

            def fpm():
                fr_ctr[0] += 1
                return (pm1, "pm1") if fr_ctr[0] % 2 == 0 else (pm2, "pm2")

            def front_closures(branch, p, sl):
                cl = []
                wb, wbkey = wbuf[sl], f"wbuf{sl}"
                qp, qk = q_pair[sl], f"q_pair{sl}"
                kp_, kk = k_pair[sl], f"k_pair{sl}"
                vp = v_pair[sl]
                qs, qsk = qsq[sl], f"qsq{sl}"

                if branch == "attn":
                    def add_w():
                        for wi, (dsrc, off) in enumerate(((d_wqT, 0), (d_wkT, 1024), (d_wvT, 2048))):
                            ap = dsrc[:, :].rearrange("p (c e) -> p c e", c=8)[:, :, p * 128:(p + 1) * 128]
                            em.dma_pool(lambda e, ap=ap, off=off, wb=wb: e.dma_start(
                                wb[:, off:off + 1024].rearrange("p (c e) -> p c e", c=8), ap),
                                r=(), w=(wbkey,), sem=f"dw{sl}", tag=f"ldw.a{p}.{wi}")
                    cl.append(add_w)

                    def wq_sl(c, wb=wb):
                        return wb[:, c * 128:(c + 1) * 128]

                    def wk_sl(c, wb=wb):
                        return wb[:, 1024 + c * 128:1024 + (c + 1) * 128]

                    def wv_sl(c, wb=wb):
                        return wb[:, 2048 + c * 128:2048 + (c + 1) * 128]
                else:
                    def add_w():
                        ap = d_dwqkv[:, :].rearrange("p (c e) -> p c e", c=8)[:, :, p * 384:(p + 1) * 384]
                        em.dma_pool(lambda e, ap=ap, wb=wb: e.dma_start(
                            wb[:, 0:3072].rearrange("p (c e) -> p c e", c=8), ap),
                            r=(), w=(wbkey,), sem=f"dw{sl}", tag=f"ldw.d{p}")
                    cl.append(add_w)

                    def wq_sl(c, wb=wb):
                        return wb[:, c * 384:c * 384 + 128]

                    def wk_sl(c, wb=wb):
                        return wb[:, c * 384 + 128:c * 384 + 256]

                    def wv_sl(c, wb=wb):
                        return wb[:, c * 384 + 256:c * 384 + 384]

                if branch == "attn":
                    for ns in range(4):
                        for (dst, dkey, wsl) in ((qp, qk, wq_sl), (kp_, kk, wk_sl)):
                            def projqk(dst=dst, dkey=dkey, wsl=wsl, ns=ns):
                                pm, pmk = fpm()
                                xnk = tuple(f"xnfm.{t4}.{g}" for t4 in range(ns * 4, ns * 4 + 4)
                                            for g in range(2))
                                for c in range(NCH):
                                    em.pe(lambda e, c=c, ns=ns, wsl=wsl, xn_fm=xn_fm, pm=pm: e.matmul(
                                        pm[:, 0:512], wsl(c), xn_fm[:, c * T + ns * 512:c * T + (ns + 1) * 512],
                                        start=(c == 0), stop=(c == NCH - 1)),
                                        r=(wbkey,) + xnk, w=(pmk,), tag="prj")
                                em.act(lambda e, ns=ns, dst=dst, pm=pm: e.activation(
                                    dst[:, ns * 512:(ns + 1) * 512], pm[:, 0:512], AF.Copy),
                                    r=(pmk,), w=(dkey,))
                            cl.append(projqk)
                else:
                    for (dst, dkey, wsl) in ((qp, qk, wq_sl), (kp_, kk, wk_sl)):
                        pend_b = [None]
                        for tt in range(NT):
                            def projtm_a(dst=dst, dkey=dkey, wsl=wsl, tt=tt, pend_b=pend_b):
                                pm, pmk = fpm()
                                qb, qbk = qtmb[tt % 2], f"qtmb{tt % 2}"
                                xnk = (f"xnfm.{tt}.0", f"xnfm.{tt}.1")
                                for c in range(NCH):
                                    em.pe(lambda e, c=c, tt=tt, wsl=wsl, xn_fm=xn_fm, pm=pm: e.matmul(
                                        pm[:, 0:128], xn_fm[:, c * T + tt * 128:c * T + (tt + 1) * 128], wsl(c),
                                        start=(c == 0), stop=(c == NCH - 1)),
                                        r=(wbkey,) + xnk, w=(pmk,), tag="prjtm")
                                em.act(lambda e, qb=qb, pm=pm: e.activation(qb[:], pm[:, 0:128], AF.Copy),
                                       r=(pmk,), w=(qbk,))
                                for hh in range(2):
                                    rope_apply_tm(qb, qbk, hh * 64,
                                                  cos_tm[:, tt * 16:(tt + 1) * 16],
                                                  sin_tm[:, tt * 16:(tt + 1) * 16], "rtkcos", "rtksin")

                                def projtm_b(qb=qb, qbk=qbk, tt=tt, dst=dst, dkey=dkey):
                                    em.pe(lambda e, qb=qb, tt=tt: e.transpose(
                                        ptr[:, (tt % 4) * 128:(tt % 4 + 1) * 128], qb[:], c_id[:]),
                                        r=(qbk, "c_id"), w=("ptr",), tag="trQ")
                                    if tt % 4 == 3:
                                        g = tt // 4
                                        em.act(lambda e, g=g, dst=dst: e.activation(
                                            dst[:, g * 512:(g + 1) * 512], ptr[:, 0:512], AF.Copy),
                                            r=("ptr",), w=(dkey,))
                                pend_b[0] = projtm_b

                            def projtm_ab(tt=tt, pend_b=pend_b, projtm_a=projtm_a):
                                bprev = pend_b[0]
                                projtm_a()
                                if bprev is not None:
                                    bprev()
                            cl.append(projtm_ab)

                        def projtm_flush(pend_b=pend_b):
                            if pend_b[0] is not None:
                                pend_b[0]()
                                pend_b[0] = None
                        cl.append(projtm_flush)
                for tt in range(NT):
                    def projv(tt=tt, wsl=wv_sl, vp=vp):
                        pm, pmk = fpm()
                        xnk = (f"xnfm.{tt}.0", f"xnfm.{tt}.1")
                        for c in range(NCH):
                            em.pe(lambda e, c=c, tt=tt, wsl=wsl, xn_fm=xn_fm, pm=pm: e.matmul(
                                pm[:, 0:128], xn_fm[:, c * T + tt * 128:c * T + (tt + 1) * 128], wsl(c),
                                start=(c == 0), stop=(c == NCH - 1)),
                                r=(wbkey,) + xnk, w=(pmk,), tag="prj.v")
                        em.dve(lambda e, tt=tt, vp=vp, pm=pm: e.tensor_copy(
                            vp[:, tt * 192:(tt + 1) * 192].rearrange(
                                "p (g c) -> p g c", c=64)[:, 0:3:2, :],
                            pm[:, 0:128].rearrange("p (g c) -> p g c", c=64)),
                            r=(pmk,), w=(f"v_pair{sl}.{tt}",))
                    cl.append(projv)

                def mk_msq(src_t, skey):
                    def msq(src_t=src_t, skey=skey):
                        em.act(lambda e, src_t=src_t, qs=qs: e.activation(qs[:], src_t[:], AF.Square),
                               r=(skey,), w=(qsk,))
                    return msq

                def mk_mred(col):
                    def mred(col=col):
                        for ns in range(4):
                            pm, pmk = fpm()
                            em.pe(lambda e, ns=ns, qs=qs, pm=pm: e.matmul(
                                pm[0:2, 0:512], c_bo[:], qs[:, ns * 512:(ns + 1) * 512],
                                start=True, stop=True),
                                r=("c_bo", qsk), w=(pmk,))
                            em.dve(lambda e, ns=ns, col=col, pm=pm: e.tensor_reduce(
                                nrm_s[0:2, col + ns:col + ns + 1], pm[0:2, 0:512], axis=AXX, op=ALU.max),
                                r=(pmk,), w=(f"nrm{col + ns}",))
                    return mred

                def mnorm_fin():
                    em.dve(lambda e: e.tensor_reduce(vtr_a[0:2, 2:3], nrm_s[0:2, 0:4], axis=AXX, op=ALU.max),
                           r=("nrm0", "nrm1", "nrm2", "nrm3"), w=("nq",))
                    em.dve(lambda e: e.tensor_reduce(vtr_a[0:2, 3:4], nrm_s[0:2, 4:8], axis=AXX, op=ALU.max),
                           r=("nrm4", "nrm5", "nrm6", "nrm7"), w=("nk",))
                    em.act(lambda e: e.activation(vtr_a[0:2, 4:5], vtr_a[0:2, 2:3], AF.Ln),
                           r=("nq",), w=("lnq",))
                    em.act(lambda e: e.activation(vtr_a[0:2, 5:6], vtr_a[0:2, 3:4], AF.Ln),
                           r=("nk",), w=("lnk",))
                    em.dve(lambda e: e.tensor_tensor(vtr_a[0:2, 6:7], vtr_a[0:2, 4:5], vtr_a[0:2, 5:6], op=ALU.add),
                           r=("lnq", "lnk"), w=("lnsum",))
                    em.act(lambda e: e.activation(vtr_a[0:2, 0:1], vtr_a[0:2, 6:7], AF.Exp, bias=0.0, scale=0.5),
                           r=("lnsum",), w=("vtra",))
                    em.dve(lambda e: e.tensor_scalar_mul(vtr_a[0:2, 0:1], vtr_a[0:2, 0:1], -0.125),
                           r=("vtra",), w=("vtra",))
                    em.dve(lambda e: e.transpose(vtr_b[:], vtr_a[:]), r=("vtra",), w=("vtrb",))
                    em.dve(lambda e: e.tensor_reduce(vtr_b[0:1, 2:3], vtr_b[0:1, 0:2],
                                                     axis=AXX, op=ALU.min),
                           r=("vtrb",), w=("vtrb",))

                def mnorm_bc():
                    em.pe(lambda e: e.matmul(pm1[:, 0:1], c_or[:], vtr_b[0:1, 2:3],
                                             start=True, stop=True),
                          r=("vtrb", "c_or"), w=("pm1",), tag="bcM")
                    em.act(lambda e: e.activation(bias_hd[sl][0][:], pm1[:, 0:1], AF.Copy),
                           r=("pm1",), w=(f"bias{sl}.0",))
                cl.append(mk_msq(qp, qk))
                cl.append(mk_mred(0))
                cl.append(mk_msq(kp_, kk))
                cl.append(mk_mred(4))
                cl.append(mnorm_fin)
                cl.append(mnorm_bc)
                return cl

            seq = []
            if en_attn:
                seq += [("attn", p, av_attn[p], f"ava{p}") for p in range(8)]
            if en_dense:
                seq += [("dense", p, av_dense[p], f"avd{p}") for p in range(4)]
            if seq:
                fq.extend(front_closures(seq[0][0], seq[0][1], 0))

            for tt in range(NT):
                xb, xkey = xt[tt % 2], f"xt{tt % 2}"
                xnb, xnkey = xnt[tt % 2], f"xnt{tt % 2}"
                src = x_in[b, tt * 128:(tt + 1) * 128, :]
                em.dma_sp(lambda e, d=xb, s=src: e.dma_start(d[:], s), r=(), w=(xkey,),
                          sem=f"dx{tt % 2}", tag=f"ldx{b}.{tt}")
                o = (tt % 2) * 8
                pk = str(tt % 2)
                em.dve(lambda e, xb=xb, o=o: e.tensor_reduce(ln_s[:, o:o + 1], xb[:], axis=AXX, op=ALU.add),
                       r=(xkey,), w=("ln0" + pk,))
                em.act(lambda e, xb=xb, o=o, tt=tt: e.activation(
                    junk2[:, (tt % 2) * H:(tt % 2) * H + H], xb[:], AF.Square, accum_out=ln_s[:, o + 1:o + 2]),
                    r=(xkey,), w=("ln1" + pk, "junk2." + pk))
                em.dve(lambda e, o=o: e.tensor_scalar_mul(ln_s[:, o + 2:o + 3], ln_s[:, o:o + 1], 1.0 / H),
                       r=("ln0" + pk,), w=("ln2" + pk,))
                em.dve(lambda e, o=o: e.tensor_tensor(ln_s[:, o + 3:o + 4], ln_s[:, o + 2:o + 3],
                                                      ln_s[:, o + 2:o + 3], op=ALU.mult),
                       r=("ln2" + pk,), w=("ln3" + pk,))
                em.dve(lambda e, o=o: e.scalar_tensor_tensor(ln_s[:, o + 4:o + 5], ln_s[:, o + 1:o + 2],
                                                             1.0 / H, ln_s[:, o + 3:o + 4],
                                                             op0=ALU.mult, op1=ALU.subtract),
                       r=("ln1" + pk, "ln3" + pk), w=("ln4" + pk,))
                em.act(lambda e, o=o: e.activation(ln_s[:, o + 5:o + 6], ln_s[:, o + 4:o + 5],
                                                   AF.Ln, bias=z_eps[:], scale=1.0),
                       r=("ln4" + pk, "z_eps"), w=("ln5" + pk,))
                em.act(lambda e, o=o: e.activation(ln_s[:, o + 6:o + 7], ln_s[:, o + 5:o + 6],
                                                   AF.Exp, bias=0.0, scale=-0.5),
                       r=("ln5" + pk,), w=("ln6" + pk,))
                em.dve(lambda e, xb=xb, xnb=xnb, o=o: e.scalar_tensor_tensor(
                    xnb[:], xb[:], ln_s[:, o + 2:o + 3], ln_s[:, o + 6:o + 7].to_broadcast([128, H]),
                    op0=ALU.subtract, op1=ALU.mult),
                    r=(xkey, "ln2" + pk, "ln6" + pk), w=(xnkey,))
                em.dma_sp(lambda e, xnb=xnb, d=xn_scr[b, tt * 128:(tt + 1) * 128, :]: e.dma_start(d, xnb[:]),
                          r=(xnkey,), w=(xn_scr_key + f".{tt}",), sem="dxn", tag=f"stxn{b}.{tt}")
                for g in range(2):
                    for c4 in range(4):
                        c = g * 4 + c4
                        em.pe(lambda e, xnb=xnb, c=c, c4=c4: e.transpose(
                            ptr[:, c4 * 128:(c4 + 1) * 128], xnb[:, c * 128:(c + 1) * 128], c_id[:]),
                            r=(xnkey, "c_id"), w=("ptr",))
                    em.dve(lambda e, g=g, tt=tt, xn_fm=xn_fm: e.tensor_copy(
                        xn_fm[:].rearrange("p (c t) -> p c t", c=NCH)[:, g * 4:(g + 1) * 4, tt * 128:(tt + 1) * 128],
                        ptr[:].rearrange("p (c t) -> p c t", c=4)),
                        r=("ptr",), w=(f"xnfm.{tt}.{g}",))
                if tt == 0:
                    drain_front(1)
                elif tt % 4 == 3:
                    drain_front(2)
            xn_keys = tuple(f"xnfm.{tt}.{g}" for tt in range(NT) for g in range(2))

            if en_mosa:
                for ns in range(4):
                    for c in range(NCH):
                        em.pe(lambda e, ns=ns, c=c, xn_fm=xn_fm: e.matmul(
                            pm1[0:8, 0:512], c_router[:, c * 8:(c + 1) * 8],
                            xn_fm[:, c * T + ns * 512:c * T + (ns + 1) * 512],
                            start=(c == 0), stop=(c == NCH - 1)),
                            r=("c_router",) + xn_keys, w=("pm1",), tag="router")
                    em.act(lambda e, ns=ns: e.activation(
                        r_scores[:, ns * 512:(ns + 1) * 512], pm1[0:8, 0:512], AF.Copy),
                        r=("pm1",), w=("r_scores",), tag="rsc")

                def bis_init():
                    em.dve(lambda e: e.tensor_reduce(b_hi[0][:], r_scores[:], axis=AXX, op=ALU.max),
                           r=("r_scores",), w=("b_hi0",))
                    em.dve(lambda e: e.tensor_reduce(b_lo[0][:], r_scores[:], axis=AXX, op=ALU.min),
                           r=("b_lo0", "r_scores"), w=("b_lo0",))
                    em.dve(lambda e: e.tensor_scalar_add(b_lo[0][:], b_lo[0][:], -1.0),
                           r=("b_lo0",), w=("b_lo0",))
                    em.dve(lambda e: e.tensor_scalar(b_mid[:], b_lo[0][:], b_hi[0][:], 0.5,
                                                     op0=ALU.add, op1=ALU.mult),
                           r=("b_lo0", "b_hi0"), w=("b_mid",))
                bg_queue.append(bis_init)

                def mk_iter(i):
                    cur, nxt = i % 2, (i + 1) % 2
                    cls = []
                    for cc in range(4):
                        cls.append(lambda cc=cc: em.dve(lambda e, cc=cc: e.tensor_scalar(
                            junk2[0:8, cc * 512:(cc + 1) * 512], r_scores[:, cc * 512:(cc + 1) * 512],
                            b_mid[:], 0.0, op0=ALU.is_gt, op1=ALU.add,
                            accum_out=b_c4[:, cc:cc + 1]),
                            r=("r_scores", "b_mid"), w=("junk2.0", "junk2.1", "b_c4")))

                    def tail():
                        em.dve(lambda e: e.tensor_reduce(b_cnt[:], b_c4[:], axis=AXX, op=ALU.add),
                               r=("b_c4",), w=("b_cnt",))
                        em.dve(lambda e: e.tensor_single_scalar(b_cond[:], b_cnt[:], float(KSEL), op=ALU.is_ge),
                               r=("b_cnt",), w=("b_cond",))
                        em.dve(lambda e: e.select(b_lo[nxt][:], b_cond[:], b_mid[:], b_lo[cur][:]),
                               r=("b_cond", "b_mid", f"b_lo{cur}"), w=(f"b_lo{nxt}",))
                        em.dve(lambda e: e.select(b_hi[nxt][:], b_cond[:], b_hi[cur][:], b_mid[:]),
                               r=("b_cond", "b_mid", f"b_hi{cur}"), w=(f"b_hi{nxt}",))
                        em.dve(lambda e: e.tensor_scalar(b_mid[:], b_lo[nxt][:], b_hi[nxt][:], 0.5,
                                                         op0=ALU.add, op1=ALU.mult),
                               r=(f"b_lo{nxt}", f"b_hi{nxt}"), w=("b_mid",))
                    cls.append(tail)
                    return cls
                for i in range(N_BISECT):
                    bg_queue.extend(mk_iter(i))
                lo_fin = f"b_lo{N_BISECT % 2}"

                def bis_post(r_mask=r_mask, r_grid=r_grid):
                    em.dve(lambda e: e.tensor_single_scalar(r_mask[:], r_scores[:],
                                                            b_lo[N_BISECT % 2][:], op=ALU.is_gt),
                           r=("r_scores", lo_fin), w=("r_mask",))
                    em.dve(lambda e: e.tensor_tensor_scan(r_grid[:], r_mask[:],
                                                          z_one[0:8, :].to_broadcast([8, T]),
                                                          0.0, op0=ALU.add, op1=ALU.mult),
                           r=("r_mask", "z_one"), w=("r_grid",))
                    em.dve(lambda e: e.scalar_tensor_tensor(r_grid[:], r_grid[:], 1.0, r_mask[:],
                                                            op0=ALU.mult, op1=ALU.mult),
                           r=("r_grid", "r_mask"), w=("r_grid",))
                bg_queue.append(bis_post)


            def emit_attention(branch, p, sl, av_tile, avkey):
                qp, qk = q_pair[sl], f"q_pair{sl}"
                kp_, kk = k_pair[sl], f"k_pair{sl}"
                vp = v_pair[sl]
                sct = [0]
                for w in range(4):
                    w0 = 512 * w
                    w1 = w0 + 512
                    kts = [kt for kt in range(NT) if kt * 128 < w1]
                    klast = kts[-1]

                    def emit_S_pair(idx, kt, w0=w0, w1=w1):
                        qstart = max(kt * 128, w0)
                        cw = w1 - qstart
                        slab, slk = pt3[idx % 4], f"pt{idx % 4}"
                        pss = []
                        for hb in (0, 64):
                            bi = sct[0] % 3
                            ps, psk = pS2[bi], f"pS{bi}"
                            sct[0] += 1
                            pss.append((hb, bi, ps, psk))
                            em.pe(lambda e, ps=ps, cw=cw, kt=kt, hb=hb, qa=qstart: e.matmul(
                                ps[:, 0:cw],
                                kp_[hb:hb + 64, kt * 128:(kt + 1) * 128],
                                qp[hb:hb + 64, qa:qa + cw],
                                start=True, stop=True, tile_position=(hb, 0)),
                                r=(kk, qk), w=(psk,), tag="S")
                        for hb, bi, ps, psk in pss:
                            so = (hb // 64) * 512
                            em.act(lambda e, slab=slab, ps=ps, cw=cw, so=so: e.activation(
                                slab[:, so:so + cw], ps[:, 0:cw], AF.Exp,
                                bias=bias_hd[sl][0][:], scale=0.125),
                                r=(psk, f"bias{sl}.0"),
                                w=(slk + ("a" if hb == 0 else "b"),), tag="exp")
                        if kt * 128 >= w0:
                            for hb in (0, 64):
                                so = (hb // 64) * 512
                                sk2 = slk + ("a" if hb == 0 else "b")
                                em.dve(lambda e, slab=slab, so=so: e.tensor_tensor(
                                    slab[:, so:so + 128], slab[:, so:so + 128],
                                    c_um[:], op=ALU.mult),
                                    r=(sk2, "c_um"), w=(sk2,), tag="dmask")

                    def emit_AV_pair(idx, kt, w0=w0, klast=klast):
                        qstart = max(kt * 128, w0)
                        cw = 512 - (qstart - w0)
                        co = qstart - w0
                        slab, slk = pt3[idx % 4], f"pt{idx % 4}"
                        for hb in (0, 64):
                            so = (hb // 64) * 512
                            vsl = vp[:, kt * 192 + hb:kt * 192 + hb + 128]
                            em.pe(lambda e, vsl=vsl, slab=slab, kt=kt, cw=cw, co=co,
                                  so=so, hb=hb: e.matmul(
                                pav[0:128, (hb // 64) * 512 + co:(hb // 64) * 512 + co + cw],
                                vsl, slab[:, so:so + cw],
                                start=(kt == 0), stop=(kt == klast)),
                                r=(slk + ("a" if hb == 0 else "b"),
                                   f"v_pair{sl}.{kt}", f"v_ones{sl}"),
                                w=(f"pavb{hb // 64}",), tag="AV")

                    def push_evac(hb, w0=w0):
                        pk = f"pavb{hb // 64}"
                        co = (hb // 64) * 512
                        pr = ev_par[0] % 2
                        ev_par[0] += 1
                        rbb, rbk = rb_sb[pr], f"rbsb{pr}"
                        lb = 64 - hb
                        evq.append(lambda: em.act(lambda e: e.activation(
                            rbb[:, 0:512], pav[lb:lb + 64, co:co + 512], AF.Ln),
                            r=(pk,), w=(rbk,), tag="lln"))
                        evq.append(lambda: em.act(lambda e: e.activation(
                            rbb[:, 0:512], rbb[:, 0:512], AF.Exp, bias=0.0, scale=-1.0),
                            r=(rbk,), w=(rbk,), tag="lexp"))

                        def ev_rest():
                            em.dve(lambda e: e.tensor_tensor(
                                av_tile[hb:hb + 64, w0:w0 + 512],
                                pav[hb:hb + 64, co:co + 512], rbb[:, 0:512], op=ALU.mult),
                                r=(pk, rbk), w=(avkey,), tag="avn")
                        evq.append(ev_rest)

                    SK = 3
                    for idx, kt in enumerate(kts):
                        emit_S_pair(idx, kt)
                        drain_ev(1)
                        drain_front(1)
                        drain_bg(1)
                        if idx >= SK:
                            emit_AV_pair(idx - SK, kts[idx - SK])
                            drain_ev(1)
                            drain_front(1)
                    for idx in range(max(0, len(kts) - SK), len(kts)):
                        emit_AV_pair(idx, kts[idx])
                        drain_ev(1)
                        drain_front(1)
                        drain_bg(1)
                    for hb in (0, 64):
                        push_evac(hb)
                    while evq:
                        drain_ev(1)
                        drain_front(1)

            for i_, (br, p, avt, avk) in enumerate(seq):
                sl = i_ % 2
                if i_ == 0:
                    drain_front(10 ** 9)
                if i_ + 1 < len(seq):
                    fq.extend(front_closures(seq[i_ + 1][0], seq[i_ + 1][1], (i_ + 1) % 2))
                emit_attention(br, p, sl, avt, avk)
                drain_front(10 ** 9)
            drain_bg(10 ** 9)

            if en_mosa:
                for tt in range(NT):
                    em.pe(lambda e, tt=tt, r_grid=r_grid: e.transpose(
                        pav[:, tt * 8:tt * 8 + 8], r_grid[:, tt * 128:(tt + 1) * 128], c_idf[0:8, 0:8]),
                        r=("r_grid", "c_idf"), w=("pavb0",), tag="trG")
                em.dve(lambda e: e.tensor_copy(g_tm[:], pav[:, 0:128]),
                       r=("pavb0",), w=("g_tm",))
                for tt in range(NT):
                    em.pe(lambda e, tt=tt: e.transpose(
                        pm1[:, tt * 8:tt * 8 + 8], r_scores[:, tt * 128:(tt + 1) * 128], c_idf[0:8, 0:8]),
                        r=("r_scores", "c_idf"), w=("pm1",), tag="trS")
                em.dve(lambda e: e.tensor_copy(
                    sc_tm[:].rearrange("p (k n) -> p k n", n=9)[:, :, 1:9],
                    pm1[:, 0:128].rearrange("p (k n) -> p k n", n=8)),
                    r=("pm1",), w=("sc_tm",))
                em.pool(lambda e: e.tensor_copy(
                    sc_tm[:].rearrange("p (k n) -> p k n", n=9)[:, :, 0:1],
                    c_tok[:].rearrange("p (k one) -> p k one", one=1)),
                    r=("c_tok",), w=("sc_tm",))

            em.barrier()
            scA.close()

            scB = ExitStack()
            c_iotar = S(f"c_iotar{b}", (128, T), F32, scB)
            av_mosa = [S(f"av_mosa{b}_{p}", (128, T), BF16, scB) for p in range(4)]
            wo_all = S(f"wo_all{b}", (128, 8192), BF16, scB)
            dwo_all = S(f"dwo_all{b}", (128, 4096), BF16, scB)
            mwo_all = S(f"mwo_all{b}", (128, 4096), BF16, scB)
            em.dma_sp(lambda e, dd=c_iotar[:], ss=d_iota_row[:, :]: e.dma_start(dd, ss),
                      r=(), w=("c_iotar",), sem="dwo", tag="ld:c_iotar")
            for (dst_, src_, k_) in ((wo_all, d_woT, "wo_all"), (dwo_all, d_dwo, "dwo_all"),
                                     (mwo_all, d_mwo, "mwo_all")):
                em.dma_pool(lambda e, dd=dst_[:], ss=src_[:, :]: e.dma_start(dd, ss),
                            r=(), w=(k_,), sem="dwo", tag="ld:" + k_)
            em.fence_keys(("c_iotar", "wo_all", "dwo_all", "mwo_all"))
            if en_mosa:
                wbufm = [S(f"wbm{b}_{i}", (128, 1536), BF16, scB) for i in range(2)]
                oh_t = [S(f"oh_t{b}_{i}", (128, KSEL), F32, scB) for i in range(2)]
                idxsc8 = S(f"idxsc8{b}", (128, 32), F32, scB)
                cmT = S(f"cmT{b}", (2, 256), F32, scB)
                idx_u = S(f"idx_u{b}", (128, 2), U32, scB)
                gate_c = S(f"gate_c{b}", (128, 2), F32, scB)
                gate_e = S(f"gate_e{b}", (128, 2), F32, scB)
                xg = [S(f"xg{b}_{i}", (128, H), BF16, scB) for i in range(2)]
                xg_fm = S(f"xg_fm{b}", (128, NCH * KSEL), BF16, scB)
                mq8 = S(f"mq8{b}", (64, 8 * KSEL), BF16, scB)
                mk8 = S(f"mk8{b}", (64, 8 * KSEL), BF16, scB)
                mv8 = S(f"mv8{b}", (128, 8 * 130), BF16, scB)
                msq = S(f"msq{b}", (64, KSEL), BF16, scB)
                mpt = S(f"mpt{b}", (128, 3 * 128), BF16, scB)
                mavtm = S(f"mavtm{b}", (128, 2 * D), BF16, scB)
                bias_m = S(f"bias_m{b}", (128, 1), F32, scB)
                oh_rt = [S(f"oh_rt{b}_{i}", (128, 512), BF16, scB) for i in range(2)]

                em.pool(lambda e, mv8=mv8: e.memset(
                    mv8[:].rearrange("p (h s) -> p h s", s=130)[:, :, 64:130:65], 1.0),
                    w=("mv_ones",), tag="mvones")

                for hh in range(8):
                    wb, wbkey = wbufm[hh % 2], f"wbm{hh % 2}"
                    ap = d_mwqkv[:, :].rearrange("p (c e) -> p c e", c=8)[:, :, hh * 192:(hh + 1) * 192]
                    em.dma_pool(lambda e, ap=ap, wb=wb: e.dma_start(
                        wb[:, 0:1536].rearrange("p (c e) -> p c e", c=8), ap),
                        r=(), w=(wbkey,), sem=f"dw{hh % 2}", tag=f"ldmw{hh}")
                    for tt in range(NT):
                        oh, ohk = oh_t[tt % 2], f"oh{tt % 2}"
                        em.dve(lambda e, oh=oh, tt=tt, hh=hh: e.tensor_single_scalar(
                            oh[:], c_iota1[:], g_tm[:, tt * 8 + hh:tt * 8 + hh + 1], op=ALU.is_equal),
                            r=("c_iota1", "g_tm"), w=(ohk,))
                        em.pe(lambda e, oh=oh, tt=tt, hh=hh: e.matmul(
                            pm1[0:2, 0:256],
                            sc_tm[:, tt * 9:tt * 9 + hh + 2:hh + 1],
                            oh[:, 0:256],
                            start=(tt == 0), stop=(tt == NT - 1)),
                            r=(ohk, "sc_tm"), w=("pm1",), tag="cmm")
                    em.act(lambda e: e.activation(cmT[:, 0:256], pm1[0:2, 0:256], AF.Copy),
                           r=("pm1",), w=("cmT",))
                    for half in range(2):
                        em.pe(lambda e, half=half: e.transpose(
                            pm2[0:128, half * 2:half * 2 + 2],
                            cmT[:, half * 128:(half + 1) * 128], c_idf[0:2, 0:2]),
                            r=("cmT", "c_idf"), w=("pm2",), tag="cmtr")
                    em.act(lambda e, hh=hh, idxsc8=idxsc8: e.activation(
                        idxsc8[:, hh * 4:hh * 4 + 4], pm2[0:128, 0:4], AF.Copy),
                        r=("pm2",), w=(f"idxsc.{hh}",))
                    em.dve(lambda e, hh=hh, idx_u=idx_u, idxsc8=idxsc8: e.tensor_copy(
                        idx_u[:], idxsc8[:, hh * 4:hh * 4 + 4:2]),
                        r=(f"idxsc.{hh}",), w=("idx_u",))
                    for rs in range(2):
                        em.dma_pool(lambda e, rs=rs, b=b, xg=xg, idx_u=idx_u: e.indirect_dma_start(
                            xg[rs][:], None,
                            xn_scr[:, :, :].rearrange("b t h -> (b t) h"),
                            bass.IndirectOffsetOnAxis(ap=idx_u[:, rs:rs + 1], axis=0),
                            element_offset=b * T * H),
                            r=("idx_u",) + tuple(xn_scr_key + f".{tt}" for tt in range(NT)),
                            w=(f"xg{rs}",), sem=f"dg{rs}", tag=f"gather{hh}.{rs}")
                    for rs in range(2):
                        for cg in range(2):
                            for c4 in range(4):
                                c = cg * 4 + c4
                                em.pe(lambda e, rs=rs, c=c, c4=c4, xg=xg: e.transpose(
                                    ptr[:, c4 * 128:(c4 + 1) * 128], xg[rs][:, c * 128:(c + 1) * 128], c_id[:]),
                                    r=(f"xg{rs}", "c_id"), w=("ptr",), tag="trXg")
                            em.dve(lambda e, rs=rs, cg=cg, xg_fm=xg_fm: e.tensor_copy(
                                xg_fm[:].rearrange("p (c r) -> p c r", c=NCH)[:, cg * 4:(cg + 1) * 4, rs * 128:(rs + 1) * 128],
                                ptr[:].rearrange("p (c t) -> p c t", c=4)),
                                r=("ptr",), w=("xg_fm",))
                    rope_tables_tm(mcos_tm, msin_tm,
                                   lambda ti, hh=hh, idxsc8=idxsc8: idxsc8[:, hh * 4 + ti * 2:hh * 4 + ti * 2 + 1],
                                   (f"idxsc.{hh}",), 2, "mrtk")
                    for (dst8, dk8, woff, ck, sk) in (
                            (mq8, f"mq.{hh}", 0, "mrtkcos", "mrtksin"),
                            (mk8, f"mk.{hh}", 64, "mrtkcos", "mrtksin")):
                        for rs in range(2):
                            qb, qbk = qtmb[rs % 2], f"qtmb{rs % 2}"
                            for c in range(NCH):
                                em.pe(lambda e, c=c, rs=rs, wb=wb, xg_fm=xg_fm, woff=woff: e.matmul(
                                    pm1[:, 0:64], xg_fm[:, c * KSEL + rs * 128:c * KSEL + (rs + 1) * 128],
                                    wb[:, c * 192 + woff:c * 192 + woff + 64],
                                    start=(c == 0), stop=(c == NCH - 1)),
                                    r=(wbkey, "xg_fm"), w=("pm1",), tag="mqk")
                            em.act(lambda e, qb=qb: e.activation(qb[:, 0:64], pm1[:, 0:64], AF.Copy),
                                   r=("pm1",), w=(qbk,))
                            rope_apply_tm(qb, qbk, 0,
                                          mcos_tm[:, rs * 16:(rs + 1) * 16],
                                          msin_tm[:, rs * 16:(rs + 1) * 16], ck, sk)
                            em.pe(lambda e, qb=qb, rs=rs: e.transpose(
                                ptr[0:64, rs * 128:(rs + 1) * 128], qb[:, 0:64], c_id[:]),
                                r=(qbk, "c_id"), w=("ptr",), tag="trmQ")
                        em.act(lambda e, dst8=dst8, hh=hh: e.activation(
                            dst8[:, hh * KSEL:(hh + 1) * KSEL], ptr[0:64, 0:KSEL], AF.Copy),
                            r=("ptr",), w=(dk8,))
                    for rs in range(2):
                        for c in range(NCH):
                            em.pe(lambda e, c=c, rs=rs, wb=wb, xg_fm=xg_fm: e.matmul(
                                pm1[:, 0:64], xg_fm[:, c * KSEL + rs * 128:c * KSEL + (rs + 1) * 128],
                                wb[:, c * 192 + 128:c * 192 + 192],
                                start=(c == 0), stop=(c == NCH - 1)),
                                r=(wbkey, "xg_fm"), w=("pm1",), tag="mv")
                        em.act(lambda e, rs=rs, hh=hh, mv8=mv8: e.activation(
                            mv8[:, hh * 130 + rs * 65:hh * 130 + rs * 65 + 64], pm1[:, 0:64], AF.Copy),
                            r=("pm1",), w=(f"mv.{hh}.{rs}",))

                for hh in range(8):
                    mqv = mq8[:, hh * KSEL:(hh + 1) * KSEL]
                    mkv = mk8[:, hh * KSEL:(hh + 1) * KSEL]
                    em.act(lambda e, mqv=mqv, msq=msq: e.activation(msq[:], mqv, AF.Square),
                           r=(f"mq.{hh}",), w=("msq",))
                    em.pe(lambda e, msq=msq: e.matmul(pm1[0:1, 0:KSEL], c_bo[0:64, 0:1], msq[:],
                                                      start=True, stop=True),
                          r=("c_bo", "msq"), w=("pm1",), tag="mnq")
                    em.dve(lambda e: e.tensor_reduce(nrm_s[0:1, 0:1], pm1[0:1, 0:KSEL], axis=AXX, op=ALU.max),
                           r=("pm1",), w=("nq",))
                    em.act(lambda e, mkv=mkv, msq=msq: e.activation(msq[:], mkv, AF.Square),
                           r=(f"mk.{hh}",), w=("msq",))
                    em.pe(lambda e, msq=msq: e.matmul(pm1[0:1, 0:KSEL], c_bo[0:64, 0:1], msq[:],
                                                      start=True, stop=True),
                          r=("c_bo", "msq"), w=("pm1",), tag="mnk")
                    em.dve(lambda e: e.tensor_reduce(nrm_s[0:1, 4:5], pm1[0:1, 0:KSEL], axis=AXX, op=ALU.max),
                           r=("pm1",), w=("nk",))
                    em.act(lambda e: e.activation(nrm_s[0:1, 1:2], nrm_s[0:1, 0:1], AF.Ln),
                           r=("nq",), w=("lnq",))
                    em.act(lambda e: e.activation(nrm_s[0:1, 5:6], nrm_s[0:1, 4:5], AF.Ln),
                           r=("nk",), w=("lnk",))
                    em.dve(lambda e: e.tensor_tensor(nrm_s[0:1, 2:3], nrm_s[0:1, 1:2], nrm_s[0:1, 5:6], op=ALU.add),
                           r=("lnq", "lnk"), w=("lnsum",))
                    em.act(lambda e: e.activation(nrm_s[0:1, 3:4], nrm_s[0:1, 2:3], AF.Exp, bias=0.0, scale=0.5),
                           r=("lnsum",), w=("Mt",))
                    em.dve(lambda e: e.tensor_scalar_mul(nrm_s[0:1, 3:4], nrm_s[0:1, 3:4], -0.125),
                           r=("Mt",), w=("Mt",))
                    em.pe(lambda e: e.matmul(pm1[:, 0:1], c_or[:], nrm_s[0:1, 3:4], start=True, stop=True),
                          r=("Mt", "c_or"), w=("pm1",), tag="mbc")
                    em.act(lambda e, bias_m=bias_m: e.activation(bias_m[:], pm1[:, 0:1], AF.Copy),
                           r=("pm1",), w=("bias_m",))

                    tiles = [(0, 0), (0, 1), (1, 1)]
                    for ti, (kt, qt) in enumerate(tiles):
                        em.pe(lambda e, ti=ti, kt=kt, qt=qt, hh=hh, mq8=mq8, mk8=mk8: e.matmul(
                            pS2[0][:, ti * 128:(ti + 1) * 128],
                            mk8[:, hh * KSEL + kt * 128:hh * KSEL + (kt + 1) * 128],
                            mq8[:, hh * KSEL + qt * 128:hh * KSEL + (qt + 1) * 128],
                            start=True, stop=(kt != qt)),
                            r=(f"mk.{hh}", f"mq.{hh}"), w=("pS0",), tag="mS")
                        if kt == qt:
                            em.pe(lambda e, ti=ti: e.matmul(
                                pS2[0][:, ti * 128:(ti + 1) * 128], c_id[:], c_cm[:],
                                start=False, stop=True),
                                r=("c_id", "c_cm"), w=("pS0",), tag="mmask")
                    em.act(lambda e, mpt=mpt, bias_m=bias_m: e.activation(
                        mpt[:, 0:384], pS2[0][:, 0:384], AF.Exp, bias=bias_m[:], scale=0.125),
                        r=("pS0", "bias_m"), w=("mpt",), tag="mexp")
                    for ti, (kt, qt) in enumerate(tiles):
                        em.pe(lambda e, ti=ti, kt=kt, qt=qt, hh=hh, mpt=mpt, mv8=mv8: e.matmul(
                            pav[:, qt * 65:qt * 65 + 65], mpt[:, ti * 128:(ti + 1) * 128],
                            mv8[:, hh * 130 + kt * 65:hh * 130 + kt * 65 + 65],
                            start=(kt == 0), stop=(kt == qt)),
                            r=("mpt", f"mv.{hh}.{kt}", "mv_ones"), w=("pavb0",), tag="mAV")
                    em.act(lambda e, hh=hh, gate_e=gate_e, idxsc8=idxsc8: e.activation(
                        gate_e[:], idxsc8[:, hh * 4 + 1:hh * 4 + 4:2], AF.Exp, bias=0.0, scale=-1.0),
                        r=(f"idxsc.{hh}",), w=("gate_e",))
                    em.dve(lambda e, gate_e=gate_e: e.tensor_scalar_add(gate_e[:], gate_e[:], 1.0),
                           r=("gate_e",), w=("gate_e",))
                    em.dve(lambda e, gate_c=gate_c, gate_e=gate_e: e.reciprocal(gate_c[:], gate_e[:]),
                           r=("gate_e",), w=("gate_c",))
                    em.dve(lambda e: e.reciprocal(
                        nrm_s[:, 0:2], pav[:, 64:130:65]),
                        r=("pavb0",), w=("mrcp",))
                    for rs in range(2):
                        em.dve(lambda e, rs=rs, mavtm=mavtm, gate_c=gate_c: e.tensor_scalar(
                            mavtm[:, rs * 64:(rs + 1) * 64], pav[:, rs * 65:rs * 65 + 64],
                            nrm_s[:, rs:rs + 1], gate_c[:, rs:rs + 1], op0=ALU.mult, op1=ALU.mult),
                            r=("pavb0", "mrcp", "gate_c"), w=("mavtm",))
                    hb2 = (hh % 2) * 64
                    for ns in range(4):
                        for rs in range(2):
                            em.dve(lambda e, rs=rs, ns=ns, hh=hh, oh_rt=oh_rt, idxsc8=idxsc8: e.tensor_single_scalar(
                                oh_rt[rs][:], c_iotar[:, ns * 512:(ns + 1) * 512],
                                idxsc8[:, hh * 4 + rs * 2:hh * 4 + rs * 2 + 1], op=ALU.is_equal),
                                r=("c_iotar", f"idxsc.{hh}"), w=(f"ohrt{rs}",))
                        em.pe(lambda e, ns=ns, mavtm=mavtm, oh_rt=oh_rt: e.matmul(
                            pm1[0:64, 0:512], mavtm[:, 0:64], oh_rt[0][:],
                            start=True, stop=False),
                            r=("mavtm", "ohrt0"), w=("pm1",), tag="msc")
                        em.pe(lambda e, ns=ns, mavtm=mavtm, oh_rt=oh_rt: e.matmul(
                            pm1[0:64, 0:512], mavtm[:, 64:128], oh_rt[1][:],
                            start=False, stop=True),
                            r=("mavtm", "ohrt1"), w=("pm1",), tag="msc2")
                        em.act(lambda e, ns=ns, hh=hh, hb2=hb2: e.activation(
                            av_mosa[hh // 2][hb2:hb2 + 64, ns * 512:(ns + 1) * 512],
                            pm1[0:64, 0:512], AF.Copy),
                            r=("pm1",), w=(f"avm{hh // 2}",))

            srcs = []
            if en_attn:
                srcs += [(av_attn[p], wo_all, p, (f"ava{p}",), "wo_all") for p in range(8)]
            if en_dense:
                srcs += [(av_dense[p], dwo_all, p, (f"avd{p}",), "dwo_all") for p in range(4)]
            if en_mosa:
                srcs += [(av_mosa[p], mwo_all, p, (f"avm{p}",), "mwo_all") for p in range(4)]
            obanks = [(pS2[0], "pS0"), (pS2[1], "pS1"), (pS2[2], "pS2"),
                      (pav[:, 0:512], "pavb0"), (pav[:, 512:1024], "pavb1")]
            obc = [0]

            def ldx(tt):
                xb, xkey = xt[tt % 2], f"xt{tt % 2}"
                src = x_in[b, tt * 128:(tt + 1) * 128, :]
                em.dma_sp(lambda e, d=xb, s=src: e.dma_start(d[:], s), r=(), w=(xkey,),
                          sem=f"dx{tt % 2}", tag=f"rldx{b}.{tt}")
            ldx(0)
            ldx(1)
            for tt in range(NT):
                xb, xkey = xt[tt % 2], f"xt{tt % 2}"
                ob, obkey = out_sb[tt % 2], f"osb{tt % 2}"
                for ns in range(2):
                    pb, pbk = obanks[obc[0] % 5]
                    obc[0] += 1
                    if srcs:
                        for ki, (av, wsb, p, avkeys, wkey) in enumerate(srcs):
                            em.pe(lambda e, pb=pb, av=av, wsb=wsb, p=p, tt=tt, ns=ns,
                                  ki=ki, nsr=len(srcs): e.matmul(
                                pb[:, 0:512], av[:, tt * 128:(tt + 1) * 128],
                                wsb[:, p * 1024 + ns * 512:p * 1024 + (ns + 1) * 512],
                                start=(ki == 0), stop=(ki == nsr - 1)),
                                r=avkeys + (wkey,), w=(pbk,), tag="oprj")
                        em.dve(lambda e, pb=pb, ob=ob, ns=ns, xb=xb: e.tensor_tensor(
                            ob[:, ns * 512:(ns + 1) * 512], pb[:, 0:512],
                            xb[:, ns * 512:(ns + 1) * 512], op=ALU.add),
                            r=(pbk, xkey), w=(obkey,), tag="oadd")
                    else:
                        em.dve(lambda e, ob=ob, ns=ns, xb=xb: e.tensor_copy(
                            ob[:, ns * 512:(ns + 1) * 512], xb[:, ns * 512:(ns + 1) * 512]),
                            r=(xkey,), w=(obkey,), tag="ocp")
                em.dma_sp(lambda e, ob=ob, d=out_t[b, tt * 128:(tt + 1) * 128, :]: e.dma_start(d, ob[:]),
                          r=(obkey,), w=(), sem="dout", tag=f"stout{b}.{tt}")
                if tt + 2 < NT:
                    ldx(tt + 2)
            em.barrier()
            scB.close()

        if tap_outs:
            for nm, (dram, sb) in tap_outs.items():
                em.dma_sp(lambda e, dram=dram, sb=sb: e.dma_start(dram[:, :], sb[:]),
                          r=(), w=(), sem="dtap", tag="tap:" + nm)
        em.barrier()
        em.finalize_into(nc, sems, dma_sems)
    return nc


_CACHE = {}


def kernel(**inputs):
    x = np.asarray(inputs["L0_attn_x"])
    host = prep_host(inputs)
    consts = host_consts()
    if "full" not in _CACHE:
        _CACHE["full"] = build_nc()
    nc = _CACHE["full"]
    in_maps = []
    for core in range(8):
        m = {"x": np.ascontiguousarray(x[core * NB:(core + 1) * NB])}
        m.update(host)
        m.update(consts)
        in_maps.append(m)
    res = run_bass_kernel_spmd(nc, in_maps, core_ids=list(range(8)))
    out = np.concatenate([np.asarray(r["out"]) for r in res.results], axis=0)
    return out.astype(np.float32)


# version 19
# speedup vs baseline: 1.0918x; 1.0353x over previous
import numpy as np
import ml_dtypes
import concourse.bass as bass
import concourse.mybir as mybir
from concourse.bass_utils import run_bass_kernel_spmd
from contextlib import ExitStack

F32 = mybir.dt.float32
BF16 = mybir.dt.bfloat16
U32 = mybir.dt.uint32
U8 = mybir.dt.uint8
I32 = mybir.dt.int32
AF = mybir.ActivationFunctionType
ALU = mybir.AluOpType
AXX = mybir.AxisListType.X

T = 2048
H = 1024
D = 64
NB = 2
NT = T // 128
NCH = H // 128
KSEL = 256
NEG = -1e30
N_BISECT = 60

TPI = 2.0 * np.pi
TPI_HI = float(np.float32(TPI))
TPI_LO = float(np.float32(TPI - np.float64(np.float32(TPI))))
PI_F = float(np.float32(np.pi))
HALFPI = float(np.float32(np.pi / 2))


class EM:
    ENGS = ("PE", "ACT", "DVE", "POOL", "SP")

    DMA_SEMS = ("dx0", "dx1", "dxn", "dout", "dpos", "dcon", "dconp",
                "dw0", "dw1", "dwo", "dg0", "dg1", "dtap")

    def __init__(self):
        self.ops = []
        self.last_w = {}
        self.readers = {}
        self.floor = None
        self.dma_idx = {s: 0 for s in self.DMA_SEMS}

    def op(self, eng, fn, r=(), w=(), dma=None, tag=""):
        o = {"eng": eng, "fn": fn, "dma": dma, "deps": [], "inc": False,
             "tag": tag, "i": len(self.ops)}
        if dma is not None:
            self.dma_idx[dma] += 1
            o["dma_i"] = self.dma_idx[dma]
        deps = {}
        if self.floor is not None:
            deps[self.floor["i"]] = self.floor
        for k in r:
            d = self.last_w.get(k)
            if d is not None:
                deps[d["i"]] = d
        for k in w:
            d = self.last_w.get(k)
            if d is not None:
                deps[d["i"]] = d
            for rd in self.readers.get(k, ()):
                deps[rd["i"]] = rd
        for d in deps.values():
            if d is o:
                continue
            if d["dma"] is None and d["eng"] == "PE" and eng == "PE" and o["dma"] is None:
                continue
            d["inc"] = True
            o["deps"].append(d)
        for k in w:
            self.last_w[k] = o
            self.readers[k] = []
        wset = set(w)
        for k in r:
            if k not in wset:
                self.readers.setdefault(k, []).append(o)
        self.ops.append(o)
        return o

    def pe(self, fn, r=(), w=(), tag=""):
        return self.op("PE", fn, r, w, tag=tag)

    def act(self, fn, r=(), w=(), tag=""):
        return self.op("ACT", fn, r, w, tag=tag)

    def dve(self, fn, r=(), w=(), tag=""):
        return self.op("DVE", fn, r, w, tag=tag)

    def pool(self, fn, r=(), w=(), tag=""):
        return self.op("POOL", fn, r, w, tag=tag)

    def dma_sp(self, fn, r=(), w=(), sem="dcon", tag=""):
        return self.op("SP", fn, r, w, dma=sem, tag=tag)

    def dma_pool(self, fn, r=(), w=(), sem="dconp", tag=""):
        return self.op("POOL", fn, r, w, dma=sem, tag=tag)

    def fence_keys(self, keys, eng="POOL"):
        return self.op(eng, lambda e: e.nop(), r=tuple(keys), w=tuple(keys), tag="kfence")

    def barrier(self):
        o = self.op("SP", lambda e: e.nop(), (), (), tag="fence")
        deps = {}
        for d in self.last_w.values():
            deps[d["i"]] = d
        for rl in self.readers.values():
            for d in rl:
                deps[d["i"]] = d
        o["deps"] = []
        for d in deps.values():
            if d is o:
                continue
            d["inc"] = True
            o["deps"].append(d)
        o["inc"] = True
        self.last_w = {k: o for k in self.last_w}
        self.readers = {}
        self.floor = o
        return o

    def finalize_into(self, nc, sems, dma_sems):
        cnt = {e: 0 for e in self.ENGS}
        for o in self.ops:
            if o["dma"] is None and o["inc"]:
                cnt[o["eng"]] += 1
                o["inc_no"] = cnt[o["eng"]]
        for o in self.ops:
            waits = {}
            for d in o["deps"]:
                if d["dma"] is not None:
                    s, v = d["dma"], 16 * d["dma_i"]
                else:
                    s, v = d["eng"], d["inc_no"]
                if waits.get(s, 0) < v:
                    waits[s] = v
            o["waits"] = waits
        allsems = dict(sems)
        allsems.update(dma_sems)

        def emit_for(ename, eng):
            wm = {}
            for o in self.ops:
                if o["eng"] != ename:
                    continue
                for s, v in sorted(o["waits"].items()):
                    if wm.get(s, 0) >= v:
                        continue
                    eng.wait_ge(allsems[s], v)
                    wm[s] = v
                inst = o["fn"](eng)
                if o["dma"] is not None:
                    inst.then_inc(allsems[o["dma"]], 16)
                elif o["inc"]:
                    inst.then_inc(allsems[ename], 1)

        with nc.Block() as block:
            @block.tensor
            def _(e):
                emit_for("PE", e)

            @block.scalar
            def _(e):
                emit_for("ACT", e)

            @block.vector
            def _(e):
                emit_for("DVE", e)

            @block.gpsimd
            def _(e):
                emit_for("POOL", e)

            @block.sync
            def _(e):
                emit_for("SP", e)


def _chunk_major(a, nchunk):
    rows, C = a.shape
    assert rows == nchunk * 128
    return np.ascontiguousarray(
        a.reshape(nchunk, 128, C).transpose(1, 0, 2).reshape(128, nchunk * C))


def prep_host(inp):
    h = {}
    h["wqT"] = _chunk_major(np.ascontiguousarray(np.asarray(inp["L0_attn_wq"]).T), 8)
    h["wkT"] = _chunk_major(np.ascontiguousarray(np.asarray(inp["L0_attn_wk"]).T), 8)
    h["wvT"] = _chunk_major(np.ascontiguousarray(np.asarray(inp["L0_attn_wv"]).T), 8)
    h["woT"] = _chunk_major(np.ascontiguousarray(np.asarray(inp["L0_attn_wo"]).T), 8)
    dw = np.asarray(inp["L0_ffn_dense_wqkv"])
    blocks = []
    for p in range(4):
        blocks.append(np.concatenate(
            [dw[:, 2 * p, 0:64], dw[:, 2 * p + 1, 0:64],
             dw[:, 2 * p, 64:128], dw[:, 2 * p + 1, 64:128],
             dw[:, 2 * p, 128:192], dw[:, 2 * p + 1, 128:192]], axis=1))
    h["dwqkv"] = _chunk_major(np.concatenate(blocks, axis=1), 8)
    mw = np.asarray(inp["L0_ffn_mosa_wqkv"])
    blocks = []
    for hh in range(8):
        blocks.append(np.concatenate(
            [mw[:, hh, 0:64], mw[:, hh, 64:128], mw[:, hh, 128:192]], axis=1))
    h["mwqkv"] = _chunk_major(np.concatenate(blocks, axis=1), 8)
    h["dwo"] = _chunk_major(np.asarray(inp["L0_ffn_dense_wo"]), 4)
    h["mwo"] = _chunk_major(np.asarray(inp["L0_ffn_mosa_wo"]), 4)
    h["router"] = _chunk_major(np.asarray(inp["L0_ffn_mosa_router_w"]), 8)
    return h


def host_consts():
    c = {}
    ident = np.eye(128, dtype=np.float32)
    c["identb"] = ident.astype(ml_dtypes.bfloat16)
    c["identf"] = ident
    c["cmaskb"] = (np.tril(np.ones((128, 128), dtype=np.float32), -1) * NEG).astype(ml_dtypes.bfloat16)
    c["umaskb"] = np.triu(np.ones((128, 128), dtype=np.float32)).astype(ml_dtypes.bfloat16)
    bo = np.zeros((128, 2), dtype=np.float32)
    bo[0:64, 0] = 1.0
    bo[64:128, 1] = 1.0
    c["blockones"] = bo.astype(ml_dtypes.bfloat16)
    c["onesrow"] = np.ones((1, 128), dtype=np.float32)
    j = np.arange(16, dtype=np.float64)
    v = (10000.0 ** (-j / 16.0)) / (2.0 * np.pi)
    hi = v.astype(np.float32)
    lo = (v - hi.astype(np.float64)).astype(np.float32)
    c["invfh"] = np.ascontiguousarray(np.broadcast_to(hi, (128, 16)))
    c["invfl"] = np.ascontiguousarray(np.broadcast_to(lo, (128, 16)))
    c["iota_row"] = np.ascontiguousarray(np.broadcast_to(np.arange(T, dtype=np.float32), (128, T)))
    c["iota1"] = np.ascontiguousarray(np.broadcast_to(np.arange(1, KSEL + 1, dtype=np.float32), (128, KSEL)))
    tok = (np.arange(NT)[None, :] * 128 + np.arange(128)[:, None]).astype(np.float32)
    c["tokidx"] = np.ascontiguousarray(tok)
    return c


def build_nc(en_attn=True, en_dense=True, en_mosa=True, taps=()):
    nc = bass.Bass("TRN2", target_bir_lowering=False)

    def din(name, shape, dtype=F32):
        return nc.dram_tensor(name, list(shape), dtype, kind="ExternalInput")

    x_in = din("x", (NB, T, H))
    d_wqT = din("wqT", (128, 8192))
    d_wkT = din("wkT", (128, 8192))
    d_wvT = din("wvT", (128, 8192))
    d_woT = din("woT", (128, 8192))
    d_dwqkv = din("dwqkv", (128, 8 * 1536))
    d_mwqkv = din("mwqkv", (128, 8 * 1536))
    d_dwo = din("dwo", (128, 4096))
    d_mwo = din("mwo", (128, 4096))
    d_router = din("router", (128, 64))
    d_identb = din("identb", (128, 128), BF16)
    d_identf = din("identf", (128, 128), F32)
    d_cmaskb = din("cmaskb", (128, 128), BF16)
    d_umaskb = din("umaskb", (128, 128), BF16)
    d_blockones = din("blockones", (128, 2), BF16)
    d_onesrow = din("onesrow", (1, 128))
    d_invfh = din("invfh", (128, 16))
    d_invfl = din("invfl", (128, 16))
    d_iota_row = din("iota_row", (128, T))
    d_iota1 = din("iota1", (128, KSEL))
    d_tokidx = din("tokidx", (128, NT))

    out_t = nc.dram_tensor("out", [NB, T, H], F32, kind="ExternalOutput")
    xn_scr = nc.dram_tensor("xn_scr", [NB, T, H], BF16)

    tap_outs = {}
    tap_req = list(taps)
    ctx = ExitStack()
    with ctx:
        def S(name, shape, dtype, stk=None):
            return (stk or ctx).enter_context(nc.sbuf_tensor(name, list(shape), dtype))

        def P(name, shape, dtype):
            return ctx.enter_context(nc.psum_tensor(name, list(shape), dtype))

        def tap(nm, tsb):
            if nm in tap_req:
                tap_outs[nm] = (nc.dram_tensor("dbg_" + nm, [tsb.shape[0], tsb.shape[1]],
                                               tsb.dtype, kind="ExternalOutput"), tsb)

        def tapdump(nm, tsb, rkeys):
            if nm not in tap_req:
                return
            dram = nc.dram_tensor("dbg_" + nm, [tsb.shape[0], tsb.shape[1]],
                                  tsb.dtype, kind="ExternalOutput")
            em.dma_sp(lambda e, dram=dram, tsb=tsb: e.dma_start(dram[:, :], tsb[:]),
                      r=tuple(rkeys), w=(), sem="dtap", tag="tapd:" + nm)

        c_id = S("c_id", (128, 128), BF16)
        c_idf = S("c_idf", (128, 128), F32)
        c_cm = S("c_cm", (128, 128), BF16)
        c_um = S("c_um", (128, 128), BF16)
        c_bo = S("c_bo", (128, 2), BF16)
        c_or = S("c_or", (1, 128), F32)
        c_invfh = S("c_invfh", (128, 16), F32)
        c_invfl = S("c_invfl", (128, 16), F32)
        c_iota1 = S("c_iota1", (128, KSEL), F32)
        c_tok = S("c_tok", (128, NT), F32)
        c_router = S("c_router", (128, 64), BF16)
        z_eps = S("z_eps", (128, 1), F32)
        z_one = S("z_one", (128, 1), F32)

        xt = [S(f"xt{i}", (128, H), F32) for i in range(2)]
        xnt = [S(f"xnt{i}", (128, H), BF16) for i in range(2)]
        junk2 = S("junk2", (128, T), BF16)
        ln_s = S("ln_s", (128, 16), F32)

        r_scores = S("r_scores", (8, T), F32)
        b_lo = [S(f"b_lo{i}", (8, 1), F32) for i in range(2)]
        b_hi = [S(f"b_hi{i}", (8, 1), F32) for i in range(2)]
        b_mid = S("b_mid", (8, 1), F32)
        b_cnt = S("b_cnt", (8, 1), F32)
        b_cond = S("b_cond", (8, 1), U8)
        b_c4 = S("b_c4", (8, 4), F32)

        av_attn = [S(f"av_attn{p}", (128, T), BF16) for p in range(8)]
        av_dense = [S(f"av_dense{p}", (128, T), BF16) for p in range(4)]

        g_tm = S("g_tm", (128, 128), F32)
        sc_tm = S("sc_tm", (128, NT * 9), F32)

        nrm_s = S("nrm_s", (128, 8), F32)
        rb_sb = [S(f"rb_sb{i}", (64, 512), F32) for i in range(2)]
        vtr_a = S("vtr_a", (32, 32), F32)
        vtr_b = S("vtr_b", (32, 32), F32)
        bias_hd = [[S(f"bias_hd{s}_{i}", (128, 1), F32) for i in range(2)] for s in range(2)]
        out_sb = [S(f"out_sb{i}", (128, H), F32) for i in range(2)]
        rts = [S(f"rts{i}", (128, 16), F32) for i in range(3)]
        rti = S("rti", (128, 16), I32)
        rp3 = [S(f"rp3{i}", (128, 16), BF16) for i in range(3)]
        pos2b = S("pos2b", (2, 128), F32)
        cos_tm = S("cos_tm", (128, NT * 16), BF16)
        sin_tm = S("sin_tm", (128, NT * 16), BF16)
        mcos_tm = S("mcos_tm", (128, 2 * 16), BF16)
        msin_tm = S("msin_tm", (128, 2 * 16), BF16)
        qtmb = [S(f"qtmb{i}", (128, 128), BF16) for i in range(2)]

        pSx = P("pSx", (128, 1536), F32)
        pS2 = [pSx[:, i * 512:(i + 1) * 512] for i in range(3)]
        pav = P("pav", (128, 1024), F32)
        ptr = P("ptr", (128, 512), BF16)
        pm1 = P("pm1", (128, 512), F32)
        pm2 = P("pm2", (128, 512), F32)

        sems = {e: ctx.enter_context(nc.semaphore("sem_" + e)) for e in EM.ENGS}
        dma_sems = {s: ctx.enter_context(nc.semaphore("sem_" + s)) for s in EM.DMA_SEMS}

        em = EM()

        tap("r_scores", r_scores)
        tap("g_tm", g_tm)
        tap("sc_tm", sc_tm)

        def ldc(dst, src, key, pool=False):
            f = (lambda e, dd=dst, ss=src: e.dma_start(dd, ss))
            (em.dma_pool if pool else em.dma_sp)(f, r=(), w=(key,), tag="ld:" + key)

        ldc(c_id[:], d_identb[:, :], "c_id")
        ldc(c_idf[:], d_identf[:, :], "c_idf")
        ldc(c_cm[:], d_cmaskb[:, :], "c_cm")
        ldc(c_um[:], d_umaskb[:, :], "c_um")
        ldc(c_bo[:], d_blockones[:, :], "c_bo")
        ldc(c_or[:], d_onesrow[:, :], "c_or")
        ldc(c_invfh[:], d_invfh[:, :], "c_invfh")
        ldc(c_invfl[:], d_invfl[:, :], "c_invfl")
        ldc(c_iota1[:], d_iota1[:, :], "c_iota1")
        ldc(c_tok[:], d_tokidx[:, :], "c_tok")
        ldc(c_router[:], d_router[:, :], "c_router", pool=True)
        em.dve(lambda e: e.memset(z_eps[:], 1e-5), w=("z_eps",))
        em.dve(lambda e: e.memset(z_one[:], 1.0), w=("z_one",))
        em.barrier()

        def rope_tables_tm(cos_tm, sin_tm, pos_col, pos_keys, ntile, kp):
            for ti in range(ntile):
                pap = pos_col(ti)
                em.dve(lambda e, pap=pap: e.tensor_scalar_mul(rts[1][:], c_invfl[:], pap),
                       r=("c_invfl",) + pos_keys, w=(kp + "1",))
                em.dve(lambda e, pap=pap: e.scalar_tensor_tensor(
                    rts[0][:], c_invfh[:], pap, rts[1][:], op0=ALU.mult, op1=ALU.add),
                    r=("c_invfh", kp + "1") + pos_keys, w=(kp + "0",))
                em.dve(lambda e: e.tensor_copy(rti[:], rts[0][:]),
                       r=(kp + "0",), w=(kp + "i",))
                em.dve(lambda e: e.tensor_copy(rts[2][:], rti[:]),
                       r=(kp + "i",), w=(kp + "2",))
                em.dve(lambda e: e.tensor_tensor(rts[1][:], rts[0][:], rts[2][:], op=ALU.subtract),
                       r=(kp + "0", kp + "2"), w=(kp + "1",))
                em.dve(lambda e: e.tensor_scalar_mul(rts[2][:], rts[1][:], TPI_LO),
                       r=(kp + "1",), w=(kp + "2",))
                em.dve(lambda e: e.scalar_tensor_tensor(
                    rts[0][:], rts[1][:], TPI_HI, rts[2][:], op0=ALU.mult, op1=ALU.add),
                    r=(kp + "1", kp + "2"), w=(kp + "0",))
                em.act(lambda e, ti=ti: e.activation(sin_tm[:, ti * 16:(ti + 1) * 16], rts[0][:], AF.Sin),
                       r=(kp + "0",), w=(kp + "sin",))
                em.dve(lambda e: e.tensor_scalar_add(rts[1][:], rts[0][:], HALFPI),
                       r=(kp + "0",), w=(kp + "1",))
                em.dve(lambda e: e.tensor_single_scalar(rts[2][:], rts[1][:], PI_F, op=ALU.is_gt),
                       r=(kp + "1",), w=(kp + "2",))
                em.dve(lambda e: e.tensor_scalar_mul(rts[2][:], rts[2][:], TPI),
                       r=(kp + "2",), w=(kp + "2",))
                em.dve(lambda e: e.tensor_tensor(rts[1][:], rts[1][:], rts[2][:], op=ALU.subtract),
                       r=(kp + "1", kp + "2"), w=(kp + "1",))
                em.act(lambda e, ti=ti: e.activation(cos_tm[:, ti * 16:(ti + 1) * 16], rts[1][:], AF.Sin),
                       r=(kp + "1",), w=(kp + "cos",))

        def rope_apply_tm(tile, tkey, col0, cos_sl, sin_sl, ck, sk):
            x1 = tile[:, col0:col0 + 16]
            x2 = tile[:, col0 + 16:col0 + 32]
            em.dve(lambda e: e.tensor_tensor(rp3[0][:], x1, cos_sl, op=ALU.mult),
                   r=(tkey, ck), w=("rp0",))
            em.dve(lambda e: e.tensor_tensor(rp3[1][:], x2, sin_sl, op=ALU.mult),
                   r=(tkey, sk), w=("rp1",))
            em.dve(lambda e: e.tensor_tensor(rp3[2][:], x1, sin_sl, op=ALU.mult),
                   r=(tkey, sk), w=("rp2",))
            em.dve(lambda e: e.tensor_tensor(x1, rp3[0][:], rp3[1][:], op=ALU.subtract),
                   r=("rp0", "rp1"), w=(tkey,))
            em.dve(lambda e: e.tensor_tensor(rp3[1][:], x2, cos_sl, op=ALU.mult),
                   r=(tkey, ck), w=("rp1",))
            em.dve(lambda e: e.tensor_tensor(x2, rp3[2][:], rp3[1][:], op=ALU.add),
                   r=("rp2", "rp1"), w=(tkey,))

        bg_queue = []

        def drain_bg(k):
            for _ in range(min(k, len(bg_queue))):
                bg_queue.pop(0)()

        rope_done = [False]

        for b in range(NB):
            xn_scr_key = f"xnscr{b}"
            scA = ExitStack()
            xn_fm = S(f"xn_fm{b}", (128, NCH * T), BF16, scA)
            r_mask = S(f"r_mask{b}", (8, T), BF16, scA)
            r_grid = S(f"r_grid{b}", (8, T), F32, scA)
            wbuf = [S(f"wbuf{b}_{i}", (128, 3072), BF16, scA) for i in range(2)]
            q_pair = [S(f"q_pair{b}_{i}", (128, T), BF16, scA) for i in range(2)]
            k_pair = [S(f"k_pair{b}_{i}", (128, T), BF16, scA) for i in range(2)]
            v_pair = [S(f"v_pair{b}_{i}", (128, NT * 192), BF16, scA) for i in range(2)]
            qsq = [S(f"qsq{b}_{i}", (128, T), BF16, scA) for i in range(2)]
            pt3 = [S(f"pt{b}_{i}", (128, 1024), BF16, scA) for i in range(4)]

            for sl_ in range(2):
                em.pool(lambda e, vp=v_pair[sl_]: e.memset(
                    vp[:].rearrange("p (k s) -> p k s", s=192)[:, :, 64:128], 1.0),
                    w=(f"v_ones{sl_}",), tag="vones")

            if en_dense and not rope_done[0]:
                rope_tables_tm(cos_tm, sin_tm, lambda ti: c_tok[:, ti:ti + 1],
                               ("c_tok",), NT, "rtk")
                rope_done[0] = True

            fq = []

            def drain_front(k):
                for _ in range(min(k, len(fq))):
                    fq.pop(0)()

            evq = []
            ev_par = [0]

            def drain_ev(k):
                for _ in range(min(k, len(evq))):
                    evq.pop(0)()

            fr_ctr = [0]

            def fpm():
                fr_ctr[0] += 1
                return (pm1, "pm1") if fr_ctr[0] % 2 == 0 else (pm2, "pm2")

            def front_closures(branch, p, sl):
                cl = []
                wb, wbkey = wbuf[sl], f"wbuf{sl}"
                qp, qk = q_pair[sl], f"q_pair{sl}"
                kp_, kk = k_pair[sl], f"k_pair{sl}"
                vp = v_pair[sl]
                qs, qsk = qsq[sl], f"qsq{sl}"

                if branch == "attn":
                    def add_w():
                        for wi, (dsrc, off) in enumerate(((d_wqT, 0), (d_wkT, 1024), (d_wvT, 2048))):
                            ap = dsrc[:, :].rearrange("p (c e) -> p c e", c=8)[:, :, p * 128:(p + 1) * 128]
                            em.dma_pool(lambda e, ap=ap, off=off, wb=wb: e.dma_start(
                                wb[:, off:off + 1024].rearrange("p (c e) -> p c e", c=8), ap),
                                r=(), w=(wbkey,), sem=f"dw{sl}", tag=f"ldw.a{p}.{wi}")
                    cl.append(add_w)

                    def wq_sl(c, wb=wb):
                        return wb[:, c * 128:(c + 1) * 128]

                    def wk_sl(c, wb=wb):
                        return wb[:, 1024 + c * 128:1024 + (c + 1) * 128]

                    def wv_sl(c, wb=wb):
                        return wb[:, 2048 + c * 128:2048 + (c + 1) * 128]
                else:
                    def add_w():
                        ap = d_dwqkv[:, :].rearrange("p (c e) -> p c e", c=8)[:, :, p * 384:(p + 1) * 384]
                        em.dma_pool(lambda e, ap=ap, wb=wb: e.dma_start(
                            wb[:, 0:3072].rearrange("p (c e) -> p c e", c=8), ap),
                            r=(), w=(wbkey,), sem=f"dw{sl}", tag=f"ldw.d{p}")
                    cl.append(add_w)

                    def wq_sl(c, wb=wb):
                        return wb[:, c * 384:c * 384 + 128]

                    def wk_sl(c, wb=wb):
                        return wb[:, c * 384 + 128:c * 384 + 256]

                    def wv_sl(c, wb=wb):
                        return wb[:, c * 384 + 256:c * 384 + 384]

                if branch == "attn":
                    for ns in range(4):
                        for (dst, dkey, wsl) in ((qp, qk, wq_sl), (kp_, kk, wk_sl)):
                            def projqk(dst=dst, dkey=dkey, wsl=wsl, ns=ns):
                                pm, pmk = fpm()
                                xnk = tuple(f"xnfm.{t4}.{g}" for t4 in range(ns * 4, ns * 4 + 4)
                                            for g in range(2))
                                for c in range(NCH):
                                    em.pe(lambda e, c=c, ns=ns, wsl=wsl, xn_fm=xn_fm, pm=pm: e.matmul(
                                        pm[:, 0:512], wsl(c), xn_fm[:, c * T + ns * 512:c * T + (ns + 1) * 512],
                                        start=(c == 0), stop=(c == NCH - 1)),
                                        r=(wbkey,) + xnk, w=(pmk,), tag="prj")
                                em.act(lambda e, ns=ns, dst=dst, pm=pm: e.activation(
                                    dst[:, ns * 512:(ns + 1) * 512], pm[:, 0:512], AF.Copy),
                                    r=(pmk,), w=(dkey,))
                            cl.append(projqk)
                else:
                    for (dst, dkey, wsl) in ((qp, qk, wq_sl), (kp_, kk, wk_sl)):
                        pend_b = [None]
                        for tt in range(NT):
                            def projtm_a(dst=dst, dkey=dkey, wsl=wsl, tt=tt, pend_b=pend_b):
                                pm, pmk = fpm()
                                qb, qbk = qtmb[tt % 2], f"qtmb{tt % 2}"
                                xnk = (f"xnfm.{tt}.0", f"xnfm.{tt}.1")
                                for c in range(NCH):
                                    em.pe(lambda e, c=c, tt=tt, wsl=wsl, xn_fm=xn_fm, pm=pm: e.matmul(
                                        pm[:, 0:128], xn_fm[:, c * T + tt * 128:c * T + (tt + 1) * 128], wsl(c),
                                        start=(c == 0), stop=(c == NCH - 1)),
                                        r=(wbkey,) + xnk, w=(pmk,), tag="prjtm")
                                em.act(lambda e, qb=qb, pm=pm: e.activation(qb[:], pm[:, 0:128], AF.Copy),
                                       r=(pmk,), w=(qbk,))
                                for hh in range(2):
                                    rope_apply_tm(qb, qbk, hh * 64,
                                                  cos_tm[:, tt * 16:(tt + 1) * 16],
                                                  sin_tm[:, tt * 16:(tt + 1) * 16], "rtkcos", "rtksin")

                                def projtm_b(qb=qb, qbk=qbk, tt=tt, dst=dst, dkey=dkey):
                                    em.pe(lambda e, qb=qb, tt=tt: e.transpose(
                                        ptr[:, (tt % 4) * 128:(tt % 4 + 1) * 128], qb[:], c_id[:]),
                                        r=(qbk, "c_id"), w=("ptr",), tag="trQ")
                                    if tt % 4 == 3:
                                        g = tt // 4
                                        em.act(lambda e, g=g, dst=dst: e.activation(
                                            dst[:, g * 512:(g + 1) * 512], ptr[:, 0:512], AF.Copy),
                                            r=("ptr",), w=(dkey,))
                                pend_b[0] = projtm_b

                            def projtm_ab(tt=tt, pend_b=pend_b, projtm_a=projtm_a):
                                bprev = pend_b[0]
                                projtm_a()
                                if bprev is not None:
                                    bprev()
                            cl.append(projtm_ab)

                        def projtm_flush(pend_b=pend_b):
                            if pend_b[0] is not None:
                                pend_b[0]()
                                pend_b[0] = None
                        cl.append(projtm_flush)
                for tt in range(NT):
                    def projv(tt=tt, wsl=wv_sl, vp=vp):
                        pm, pmk = fpm()
                        xnk = (f"xnfm.{tt}.0", f"xnfm.{tt}.1")
                        for c in range(NCH):
                            em.pe(lambda e, c=c, tt=tt, wsl=wsl, xn_fm=xn_fm, pm=pm: e.matmul(
                                pm[:, 0:128], xn_fm[:, c * T + tt * 128:c * T + (tt + 1) * 128], wsl(c),
                                start=(c == 0), stop=(c == NCH - 1)),
                                r=(wbkey,) + xnk, w=(pmk,), tag="prj.v")
                        em.dve(lambda e, tt=tt, vp=vp, pm=pm: e.tensor_copy(
                            vp[:, tt * 192:(tt + 1) * 192].rearrange(
                                "p (g c) -> p g c", c=64)[:, 0:3:2, :],
                            pm[:, 0:128].rearrange("p (g c) -> p g c", c=64)),
                            r=(pmk,), w=(f"v_pair{sl}.{tt}",))
                    cl.append(projv)

                def mk_msq(src_t, skey):
                    def msq(src_t=src_t, skey=skey):
                        em.act(lambda e, src_t=src_t, qs=qs: e.activation(qs[:], src_t[:], AF.Square),
                               r=(skey,), w=(qsk,))
                    return msq

                def mk_mred(col):
                    def mred(col=col):
                        for ns in range(4):
                            pm, pmk = fpm()
                            em.pe(lambda e, ns=ns, qs=qs, pm=pm: e.matmul(
                                pm[0:2, 0:512], c_bo[:], qs[:, ns * 512:(ns + 1) * 512],
                                start=True, stop=True),
                                r=("c_bo", qsk), w=(pmk,))
                            em.dve(lambda e, ns=ns, col=col, pm=pm: e.tensor_reduce(
                                nrm_s[0:2, col + ns:col + ns + 1], pm[0:2, 0:512], axis=AXX, op=ALU.max),
                                r=(pmk,), w=(f"nrm{col + ns}",))
                    return mred

                def mnorm_fin():
                    em.dve(lambda e: e.tensor_reduce(vtr_a[0:2, 2:3], nrm_s[0:2, 0:4], axis=AXX, op=ALU.max),
                           r=("nrm0", "nrm1", "nrm2", "nrm3"), w=("nq",))
                    em.dve(lambda e: e.tensor_reduce(vtr_a[0:2, 3:4], nrm_s[0:2, 4:8], axis=AXX, op=ALU.max),
                           r=("nrm4", "nrm5", "nrm6", "nrm7"), w=("nk",))
                    em.act(lambda e: e.activation(vtr_a[0:2, 4:5], vtr_a[0:2, 2:3], AF.Ln),
                           r=("nq",), w=("lnq",))
                    em.act(lambda e: e.activation(vtr_a[0:2, 5:6], vtr_a[0:2, 3:4], AF.Ln),
                           r=("nk",), w=("lnk",))
                    em.dve(lambda e: e.tensor_tensor(vtr_a[0:2, 6:7], vtr_a[0:2, 4:5], vtr_a[0:2, 5:6], op=ALU.add),
                           r=("lnq", "lnk"), w=("lnsum",))
                    em.act(lambda e: e.activation(vtr_a[0:2, 0:1], vtr_a[0:2, 6:7], AF.Exp, bias=0.0, scale=0.5),
                           r=("lnsum",), w=("vtra",))
                    em.dve(lambda e: e.tensor_scalar_mul(vtr_a[0:2, 0:1], vtr_a[0:2, 0:1], -0.125),
                           r=("vtra",), w=("vtra",))
                    em.dve(lambda e: e.transpose(vtr_b[:], vtr_a[:]), r=("vtra",), w=("vtrb",))
                    em.dve(lambda e: e.tensor_reduce(vtr_b[0:1, 2:3], vtr_b[0:1, 0:2],
                                                     axis=AXX, op=ALU.min),
                           r=("vtrb",), w=("vtrb",))

                def mnorm_bc():
                    em.pe(lambda e: e.matmul(pm1[:, 0:1], c_or[:], vtr_b[0:1, 2:3],
                                             start=True, stop=True),
                          r=("vtrb", "c_or"), w=("pm1",), tag="bcM")
                    em.act(lambda e: e.activation(bias_hd[sl][0][:], pm1[:, 0:1], AF.Copy),
                           r=("pm1",), w=(f"bias{sl}.0",))
                cl.append(mk_msq(qp, qk))
                cl.append(mk_mred(0))
                cl.append(mk_msq(kp_, kk))
                cl.append(mk_mred(4))
                cl.append(mnorm_fin)
                cl.append(mnorm_bc)
                return cl

            seq = []
            if en_attn:
                seq += [("attn", p, av_attn[p], f"ava{p}") for p in range(8)]
            if en_dense:
                seq += [("dense", p, av_dense[p], f"avd{p}") for p in range(4)]
            if seq:
                fq.extend(front_closures(seq[0][0], seq[0][1], 0))

            for tt in range(NT):
                xb, xkey = xt[tt % 2], f"xt{tt % 2}"
                xnb, xnkey = xnt[tt % 2], f"xnt{tt % 2}"
                src = x_in[b, tt * 128:(tt + 1) * 128, :]
                em.dma_sp(lambda e, d=xb, s=src: e.dma_start(d[:], s), r=(), w=(xkey,),
                          sem=f"dx{tt % 2}", tag=f"ldx{b}.{tt}")
                o = (tt % 2) * 8
                pk = str(tt % 2)
                em.dve(lambda e, xb=xb, o=o: e.tensor_reduce(ln_s[:, o:o + 1], xb[:], axis=AXX, op=ALU.add),
                       r=(xkey,), w=("ln0" + pk,))
                em.act(lambda e, xb=xb, o=o, tt=tt: e.activation(
                    junk2[:, (tt % 2) * H:(tt % 2) * H + H], xb[:], AF.Square, accum_out=ln_s[:, o + 1:o + 2]),
                    r=(xkey,), w=("ln1" + pk, "junk2." + pk))
                em.dve(lambda e, o=o: e.tensor_scalar_mul(ln_s[:, o + 2:o + 3], ln_s[:, o:o + 1], 1.0 / H),
                       r=("ln0" + pk,), w=("ln2" + pk,))
                em.dve(lambda e, o=o: e.tensor_tensor(ln_s[:, o + 3:o + 4], ln_s[:, o + 2:o + 3],
                                                      ln_s[:, o + 2:o + 3], op=ALU.mult),
                       r=("ln2" + pk,), w=("ln3" + pk,))
                em.dve(lambda e, o=o: e.scalar_tensor_tensor(ln_s[:, o + 4:o + 5], ln_s[:, o + 1:o + 2],
                                                             1.0 / H, ln_s[:, o + 3:o + 4],
                                                             op0=ALU.mult, op1=ALU.subtract),
                       r=("ln1" + pk, "ln3" + pk), w=("ln4" + pk,))
                em.act(lambda e, o=o: e.activation(ln_s[:, o + 5:o + 6], ln_s[:, o + 4:o + 5],
                                                   AF.Ln, bias=z_eps[:], scale=1.0),
                       r=("ln4" + pk, "z_eps"), w=("ln5" + pk,))
                em.act(lambda e, o=o: e.activation(ln_s[:, o + 6:o + 7], ln_s[:, o + 5:o + 6],
                                                   AF.Exp, bias=0.0, scale=-0.5),
                       r=("ln5" + pk,), w=("ln6" + pk,))
                em.dve(lambda e, xb=xb, xnb=xnb, o=o: e.scalar_tensor_tensor(
                    xnb[:], xb[:], ln_s[:, o + 2:o + 3], ln_s[:, o + 6:o + 7].to_broadcast([128, H]),
                    op0=ALU.subtract, op1=ALU.mult),
                    r=(xkey, "ln2" + pk, "ln6" + pk), w=(xnkey,))
                em.dma_sp(lambda e, xnb=xnb, d=xn_scr[b, tt * 128:(tt + 1) * 128, :]: e.dma_start(d, xnb[:]),
                          r=(xnkey,), w=(xn_scr_key + f".{tt}",), sem="dxn", tag=f"stxn{b}.{tt}")
                for g in range(2):
                    for c4 in range(4):
                        c = g * 4 + c4
                        em.pe(lambda e, xnb=xnb, c=c, c4=c4: e.transpose(
                            ptr[:, c4 * 128:(c4 + 1) * 128], xnb[:, c * 128:(c + 1) * 128], c_id[:]),
                            r=(xnkey, "c_id"), w=("ptr",))
                    em.dve(lambda e, g=g, tt=tt, xn_fm=xn_fm: e.tensor_copy(
                        xn_fm[:].rearrange("p (c t) -> p c t", c=NCH)[:, g * 4:(g + 1) * 4, tt * 128:(tt + 1) * 128],
                        ptr[:].rearrange("p (c t) -> p c t", c=4)),
                        r=("ptr",), w=(f"xnfm.{tt}.{g}",))
                if tt == 0:
                    drain_front(1)
                elif tt % 4 == 3:
                    drain_front(2)
            xn_keys = tuple(f"xnfm.{tt}.{g}" for tt in range(NT) for g in range(2))

            if en_mosa:
                for ns in range(4):
                    for c in range(NCH):
                        em.pe(lambda e, ns=ns, c=c, xn_fm=xn_fm: e.matmul(
                            pm1[0:8, 0:512], c_router[:, c * 8:(c + 1) * 8],
                            xn_fm[:, c * T + ns * 512:c * T + (ns + 1) * 512],
                            start=(c == 0), stop=(c == NCH - 1)),
                            r=("c_router",) + xn_keys, w=("pm1",), tag="router")
                    em.act(lambda e, ns=ns: e.activation(
                        r_scores[:, ns * 512:(ns + 1) * 512], pm1[0:8, 0:512], AF.Copy),
                        r=("pm1",), w=("r_scores",), tag="rsc")

                def bis_init():
                    em.dve(lambda e: e.tensor_reduce(b_hi[0][:], r_scores[:], axis=AXX, op=ALU.max),
                           r=("r_scores",), w=("b_hi0",))
                    em.dve(lambda e: e.tensor_reduce(b_lo[0][:], r_scores[:], axis=AXX, op=ALU.min),
                           r=("b_lo0", "r_scores"), w=("b_lo0",))
                    em.dve(lambda e: e.tensor_scalar_add(b_lo[0][:], b_lo[0][:], -1.0),
                           r=("b_lo0",), w=("b_lo0",))
                    em.dve(lambda e: e.tensor_scalar(b_mid[:], b_lo[0][:], b_hi[0][:], 0.5,
                                                     op0=ALU.add, op1=ALU.mult),
                           r=("b_lo0", "b_hi0"), w=("b_mid",))
                bg_queue.append(bis_init)

                def mk_iter(i):
                    cur, nxt = i % 2, (i + 1) % 2
                    cls = []
                    for cc in range(4):
                        cls.append(lambda cc=cc: em.dve(lambda e, cc=cc: e.tensor_scalar(
                            junk2[0:8, cc * 512:(cc + 1) * 512], r_scores[:, cc * 512:(cc + 1) * 512],
                            b_mid[:], 0.0, op0=ALU.is_gt, op1=ALU.add,
                            accum_out=b_c4[:, cc:cc + 1]),
                            r=("r_scores", "b_mid"), w=("junk2.0", "junk2.1", "b_c4")))

                    def tail():
                        em.dve(lambda e: e.tensor_reduce(b_cnt[:], b_c4[:], axis=AXX, op=ALU.add),
                               r=("b_c4",), w=("b_cnt",))
                        em.dve(lambda e: e.tensor_single_scalar(b_cond[:], b_cnt[:], float(KSEL), op=ALU.is_ge),
                               r=("b_cnt",), w=("b_cond",))
                        em.dve(lambda e: e.select(b_lo[nxt][:], b_cond[:], b_mid[:], b_lo[cur][:]),
                               r=("b_cond", "b_mid", f"b_lo{cur}"), w=(f"b_lo{nxt}",))
                        em.dve(lambda e: e.select(b_hi[nxt][:], b_cond[:], b_hi[cur][:], b_mid[:]),
                               r=("b_cond", "b_mid", f"b_hi{cur}"), w=(f"b_hi{nxt}",))
                        em.dve(lambda e: e.tensor_scalar(b_mid[:], b_lo[nxt][:], b_hi[nxt][:], 0.5,
                                                         op0=ALU.add, op1=ALU.mult),
                               r=(f"b_lo{nxt}", f"b_hi{nxt}"), w=("b_mid",))
                    cls.append(tail)
                    return cls
                for i in range(N_BISECT):
                    bg_queue.extend(mk_iter(i))
                lo_fin = f"b_lo{N_BISECT % 2}"

                def bis_post(r_mask=r_mask, r_grid=r_grid):
                    em.dve(lambda e: e.tensor_single_scalar(r_mask[:], r_scores[:],
                                                            b_lo[N_BISECT % 2][:], op=ALU.is_gt),
                           r=("r_scores", lo_fin), w=("r_mask",))
                    em.dve(lambda e: e.tensor_tensor_scan(r_grid[:], r_mask[:],
                                                          z_one[0:8, :].to_broadcast([8, T]),
                                                          0.0, op0=ALU.add, op1=ALU.mult),
                           r=("r_mask", "z_one"), w=("r_grid",))
                    em.dve(lambda e: e.scalar_tensor_tensor(r_grid[:], r_grid[:], 1.0, r_mask[:],
                                                            op0=ALU.mult, op1=ALU.mult),
                           r=("r_grid", "r_mask"), w=("r_grid",))
                bg_queue.append(bis_post)


            def emit_attention(branch, p, sl, av_tile, avkey):
                qp, qk = q_pair[sl], f"q_pair{sl}"
                kp_, kk = k_pair[sl], f"k_pair{sl}"
                vp = v_pair[sl]
                sct = [0]
                for w in range(4):
                    w0 = 512 * w
                    w1 = w0 + 512
                    kts = [kt for kt in range(NT) if kt * 128 < w1]
                    klast = kts[-1]

                    def emit_S_pair(idx, kt, w0=w0, w1=w1):
                        qstart = max(kt * 128, w0)
                        cw = w1 - qstart
                        slab, slk = pt3[idx % 4], f"pt{idx % 4}"
                        pss = []
                        for hb in (0, 64):
                            bi = sct[0] % 3
                            ps, psk = pS2[bi], f"pS{bi}"
                            sct[0] += 1
                            pss.append((hb, bi, ps, psk))
                            em.pe(lambda e, ps=ps, cw=cw, kt=kt, hb=hb, qa=qstart: e.matmul(
                                ps[:, 0:cw],
                                kp_[hb:hb + 64, kt * 128:(kt + 1) * 128],
                                qp[hb:hb + 64, qa:qa + cw],
                                start=True, stop=True, tile_position=(hb, 0)),
                                r=(kk, qk), w=(psk,), tag="S")
                        for hb, bi, ps, psk in pss:
                            so = (hb // 64) * 512
                            em.act(lambda e, slab=slab, ps=ps, cw=cw, so=so: e.activation(
                                slab[:, so:so + cw], ps[:, 0:cw], AF.Exp,
                                bias=bias_hd[sl][0][:], scale=0.125),
                                r=(psk, f"bias{sl}.0"),
                                w=(slk + ("a" if hb == 0 else "b"),), tag="exp")
                        if kt * 128 >= w0:
                            for hb in (0, 64):
                                so = (hb // 64) * 512
                                sk2 = slk + ("a" if hb == 0 else "b")
                                em.dve(lambda e, slab=slab, so=so: e.tensor_tensor(
                                    slab[:, so:so + 128], slab[:, so:so + 128],
                                    c_um[:], op=ALU.mult),
                                    r=(sk2, "c_um"), w=(sk2,), tag="dmask")

                    def emit_AV_pair(idx, kt, w0=w0, klast=klast):
                        qstart = max(kt * 128, w0)
                        cw = 512 - (qstart - w0)
                        co = qstart - w0
                        slab, slk = pt3[idx % 4], f"pt{idx % 4}"
                        for hb in (0, 64):
                            so = (hb // 64) * 512
                            vsl = vp[:, kt * 192 + hb:kt * 192 + hb + 128]
                            em.pe(lambda e, vsl=vsl, slab=slab, kt=kt, cw=cw, co=co,
                                  so=so, hb=hb: e.matmul(
                                pav[0:128, (hb // 64) * 512 + co:(hb // 64) * 512 + co + cw],
                                vsl, slab[:, so:so + cw],
                                start=(kt == 0), stop=(kt == klast)),
                                r=(slk + ("a" if hb == 0 else "b"),
                                   f"v_pair{sl}.{kt}", f"v_ones{sl}"),
                                w=(f"pavb{hb // 64}",), tag="AV")

                    def push_evac(hb, w0=w0):
                        pk = f"pavb{hb // 64}"
                        co = (hb // 64) * 512
                        pr = ev_par[0] % 2
                        ev_par[0] += 1
                        rbb, rbk = rb_sb[pr], f"rbsb{pr}"
                        lb = 64 - hb
                        evq.append(lambda: em.act(lambda e: e.activation(
                            rbb[:, 0:512], pav[lb:lb + 64, co:co + 512], AF.Ln),
                            r=(pk,), w=(rbk,), tag="lln"))
                        evq.append(lambda: em.act(lambda e: e.activation(
                            rbb[:, 0:512], rbb[:, 0:512], AF.Exp, bias=0.0, scale=-1.0),
                            r=(rbk,), w=(rbk,), tag="lexp"))

                        def ev_rest():
                            em.dve(lambda e: e.tensor_tensor(
                                av_tile[hb:hb + 64, w0:w0 + 512],
                                pav[hb:hb + 64, co:co + 512], rbb[:, 0:512], op=ALU.mult),
                                r=(pk, rbk), w=(avkey,), tag="avn")
                        evq.append(ev_rest)

                    SK = 3
                    for idx, kt in enumerate(kts):
                        emit_S_pair(idx, kt)
                        drain_ev(1)
                        drain_front(1)
                        drain_bg(1)
                        if idx >= SK:
                            emit_AV_pair(idx - SK, kts[idx - SK])
                            drain_ev(1)
                            drain_front(1)
                    for idx in range(max(0, len(kts) - SK), len(kts)):
                        emit_AV_pair(idx, kts[idx])
                        drain_ev(1)
                        drain_front(1)
                        drain_bg(1)
                    for hb in (0, 64):
                        push_evac(hb)
                    while evq:
                        drain_ev(1)
                        drain_front(1)

            for i_, (br, p, avt, avk) in enumerate(seq):
                sl = i_ % 2
                if i_ == 0:
                    drain_front(10 ** 9)
                if i_ + 1 < len(seq):
                    fq.extend(front_closures(seq[i_ + 1][0], seq[i_ + 1][1], (i_ + 1) % 2))
                emit_attention(br, p, sl, avt, avk)
                drain_front(10 ** 9)
            drain_bg(10 ** 9)

            if en_mosa:
                for tt in range(NT):
                    em.pe(lambda e, tt=tt, r_grid=r_grid: e.transpose(
                        pav[:, tt * 8:tt * 8 + 8], r_grid[:, tt * 128:(tt + 1) * 128], c_idf[0:8, 0:8]),
                        r=("r_grid", "c_idf"), w=("pavb0",), tag="trG")
                em.dve(lambda e: e.tensor_copy(g_tm[:], pav[:, 0:128]),
                       r=("pavb0",), w=("g_tm",))
                for tt in range(NT):
                    em.pe(lambda e, tt=tt: e.transpose(
                        pm1[:, tt * 8:tt * 8 + 8], r_scores[:, tt * 128:(tt + 1) * 128], c_idf[0:8, 0:8]),
                        r=("r_scores", "c_idf"), w=("pm1",), tag="trS")
                em.dve(lambda e: e.tensor_copy(
                    sc_tm[:].rearrange("p (k n) -> p k n", n=9)[:, :, 1:9],
                    pm1[:, 0:128].rearrange("p (k n) -> p k n", n=8)),
                    r=("pm1",), w=("sc_tm",))
                em.pool(lambda e: e.tensor_copy(
                    sc_tm[:].rearrange("p (k n) -> p k n", n=9)[:, :, 0:1],
                    c_tok[:].rearrange("p (k one) -> p k one", one=1)),
                    r=("c_tok",), w=("sc_tm",))

            em.barrier()
            scA.close()

            scB = ExitStack()
            c_iotar = S(f"c_iotar{b}", (128, T), F32, scB)
            av_mosa = [S(f"av_mosa{b}_{p}", (128, T), BF16, scB) for p in range(4)]
            wo_all = S(f"wo_all{b}", (128, 8192), BF16, scB)
            dwo_all = S(f"dwo_all{b}", (128, 4096), BF16, scB)
            mwo_all = S(f"mwo_all{b}", (128, 4096), BF16, scB)
            em.dma_sp(lambda e, dd=c_iotar[:], ss=d_iota_row[:, :]: e.dma_start(dd, ss),
                      r=(), w=("c_iotar",), sem="dwo", tag="ld:c_iotar")
            for (dst_, src_, k_) in ((wo_all, d_woT, "wo_all"), (dwo_all, d_dwo, "dwo_all"),
                                     (mwo_all, d_mwo, "mwo_all")):
                em.dma_pool(lambda e, dd=dst_[:], ss=src_[:, :]: e.dma_start(dd, ss),
                            r=(), w=(k_,), sem="dwo", tag="ld:" + k_)
            em.fence_keys(("c_iotar", "wo_all", "dwo_all", "mwo_all"))
            if en_mosa:
                wbufm = [S(f"wbm{b}_{i}", (128, 1536), BF16, scB) for i in range(2)]
                oh_t = [S(f"oh_t{b}_{i}", (128, KSEL), F32, scB) for i in range(2)]
                idxsc8 = S(f"idxsc8{b}", (128, 32), F32, scB)
                cmT = S(f"cmT{b}", (2, 256), F32, scB)
                idx_u = [S(f"idx_u{b}_{i}", (128, 2), U32, scB) for i in range(2)]
                gate_c = S(f"gate_c{b}", (128, 2), F32, scB)
                gate_e = S(f"gate_e{b}", (128, 2), F32, scB)
                xg = [[S(f"xg{b}_{pp}_{i}", (128, H), BF16, scB) for i in range(2)]
                      for pp in range(2)]
                xg_fm = S(f"xg_fm{b}", (128, NCH * KSEL), BF16, scB)
                mq8 = S(f"mq8{b}", (64, 8 * KSEL), BF16, scB)
                mk8 = S(f"mk8{b}", (64, 8 * KSEL), BF16, scB)
                mv8 = S(f"mv8{b}", (128, 8 * 130), BF16, scB)
                msq = S(f"msq{b}", (64, KSEL), BF16, scB)
                mpt = S(f"mpt{b}", (128, 3 * 128), BF16, scB)
                mavtm = S(f"mavtm{b}", (128, 2 * D), BF16, scB)
                bias_m = S(f"bias_m{b}", (128, 1), F32, scB)
                oh_rt = [S(f"oh_rt{b}_{i}", (128, 512), BF16, scB) for i in range(2)]

                em.pool(lambda e, mv8=mv8: e.memset(
                    mv8[:].rearrange("p (h s) -> p h s", s=130)[:, :, 64:130:65], 1.0),
                    w=("mv_ones",), tag="mvones")

                def m_stage1(hh):
                    par = hh % 2
                    wb, wbkey = wbufm[par], f"wbm{par}"
                    ap = d_mwqkv[:, :].rearrange("p (c e) -> p c e", c=8)[:, :, hh * 192:(hh + 1) * 192]
                    em.dma_pool(lambda e, ap=ap, wb=wb: e.dma_start(
                        wb[:, 0:1536].rearrange("p (c e) -> p c e", c=8), ap),
                        r=(), w=(wbkey,), sem=f"dw{par}", tag=f"ldmw{hh}")
                    for tt in range(NT):
                        oh, ohk = oh_t[tt % 2], f"oh{tt % 2}"
                        em.dve(lambda e, oh=oh, tt=tt, hh=hh: e.tensor_single_scalar(
                            oh[:], c_iota1[:], g_tm[:, tt * 8 + hh:tt * 8 + hh + 1], op=ALU.is_equal),
                            r=("c_iota1", "g_tm"), w=(ohk,))
                        em.pe(lambda e, oh=oh, tt=tt, hh=hh: e.matmul(
                            pm1[0:2, 0:256],
                            sc_tm[:, tt * 9:tt * 9 + hh + 2:hh + 1],
                            oh[:, 0:256],
                            start=(tt == 0), stop=(tt == NT - 1)),
                            r=(ohk, "sc_tm"), w=("pm1",), tag="cmm")
                    em.act(lambda e: e.activation(cmT[:, 0:256], pm1[0:2, 0:256], AF.Copy),
                           r=("pm1",), w=("cmT",))
                    for half in range(2):
                        em.pe(lambda e, half=half: e.transpose(
                            pm2[0:128, half * 2:half * 2 + 2],
                            cmT[:, half * 128:(half + 1) * 128], c_idf[0:2, 0:2]),
                            r=("cmT", "c_idf"), w=("pm2",), tag="cmtr")
                    em.act(lambda e, hh=hh, idxsc8=idxsc8: e.activation(
                        idxsc8[:, hh * 4:hh * 4 + 4], pm2[0:128, 0:4], AF.Copy),
                        r=("pm2",), w=(f"idxsc.{hh}",))
                    em.dve(lambda e, par=par, hh=hh, idxsc8=idxsc8: e.tensor_copy(
                        idx_u[par][:], idxsc8[:, hh * 4:hh * 4 + 4:2]),
                        r=(f"idxsc.{hh}",), w=(f"idx_u{par}",))
                    for rs in range(2):
                        em.dma_pool(lambda e, rs=rs, b=b, par=par: e.indirect_dma_start(
                            xg[par][rs][:], None,
                            xn_scr[:, :, :].rearrange("b t h -> (b t) h"),
                            bass.IndirectOffsetOnAxis(ap=idx_u[par][:, rs:rs + 1], axis=0),
                            element_offset=b * T * H),
                            r=(f"idx_u{par}",) + tuple(xn_scr_key + f".{tt}" for tt in range(NT)),
                            w=(f"xg{par}{rs}",), sem=f"dg{rs}", tag=f"gather{hh}.{rs}")

                def m_stage2(hh):
                    par = hh % 2
                    wb, wbkey = wbufm[par], f"wbm{par}"
                    mpb = [0]

                    def mpav():
                        mpb[0] += 1
                        return ((pav[:, 0:64], "pavb0") if mpb[0] % 2
                                else (pav[:, 512:576], "pavb1"))
                    for rs in range(2):
                        for cg in range(2):
                            for c4 in range(4):
                                c = cg * 4 + c4
                                em.pe(lambda e, rs=rs, c=c, c4=c4, par=par: e.transpose(
                                    ptr[:, c4 * 128:(c4 + 1) * 128], xg[par][rs][:, c * 128:(c + 1) * 128], c_id[:]),
                                    r=(f"xg{par}{rs}", "c_id"), w=("ptr",), tag="trXg")
                            em.dve(lambda e, rs=rs, cg=cg, xg_fm=xg_fm: e.tensor_copy(
                                xg_fm[:].rearrange("p (c r) -> p c r", c=NCH)[:, cg * 4:(cg + 1) * 4, rs * 128:(rs + 1) * 128],
                                ptr[:].rearrange("p (c t) -> p c t", c=4)),
                                r=("ptr",), w=("xg_fm",))
                    rope_tables_tm(mcos_tm, msin_tm,
                                   lambda ti, hh=hh, idxsc8=idxsc8: idxsc8[:, hh * 4 + ti * 2:hh * 4 + ti * 2 + 1],
                                   (f"idxsc.{hh}",), 2, "mrtk")
                    for (dst8, dk8, woff, ck, sk) in (
                            (mq8, f"mq.{hh}", 0, "mrtkcos", "mrtksin"),
                            (mk8, f"mk.{hh}", 64, "mrtkcos", "mrtksin")):
                        for rs in range(2):
                            qb, qbk = qtmb[rs % 2], f"qtmb{rs % 2}"
                            mpv, mpk = mpav()
                            for c in range(NCH):
                                em.pe(lambda e, c=c, rs=rs, wb=wb, xg_fm=xg_fm, woff=woff, mpv=mpv: e.matmul(
                                    mpv[:, 0:64], xg_fm[:, c * KSEL + rs * 128:c * KSEL + (rs + 1) * 128],
                                    wb[:, c * 192 + woff:c * 192 + woff + 64],
                                    start=(c == 0), stop=(c == NCH - 1)),
                                    r=(wbkey, "xg_fm"), w=(mpk,), tag="mqk")
                            em.act(lambda e, qb=qb, mpv=mpv: e.activation(qb[:, 0:64], mpv[:, 0:64], AF.Copy),
                                   r=(mpk,), w=(qbk,))
                            rope_apply_tm(qb, qbk, 0,
                                          mcos_tm[:, rs * 16:(rs + 1) * 16],
                                          msin_tm[:, rs * 16:(rs + 1) * 16], ck, sk)
                            em.pe(lambda e, qb=qb, rs=rs: e.transpose(
                                ptr[0:64, rs * 128:(rs + 1) * 128], qb[:, 0:64], c_id[:]),
                                r=(qbk, "c_id"), w=("ptr",), tag="trmQ")
                        em.act(lambda e, dst8=dst8, hh=hh: e.activation(
                            dst8[:, hh * KSEL:(hh + 1) * KSEL], ptr[0:64, 0:KSEL], AF.Copy),
                            r=("ptr",), w=(dk8,))
                    for rs in range(2):
                        mpv, mpk = mpav()
                        for c in range(NCH):
                            em.pe(lambda e, c=c, rs=rs, wb=wb, xg_fm=xg_fm, mpv=mpv: e.matmul(
                                mpv[:, 0:64], xg_fm[:, c * KSEL + rs * 128:c * KSEL + (rs + 1) * 128],
                                wb[:, c * 192 + 128:c * 192 + 192],
                                start=(c == 0), stop=(c == NCH - 1)),
                                r=(wbkey, "xg_fm"), w=(mpk,), tag="mv")
                        em.act(lambda e, rs=rs, hh=hh, mv8=mv8, mpv=mpv: e.activation(
                            mv8[:, hh * 130 + rs * 65:hh * 130 + rs * 65 + 64], mpv[:, 0:64], AF.Copy),
                            r=(mpk,), w=(f"mv.{hh}.{rs}",))

                m_stage1(0)
                for hh in range(8):
                    if hh + 1 < 8:
                        m_stage1(hh + 1)
                    m_stage2(hh)

                for hh in range(8):
                    mqv = mq8[:, hh * KSEL:(hh + 1) * KSEL]
                    mkv = mk8[:, hh * KSEL:(hh + 1) * KSEL]
                    em.act(lambda e, mqv=mqv, msq=msq: e.activation(msq[:], mqv, AF.Square),
                           r=(f"mq.{hh}",), w=("msq",))
                    em.pe(lambda e, msq=msq: e.matmul(pm1[0:1, 0:KSEL], c_bo[0:64, 0:1], msq[:],
                                                      start=True, stop=True),
                          r=("c_bo", "msq"), w=("pm1",), tag="mnq")
                    em.dve(lambda e: e.tensor_reduce(nrm_s[0:1, 0:1], pm1[0:1, 0:KSEL], axis=AXX, op=ALU.max),
                           r=("pm1",), w=("nq",))
                    em.act(lambda e, mkv=mkv, msq=msq: e.activation(msq[:], mkv, AF.Square),
                           r=(f"mk.{hh}",), w=("msq",))
                    em.pe(lambda e, msq=msq: e.matmul(pm1[0:1, 0:KSEL], c_bo[0:64, 0:1], msq[:],
                                                      start=True, stop=True),
                          r=("c_bo", "msq"), w=("pm1",), tag="mnk")
                    em.dve(lambda e: e.tensor_reduce(nrm_s[0:1, 4:5], pm1[0:1, 0:KSEL], axis=AXX, op=ALU.max),
                           r=("pm1",), w=("nk",))
                    em.act(lambda e: e.activation(nrm_s[0:1, 1:2], nrm_s[0:1, 0:1], AF.Ln),
                           r=("nq",), w=("lnq",))
                    em.act(lambda e: e.activation(nrm_s[0:1, 5:6], nrm_s[0:1, 4:5], AF.Ln),
                           r=("nk",), w=("lnk",))
                    em.dve(lambda e: e.tensor_tensor(nrm_s[0:1, 2:3], nrm_s[0:1, 1:2], nrm_s[0:1, 5:6], op=ALU.add),
                           r=("lnq", "lnk"), w=("lnsum",))
                    em.act(lambda e: e.activation(nrm_s[0:1, 3:4], nrm_s[0:1, 2:3], AF.Exp, bias=0.0, scale=0.5),
                           r=("lnsum",), w=("Mt",))
                    em.dve(lambda e: e.tensor_scalar_mul(nrm_s[0:1, 3:4], nrm_s[0:1, 3:4], -0.125),
                           r=("Mt",), w=("Mt",))
                    em.pe(lambda e: e.matmul(pm1[:, 0:1], c_or[:], nrm_s[0:1, 3:4], start=True, stop=True),
                          r=("Mt", "c_or"), w=("pm1",), tag="mbc")
                    em.act(lambda e, bias_m=bias_m: e.activation(bias_m[:], pm1[:, 0:1], AF.Copy),
                           r=("pm1",), w=("bias_m",))

                    tiles = [(0, 0), (0, 1), (1, 1)]
                    for ti, (kt, qt) in enumerate(tiles):
                        em.pe(lambda e, ti=ti, kt=kt, qt=qt, hh=hh, mq8=mq8, mk8=mk8: e.matmul(
                            pS2[0][:, ti * 128:(ti + 1) * 128],
                            mk8[:, hh * KSEL + kt * 128:hh * KSEL + (kt + 1) * 128],
                            mq8[:, hh * KSEL + qt * 128:hh * KSEL + (qt + 1) * 128],
                            start=True, stop=(kt != qt)),
                            r=(f"mk.{hh}", f"mq.{hh}"), w=("pS0",), tag="mS")
                        if kt == qt:
                            em.pe(lambda e, ti=ti: e.matmul(
                                pS2[0][:, ti * 128:(ti + 1) * 128], c_id[:], c_cm[:],
                                start=False, stop=True),
                                r=("c_id", "c_cm"), w=("pS0",), tag="mmask")
                    em.act(lambda e, mpt=mpt, bias_m=bias_m: e.activation(
                        mpt[:, 0:384], pS2[0][:, 0:384], AF.Exp, bias=bias_m[:], scale=0.125),
                        r=("pS0", "bias_m"), w=("mpt",), tag="mexp")
                    for ti, (kt, qt) in enumerate(tiles):
                        em.pe(lambda e, ti=ti, kt=kt, qt=qt, hh=hh, mpt=mpt, mv8=mv8: e.matmul(
                            pav[:, qt * 65:qt * 65 + 65], mpt[:, ti * 128:(ti + 1) * 128],
                            mv8[:, hh * 130 + kt * 65:hh * 130 + kt * 65 + 65],
                            start=(kt == 0), stop=(kt == qt)),
                            r=("mpt", f"mv.{hh}.{kt}", "mv_ones"), w=("pavb0",), tag="mAV")
                    em.act(lambda e, hh=hh, gate_e=gate_e, idxsc8=idxsc8: e.activation(
                        gate_e[:], idxsc8[:, hh * 4 + 1:hh * 4 + 4:2], AF.Exp, bias=0.0, scale=-1.0),
                        r=(f"idxsc.{hh}",), w=("gate_e",))
                    em.dve(lambda e, gate_e=gate_e: e.tensor_scalar_add(gate_e[:], gate_e[:], 1.0),
                           r=("gate_e",), w=("gate_e",))
                    em.dve(lambda e, gate_c=gate_c, gate_e=gate_e: e.reciprocal(gate_c[:], gate_e[:]),
                           r=("gate_e",), w=("gate_c",))
                    em.dve(lambda e: e.reciprocal(
                        nrm_s[:, 0:2], pav[:, 64:130:65]),
                        r=("pavb0",), w=("mrcp",))
                    for rs in range(2):
                        em.dve(lambda e, rs=rs, mavtm=mavtm, gate_c=gate_c: e.tensor_scalar(
                            mavtm[:, rs * 64:(rs + 1) * 64], pav[:, rs * 65:rs * 65 + 64],
                            nrm_s[:, rs:rs + 1], gate_c[:, rs:rs + 1], op0=ALU.mult, op1=ALU.mult),
                            r=("pavb0", "mrcp", "gate_c"), w=("mavtm",))
                    hb2 = (hh % 2) * 64
                    for ns in range(4):
                        for rs in range(2):
                            em.dve(lambda e, rs=rs, ns=ns, hh=hh, oh_rt=oh_rt, idxsc8=idxsc8: e.tensor_single_scalar(
                                oh_rt[rs][:], c_iotar[:, ns * 512:(ns + 1) * 512],
                                idxsc8[:, hh * 4 + rs * 2:hh * 4 + rs * 2 + 1], op=ALU.is_equal),
                                r=("c_iotar", f"idxsc.{hh}"), w=(f"ohrt{rs}",))
                        em.pe(lambda e, ns=ns, mavtm=mavtm, oh_rt=oh_rt: e.matmul(
                            pm1[0:64, 0:512], mavtm[:, 0:64], oh_rt[0][:],
                            start=True, stop=False),
                            r=("mavtm", "ohrt0"), w=("pm1",), tag="msc")
                        em.pe(lambda e, ns=ns, mavtm=mavtm, oh_rt=oh_rt: e.matmul(
                            pm1[0:64, 0:512], mavtm[:, 64:128], oh_rt[1][:],
                            start=False, stop=True),
                            r=("mavtm", "ohrt1"), w=("pm1",), tag="msc2")
                        em.act(lambda e, ns=ns, hh=hh, hb2=hb2: e.activation(
                            av_mosa[hh // 2][hb2:hb2 + 64, ns * 512:(ns + 1) * 512],
                            pm1[0:64, 0:512], AF.Copy),
                            r=("pm1",), w=(f"avm{hh // 2}",))

            srcs = []
            if en_attn:
                srcs += [(av_attn[p], wo_all, p, (f"ava{p}",), "wo_all") for p in range(8)]
            if en_dense:
                srcs += [(av_dense[p], dwo_all, p, (f"avd{p}",), "dwo_all") for p in range(4)]
            if en_mosa:
                srcs += [(av_mosa[p], mwo_all, p, (f"avm{p}",), "mwo_all") for p in range(4)]
            obanks = [(pS2[0], "pS0"), (pS2[1], "pS1"), (pS2[2], "pS2"),
                      (pav[:, 0:512], "pavb0"), (pav[:, 512:1024], "pavb1")]
            obc = [0]

            def ldx(tt):
                xb, xkey = xt[tt % 2], f"xt{tt % 2}"
                src = x_in[b, tt * 128:(tt + 1) * 128, :]
                em.dma_sp(lambda e, d=xb, s=src: e.dma_start(d[:], s), r=(), w=(xkey,),
                          sem=f"dx{tt % 2}", tag=f"rldx{b}.{tt}")
            ldx(0)
            ldx(1)
            for tt in range(NT):
                xb, xkey = xt[tt % 2], f"xt{tt % 2}"
                ob, obkey = out_sb[tt % 2], f"osb{tt % 2}"
                for ns in range(2):
                    pb, pbk = obanks[obc[0] % 5]
                    obc[0] += 1
                    if srcs:
                        for ki, (av, wsb, p, avkeys, wkey) in enumerate(srcs):
                            em.pe(lambda e, pb=pb, av=av, wsb=wsb, p=p, tt=tt, ns=ns,
                                  ki=ki, nsr=len(srcs): e.matmul(
                                pb[:, 0:512], av[:, tt * 128:(tt + 1) * 128],
                                wsb[:, p * 1024 + ns * 512:p * 1024 + (ns + 1) * 512],
                                start=(ki == 0), stop=(ki == nsr - 1)),
                                r=avkeys + (wkey,), w=(pbk,), tag="oprj")
                        em.dve(lambda e, pb=pb, ob=ob, ns=ns, xb=xb: e.tensor_tensor(
                            ob[:, ns * 512:(ns + 1) * 512], pb[:, 0:512],
                            xb[:, ns * 512:(ns + 1) * 512], op=ALU.add),
                            r=(pbk, xkey), w=(obkey,), tag="oadd")
                    else:
                        em.dve(lambda e, ob=ob, ns=ns, xb=xb: e.tensor_copy(
                            ob[:, ns * 512:(ns + 1) * 512], xb[:, ns * 512:(ns + 1) * 512]),
                            r=(xkey,), w=(obkey,), tag="ocp")
                em.dma_sp(lambda e, ob=ob, d=out_t[b, tt * 128:(tt + 1) * 128, :]: e.dma_start(d, ob[:]),
                          r=(obkey,), w=(), sem="dout", tag=f"stout{b}.{tt}")
                if tt + 2 < NT:
                    ldx(tt + 2)
            em.barrier()
            scB.close()

        if tap_outs:
            for nm, (dram, sb) in tap_outs.items():
                em.dma_sp(lambda e, dram=dram, sb=sb: e.dma_start(dram[:, :], sb[:]),
                          r=(), w=(), sem="dtap", tag="tap:" + nm)
        em.barrier()
        em.finalize_into(nc, sems, dma_sems)
    return nc


_CACHE = {}


def kernel(**inputs):
    x = np.asarray(inputs["L0_attn_x"])
    host = prep_host(inputs)
    consts = host_consts()
    if "full" not in _CACHE:
        _CACHE["full"] = build_nc()
    nc = _CACHE["full"]
    in_maps = []
    for core in range(8):
        m = {"x": np.ascontiguousarray(x[core * NB:(core + 1) * NB])}
        m.update(host)
        m.update(consts)
        in_maps.append(m)
    res = run_bass_kernel_spmd(nc, in_maps, core_ids=list(range(8)))
    out = np.concatenate([np.asarray(r["out"]) for r in res.results], axis=0)
    return out.astype(np.float32)


# version 20
# speedup vs baseline: 1.1769x; 1.0779x over previous
import numpy as np
import ml_dtypes
import concourse.bass as bass
import concourse.mybir as mybir
from concourse.bass_utils import run_bass_kernel_spmd
from contextlib import ExitStack

F32 = mybir.dt.float32
BF16 = mybir.dt.bfloat16
U32 = mybir.dt.uint32
U8 = mybir.dt.uint8
I32 = mybir.dt.int32
AF = mybir.ActivationFunctionType
ALU = mybir.AluOpType
AXX = mybir.AxisListType.X

T = 2048
H = 1024
D = 64
NB = 2
NT = T // 128
NCH = H // 128
KSEL = 256
NEG = -1e30
N_BISECT = 60

TPI = 2.0 * np.pi
TPI_HI = float(np.float32(TPI))
TPI_LO = float(np.float32(TPI - np.float64(np.float32(TPI))))
PI_F = float(np.float32(np.pi))
HALFPI = float(np.float32(np.pi / 2))


class EM:
    ENGS = ("PE", "ACT", "DVE", "POOL", "SP")

    DMA_SEMS = ("dx0", "dx1", "dxn", "dout", "dpos", "dcon", "dconp",
                "dw0", "dw1", "dwo", "dg0", "dg1", "dtap")

    def __init__(self):
        self.ops = []
        self.last_w = {}
        self.readers = {}
        self.floor = None
        self.dma_idx = {s: 0 for s in self.DMA_SEMS}

    def op(self, eng, fn, r=(), w=(), dma=None, tag=""):
        o = {"eng": eng, "fn": fn, "dma": dma, "deps": [], "inc": False,
             "tag": tag, "i": len(self.ops)}
        if dma is not None:
            self.dma_idx[dma] += 1
            o["dma_i"] = self.dma_idx[dma]
        deps = {}
        if self.floor is not None:
            deps[self.floor["i"]] = self.floor
        for k in r:
            d = self.last_w.get(k)
            if d is not None:
                deps[d["i"]] = d
        for k in w:
            d = self.last_w.get(k)
            if d is not None:
                deps[d["i"]] = d
            for rd in self.readers.get(k, ()):
                deps[rd["i"]] = rd
        for d in deps.values():
            if d is o:
                continue
            if d["dma"] is None and d["eng"] == "PE" and eng == "PE" and o["dma"] is None:
                continue
            d["inc"] = True
            o["deps"].append(d)
        for k in w:
            self.last_w[k] = o
            self.readers[k] = []
        wset = set(w)
        for k in r:
            if k not in wset:
                self.readers.setdefault(k, []).append(o)
        self.ops.append(o)
        return o

    def pe(self, fn, r=(), w=(), tag=""):
        return self.op("PE", fn, r, w, tag=tag)

    def act(self, fn, r=(), w=(), tag=""):
        return self.op("ACT", fn, r, w, tag=tag)

    def dve(self, fn, r=(), w=(), tag=""):
        return self.op("DVE", fn, r, w, tag=tag)

    def pool(self, fn, r=(), w=(), tag=""):
        return self.op("POOL", fn, r, w, tag=tag)

    def dma_sp(self, fn, r=(), w=(), sem="dcon", tag=""):
        return self.op("SP", fn, r, w, dma=sem, tag=tag)

    def dma_pool(self, fn, r=(), w=(), sem="dconp", tag=""):
        return self.op("POOL", fn, r, w, dma=sem, tag=tag)

    def fence_keys(self, keys, eng="POOL"):
        return self.op(eng, lambda e: e.nop(), r=tuple(keys), w=tuple(keys), tag="kfence")

    def barrier(self):
        o = self.op("SP", lambda e: e.nop(), (), (), tag="fence")
        deps = {}
        for d in self.last_w.values():
            deps[d["i"]] = d
        for rl in self.readers.values():
            for d in rl:
                deps[d["i"]] = d
        o["deps"] = []
        for d in deps.values():
            if d is o:
                continue
            d["inc"] = True
            o["deps"].append(d)
        o["inc"] = True
        self.last_w = {k: o for k in self.last_w}
        self.readers = {}
        self.floor = o
        return o

    def finalize_into(self, nc, sems, dma_sems):
        cnt = {e: 0 for e in self.ENGS}
        for o in self.ops:
            if o["dma"] is None and o["inc"]:
                cnt[o["eng"]] += 1
                o["inc_no"] = cnt[o["eng"]]
        for o in self.ops:
            waits = {}
            for d in o["deps"]:
                if d["dma"] is not None:
                    s, v = d["dma"], 16 * d["dma_i"]
                else:
                    s, v = d["eng"], d["inc_no"]
                if waits.get(s, 0) < v:
                    waits[s] = v
            o["waits"] = waits
        allsems = dict(sems)
        allsems.update(dma_sems)

        def emit_for(ename, eng):
            wm = {}
            for o in self.ops:
                if o["eng"] != ename:
                    continue
                for s, v in sorted(o["waits"].items()):
                    if wm.get(s, 0) >= v:
                        continue
                    eng.wait_ge(allsems[s], v)
                    wm[s] = v
                inst = o["fn"](eng)
                if o["dma"] is not None:
                    inst.then_inc(allsems[o["dma"]], 16)
                elif o["inc"]:
                    inst.then_inc(allsems[ename], 1)

        with nc.Block() as block:
            @block.tensor
            def _(e):
                emit_for("PE", e)

            @block.scalar
            def _(e):
                emit_for("ACT", e)

            @block.vector
            def _(e):
                emit_for("DVE", e)

            @block.gpsimd
            def _(e):
                emit_for("POOL", e)

            @block.sync
            def _(e):
                emit_for("SP", e)


def _chunk_major(a, nchunk):
    rows, C = a.shape
    assert rows == nchunk * 128
    return np.ascontiguousarray(
        a.reshape(nchunk, 128, C).transpose(1, 0, 2).reshape(128, nchunk * C))


def prep_host(inp):
    h = {}
    h["wqT"] = _chunk_major(np.ascontiguousarray(np.asarray(inp["L0_attn_wq"]).T), 8)
    h["wkT"] = _chunk_major(np.ascontiguousarray(np.asarray(inp["L0_attn_wk"]).T), 8)
    h["wvT"] = _chunk_major(np.ascontiguousarray(np.asarray(inp["L0_attn_wv"]).T), 8)
    h["woT"] = _chunk_major(np.ascontiguousarray(np.asarray(inp["L0_attn_wo"]).T), 8)
    dw = np.asarray(inp["L0_ffn_dense_wqkv"])
    blocks = []
    for p in range(4):
        blocks.append(np.concatenate(
            [dw[:, 2 * p, 0:64], dw[:, 2 * p + 1, 0:64],
             dw[:, 2 * p, 64:128], dw[:, 2 * p + 1, 64:128],
             dw[:, 2 * p, 128:192], dw[:, 2 * p + 1, 128:192]], axis=1))
    h["dwqkv"] = _chunk_major(np.concatenate(blocks, axis=1), 8)
    mw = np.asarray(inp["L0_ffn_mosa_wqkv"])
    blocks = []
    for hh in range(8):
        blocks.append(np.concatenate(
            [mw[:, hh, 0:64], mw[:, hh, 64:128], mw[:, hh, 128:192]], axis=1))
    h["mwqkv"] = _chunk_major(np.concatenate(blocks, axis=1), 8)
    h["dwo"] = _chunk_major(np.asarray(inp["L0_ffn_dense_wo"]), 4)
    h["mwo"] = _chunk_major(np.asarray(inp["L0_ffn_mosa_wo"]), 4)
    h["router"] = _chunk_major(np.asarray(inp["L0_ffn_mosa_router_w"]), 8)
    return h


def host_consts():
    c = {}
    ident = np.eye(128, dtype=np.float32)
    c["identb"] = ident.astype(ml_dtypes.bfloat16)
    c["identf"] = ident
    c["cmaskb"] = (np.tril(np.ones((128, 128), dtype=np.float32), -1) * NEG).astype(ml_dtypes.bfloat16)
    c["umaskb"] = np.triu(np.ones((128, 128), dtype=np.float32)).astype(ml_dtypes.bfloat16)
    bo = np.zeros((128, 2), dtype=np.float32)
    bo[0:64, 0] = 1.0
    bo[64:128, 1] = 1.0
    c["blockones"] = bo.astype(ml_dtypes.bfloat16)
    c["onesrow"] = np.ones((1, 128), dtype=np.float32)
    j = np.arange(16, dtype=np.float64)
    v = (10000.0 ** (-j / 16.0)) / (2.0 * np.pi)
    hi = v.astype(np.float32)
    lo = (v - hi.astype(np.float64)).astype(np.float32)
    hi2 = np.concatenate([hi, hi])
    lo2 = np.concatenate([lo, lo])
    c["invfh"] = np.ascontiguousarray(np.broadcast_to(hi2, (128, 32)))
    c["invfl"] = np.ascontiguousarray(np.broadcast_to(lo2, (128, 32)))
    c["iota_row"] = np.ascontiguousarray(np.broadcast_to(np.arange(T, dtype=np.float32), (128, T)))
    c["iota1"] = np.ascontiguousarray(np.broadcast_to(np.arange(1, KSEL + 1, dtype=np.float32), (128, KSEL)))
    tok = (np.arange(NT)[None, :] * 128 + np.arange(128)[:, None]).astype(np.float32)
    c["tokidx"] = np.ascontiguousarray(tok)
    return c


def build_nc(en_attn=True, en_dense=True, en_mosa=True, taps=()):
    nc = bass.Bass("TRN2", target_bir_lowering=False)

    def din(name, shape, dtype=F32):
        return nc.dram_tensor(name, list(shape), dtype, kind="ExternalInput")

    x_in = din("x", (NB, T, H))
    d_wqT = din("wqT", (128, 8192))
    d_wkT = din("wkT", (128, 8192))
    d_wvT = din("wvT", (128, 8192))
    d_woT = din("woT", (128, 8192))
    d_dwqkv = din("dwqkv", (128, 8 * 1536))
    d_mwqkv = din("mwqkv", (128, 8 * 1536))
    d_dwo = din("dwo", (128, 4096))
    d_mwo = din("mwo", (128, 4096))
    d_router = din("router", (128, 64))
    d_identb = din("identb", (128, 128), BF16)
    d_identf = din("identf", (128, 128), F32)
    d_cmaskb = din("cmaskb", (128, 128), BF16)
    d_umaskb = din("umaskb", (128, 128), BF16)
    d_blockones = din("blockones", (128, 2), BF16)
    d_onesrow = din("onesrow", (1, 128))
    d_invfh = din("invfh", (128, 32))
    d_invfl = din("invfl", (128, 32))
    d_iota_row = din("iota_row", (128, T))
    d_iota1 = din("iota1", (128, KSEL))
    d_tokidx = din("tokidx", (128, NT))

    out_t = nc.dram_tensor("out", [NB, T, H], F32, kind="ExternalOutput")
    xn_scr = nc.dram_tensor("xn_scr", [NB, T, H], BF16)

    tap_outs = {}
    tap_req = list(taps)
    ctx = ExitStack()
    with ctx:
        def S(name, shape, dtype, stk=None):
            return (stk or ctx).enter_context(nc.sbuf_tensor(name, list(shape), dtype))

        def P(name, shape, dtype):
            return ctx.enter_context(nc.psum_tensor(name, list(shape), dtype))

        def tap(nm, tsb):
            if nm in tap_req:
                tap_outs[nm] = (nc.dram_tensor("dbg_" + nm, [tsb.shape[0], tsb.shape[1]],
                                               tsb.dtype, kind="ExternalOutput"), tsb)

        def tapdump(nm, tsb, rkeys):
            if nm not in tap_req:
                return
            dram = nc.dram_tensor("dbg_" + nm, [tsb.shape[0], tsb.shape[1]],
                                  tsb.dtype, kind="ExternalOutput")
            em.dma_sp(lambda e, dram=dram, tsb=tsb: e.dma_start(dram[:, :], tsb[:]),
                      r=tuple(rkeys), w=(), sem="dtap", tag="tapd:" + nm)

        c_id = S("c_id", (128, 128), BF16)
        c_idf = S("c_idf", (128, 128), F32)
        c_cm = S("c_cm", (128, 128), BF16)
        c_um = S("c_um", (128, 128), BF16)
        c_bo = S("c_bo", (128, 2), BF16)
        c_or = S("c_or", (1, 128), F32)
        c_invfh = S("c_invfh", (128, 32), F32)
        c_invfl = S("c_invfl", (128, 32), F32)
        c_iota1 = S("c_iota1", (128, KSEL), F32)
        c_tok = S("c_tok", (128, NT), F32)
        c_router = S("c_router", (128, 64), BF16)
        z_eps = S("z_eps", (128, 1), F32)
        z_one = S("z_one", (128, 1), F32)

        xt = [S(f"xt{i}", (128, H), F32) for i in range(2)]
        xnt = [S(f"xnt{i}", (128, H), BF16) for i in range(2)]
        junk2 = S("junk2", (128, T), BF16)
        ln_s = S("ln_s", (128, 16), F32)

        r_scores = S("r_scores", (8, T), F32)
        b_lo = [S(f"b_lo{i}", (8, 1), F32) for i in range(2)]
        b_hi = [S(f"b_hi{i}", (8, 1), F32) for i in range(2)]
        b_mid = S("b_mid", (8, 1), F32)
        b_cnt = S("b_cnt", (8, 1), F32)
        b_cond = S("b_cond", (8, 1), U8)
        b_c4 = S("b_c4", (8, 4), F32)

        av_attn = [S(f"av_attn{p}", (128, T), BF16) for p in range(8)]
        av_dense = [S(f"av_dense{p}", (128, T), BF16) for p in range(4)]

        g_tm = S("g_tm", (128, 128), F32)
        sc_tm = S("sc_tm", (128, NT * 9), F32)

        nrm_s = S("nrm_s", (128, 8), F32)
        rb_sb = [S(f"rb_sb{i}", (64, 512), F32) for i in range(2)]
        vtr_a = S("vtr_a", (32, 32), F32)
        vtr_b = S("vtr_b", (32, 32), F32)
        bias_hd = [[S(f"bias_hd{s}_{i}", (128, 1), F32) for i in range(2)] for s in range(2)]
        out_sb = [S(f"out_sb{i}", (128, H), F32) for i in range(2)]
        rts = [S(f"rts{i}", (128, 32), F32) for i in range(3)]
        rti = S("rti", (128, 32), I32)
        rp3 = [S(f"rp3{i}", (128, 32), BF16) for i in range(3)]
        pos2b = S("pos2b", (2, 128), F32)
        cos_tm = S("cos_tm", (128, NT * 32), BF16)
        sin_tm = S("sin_tm", (128, NT * 32), BF16)
        mcos_tm = S("mcos_tm", (128, 2 * 32), BF16)
        msin_tm = S("msin_tm", (128, 2 * 32), BF16)
        qtmb = [S(f"qtmb{i}", (128, 128), BF16) for i in range(4)]

        pSx = P("pSx", (128, 1536), F32)
        pS2 = [pSx[:, i * 512:(i + 1) * 512] for i in range(3)]
        pav = P("pav", (128, 1024), F32)
        ptr = P("ptr", (128, 512), BF16)
        pm1 = P("pm1", (128, 512), F32)
        pm2 = P("pm2", (128, 512), F32)

        sems = {e: ctx.enter_context(nc.semaphore("sem_" + e)) for e in EM.ENGS}
        dma_sems = {s: ctx.enter_context(nc.semaphore("sem_" + s)) for s in EM.DMA_SEMS}

        em = EM()

        tap("r_scores", r_scores)
        tap("g_tm", g_tm)
        tap("sc_tm", sc_tm)

        def ldc(dst, src, key, pool=False):
            f = (lambda e, dd=dst, ss=src: e.dma_start(dd, ss))
            (em.dma_pool if pool else em.dma_sp)(f, r=(), w=(key,), tag="ld:" + key)

        ldc(c_id[:], d_identb[:, :], "c_id")
        ldc(c_idf[:], d_identf[:, :], "c_idf")
        ldc(c_cm[:], d_cmaskb[:, :], "c_cm")
        ldc(c_um[:], d_umaskb[:, :], "c_um")
        ldc(c_bo[:], d_blockones[:, :], "c_bo")
        ldc(c_or[:], d_onesrow[:, :], "c_or")
        ldc(c_invfh[:], d_invfh[:, :], "c_invfh")
        ldc(c_invfl[:], d_invfl[:, :], "c_invfl")
        ldc(c_iota1[:], d_iota1[:, :], "c_iota1")
        ldc(c_tok[:], d_tokidx[:, :], "c_tok")
        ldc(c_router[:], d_router[:, :], "c_router", pool=True)
        em.dve(lambda e: e.memset(z_eps[:], 1e-5), w=("z_eps",))
        em.dve(lambda e: e.memset(z_one[:], 1.0), w=("z_one",))
        em.barrier()

        def rope_tables_tm(cos_tm, sin_tm, pos_col, pos_keys, ntile, kp):
            for ti in range(ntile):
                pap = pos_col(ti)
                em.dve(lambda e, pap=pap: e.tensor_scalar_mul(rts[1][:], c_invfl[:], pap),
                       r=("c_invfl",) + pos_keys, w=(kp + "1",))
                em.dve(lambda e, pap=pap: e.scalar_tensor_tensor(
                    rts[0][:], c_invfh[:], pap, rts[1][:], op0=ALU.mult, op1=ALU.add),
                    r=("c_invfh", kp + "1") + pos_keys, w=(kp + "0",))
                em.dve(lambda e: e.tensor_copy(rti[:], rts[0][:]),
                       r=(kp + "0",), w=(kp + "i",))
                em.dve(lambda e: e.tensor_copy(rts[2][:], rti[:]),
                       r=(kp + "i",), w=(kp + "2",))
                em.dve(lambda e: e.tensor_tensor(rts[1][:], rts[0][:], rts[2][:], op=ALU.subtract),
                       r=(kp + "0", kp + "2"), w=(kp + "1",))
                em.dve(lambda e: e.tensor_scalar_mul(rts[2][:], rts[1][:], TPI_LO),
                       r=(kp + "1",), w=(kp + "2",))
                em.dve(lambda e: e.scalar_tensor_tensor(
                    rts[0][:], rts[1][:], TPI_HI, rts[2][:], op0=ALU.mult, op1=ALU.add),
                    r=(kp + "1", kp + "2"), w=(kp + "0",))
                em.act(lambda e, ti=ti: e.activation(sin_tm[:, ti * 32:(ti + 1) * 32], rts[0][:], AF.Sin),
                       r=(kp + "0",), w=(kp + "sin",))
                em.dve(lambda e: e.tensor_scalar_add(rts[1][:], rts[0][:], HALFPI),
                       r=(kp + "0",), w=(kp + "1",))
                em.dve(lambda e: e.tensor_single_scalar(rts[2][:], rts[1][:], PI_F, op=ALU.is_gt),
                       r=(kp + "1",), w=(kp + "2",))
                em.dve(lambda e: e.tensor_scalar_mul(rts[2][:], rts[2][:], TPI),
                       r=(kp + "2",), w=(kp + "2",))
                em.dve(lambda e: e.tensor_tensor(rts[1][:], rts[1][:], rts[2][:], op=ALU.subtract),
                       r=(kp + "1", kp + "2"), w=(kp + "1",))
                em.act(lambda e, ti=ti: e.activation(cos_tm[:, ti * 32:(ti + 1) * 32], rts[1][:], AF.Sin),
                       r=(kp + "1",), w=(kp + "cos",))

        def rope_apply_tm(tile, tkey, cos_sl, sin_sl, ck, sk, nh):
            tv = tile[:].rearrange("p (g c) -> p g c", c=16)
            x1 = tv[:, 0:4 * nh:4, :]
            x2 = tv[:, 1:4 * nh:4, :]
            cs = cos_sl.rearrange("p (g c) -> p g c", c=16)[:, 0:nh, :]
            sn = sin_sl.rearrange("p (g c) -> p g c", c=16)[:, 0:nh, :]
            rp = [rp3[i][:].rearrange("p (g c) -> p g c", c=16)[:, 0:nh, :]
                  for i in range(3)]
            em.dve(lambda e: e.tensor_tensor(rp[0], x1, cs, op=ALU.mult),
                   r=(tkey, ck), w=("rp0",))
            em.dve(lambda e: e.tensor_tensor(rp[1], x2, sn, op=ALU.mult),
                   r=(tkey, sk), w=("rp1",))
            em.dve(lambda e: e.tensor_tensor(rp[2], x1, sn, op=ALU.mult),
                   r=(tkey, sk), w=("rp2",))
            em.dve(lambda e: e.tensor_tensor(x1, rp[0], rp[1], op=ALU.subtract),
                   r=("rp0", "rp1"), w=(tkey,))
            em.dve(lambda e: e.tensor_tensor(rp[1], x2, cs, op=ALU.mult),
                   r=(tkey, ck), w=("rp1",))
            em.dve(lambda e: e.tensor_tensor(x2, rp[2], rp[1], op=ALU.add),
                   r=("rp2", "rp1"), w=(tkey,))

        bg_queue = []

        def drain_bg(k):
            for _ in range(min(k, len(bg_queue))):
                bg_queue.pop(0)()

        rope_done = [False]

        for b in range(NB):
            xn_scr_key = f"xnscr{b}"
            scA = ExitStack()
            xn_fm = S(f"xn_fm{b}", (128, NCH * T), BF16, scA)
            r_mask = S(f"r_mask{b}", (8, T), BF16, scA)
            r_grid = S(f"r_grid{b}", (8, T), F32, scA)
            wbuf = [S(f"wbuf{b}_{i}", (128, 3072), BF16, scA) for i in range(2)]
            q_pair = [S(f"q_pair{b}_{i}", (128, T), BF16, scA) for i in range(2)]
            k_pair = [S(f"k_pair{b}_{i}", (128, T), BF16, scA) for i in range(2)]
            v_pair = [S(f"v_pair{b}_{i}", (128, NT * 192), BF16, scA) for i in range(2)]
            qsq = [S(f"qsq{b}_{i}", (128, T), BF16, scA) for i in range(2)]
            pt3 = [S(f"pt{b}_{i}", (128, 1024), BF16, scA) for i in range(4)]

            for sl_ in range(2):
                em.pool(lambda e, vp=v_pair[sl_]: e.memset(
                    vp[:].rearrange("p (k s) -> p k s", s=192)[:, :, 64:128], 1.0),
                    w=(f"v_ones{sl_}",), tag="vones")

            if en_dense and not rope_done[0]:
                rope_tables_tm(cos_tm, sin_tm, lambda ti: c_tok[:, ti:ti + 1],
                               ("c_tok",), NT, "rtk")
                rope_done[0] = True

            fq = []

            def drain_front(k):
                for _ in range(min(k, len(fq))):
                    fq.pop(0)()

            evq = []
            ev_par = [0]

            def drain_ev(k):
                for _ in range(min(k, len(evq))):
                    evq.pop(0)()

            fr_ctr = [0]

            def fpm():
                fr_ctr[0] += 1
                return (pm1, "pm1") if fr_ctr[0] % 2 == 0 else (pm2, "pm2")

            def front_closures(branch, p, sl):
                cl = []
                wb, wbkey = wbuf[sl], f"wbuf{sl}"
                qp, qk = q_pair[sl], f"q_pair{sl}"
                kp_, kk = k_pair[sl], f"k_pair{sl}"
                vp = v_pair[sl]
                qs, qsk = qsq[sl], f"qsq{sl}"

                if branch == "attn":
                    def add_w():
                        for wi, (dsrc, off) in enumerate(((d_wqT, 0), (d_wkT, 1024), (d_wvT, 2048))):
                            ap = dsrc[:, :].rearrange("p (c e) -> p c e", c=8)[:, :, p * 128:(p + 1) * 128]
                            em.dma_pool(lambda e, ap=ap, off=off, wb=wb: e.dma_start(
                                wb[:, off:off + 1024].rearrange("p (c e) -> p c e", c=8), ap),
                                r=(), w=(wbkey,), sem=f"dw{sl}", tag=f"ldw.a{p}.{wi}")
                    cl.append(add_w)

                    def wq_sl(c, wb=wb):
                        return wb[:, c * 128:(c + 1) * 128]

                    def wk_sl(c, wb=wb):
                        return wb[:, 1024 + c * 128:1024 + (c + 1) * 128]

                    def wv_sl(c, wb=wb):
                        return wb[:, 2048 + c * 128:2048 + (c + 1) * 128]
                else:
                    def add_w():
                        ap = d_dwqkv[:, :].rearrange("p (c e) -> p c e", c=8)[:, :, p * 384:(p + 1) * 384]
                        em.dma_pool(lambda e, ap=ap, wb=wb: e.dma_start(
                            wb[:, 0:3072].rearrange("p (c e) -> p c e", c=8), ap),
                            r=(), w=(wbkey,), sem=f"dw{sl}", tag=f"ldw.d{p}")
                    cl.append(add_w)

                    def wq_sl(c, wb=wb):
                        return wb[:, c * 384:c * 384 + 128]

                    def wk_sl(c, wb=wb):
                        return wb[:, c * 384 + 128:c * 384 + 256]

                    def wv_sl(c, wb=wb):
                        return wb[:, c * 384 + 256:c * 384 + 384]

                if branch == "attn":
                    for ns in range(4):
                        for (dst, dkey, wsl) in ((qp, qk, wq_sl), (kp_, kk, wk_sl)):
                            def projqk(dst=dst, dkey=dkey, wsl=wsl, ns=ns):
                                pm, pmk = fpm()
                                xnk = tuple(f"xnfm.{t4}.{g}" for t4 in range(ns * 4, ns * 4 + 4)
                                            for g in range(2))
                                for c in range(NCH):
                                    em.pe(lambda e, c=c, ns=ns, wsl=wsl, xn_fm=xn_fm, pm=pm: e.matmul(
                                        pm[:, 0:512], wsl(c), xn_fm[:, c * T + ns * 512:c * T + (ns + 1) * 512],
                                        start=(c == 0), stop=(c == NCH - 1)),
                                        r=(wbkey,) + xnk, w=(pmk,), tag="prj")
                                em.act(lambda e, ns=ns, dst=dst, pm=pm: e.activation(
                                    dst[:, ns * 512:(ns + 1) * 512], pm[:, 0:512], AF.Copy),
                                    r=(pmk,), w=(dkey,))
                            cl.append(projqk)
                else:
                    pend_b = [None]
                    for tt in range(NT):
                        def projtm_a(tt=tt, pend_b=pend_b):
                            pm, pmk = fpm()
                            xnk = (f"xnfm.{tt}.0", f"xnfm.{tt}.1")
                            for c in range(NCH):
                                em.pe(lambda e, c=c, tt=tt, wb=wb, xn_fm=xn_fm, pm=pm: e.matmul(
                                    pm[:, 0:384], xn_fm[:, c * T + tt * 128:c * T + (tt + 1) * 128],
                                    wb[:, c * 384:(c + 1) * 384],
                                    start=(c == 0), stop=(c == NCH - 1)),
                                    r=(wbkey,) + xnk, w=(pmk,), tag="prjtm")
                            bufs = []
                            for qi, (dst, dkey) in enumerate(((qp, qk), (kp_, kk))):
                                qb, qbk = qtmb[qi * 2 + tt % 2], f"qtmb{qi * 2 + tt % 2}"
                                em.act(lambda e, qb=qb, pm=pm, qi=qi: e.activation(
                                    qb[:], pm[:, qi * 128:(qi + 1) * 128], AF.Copy),
                                    r=(pmk,), w=(qbk,))
                                rope_apply_tm(qb, qbk,
                                              cos_tm[:, tt * 32:(tt + 1) * 32],
                                              sin_tm[:, tt * 32:(tt + 1) * 32],
                                              "rtkcos", "rtksin", 2)
                                bufs.append((qi, qb, qbk, dst, dkey))
                            em.dve(lambda e, tt=tt, vp=vp, pm=pm: e.tensor_copy(
                                vp[:, tt * 192:(tt + 1) * 192].rearrange(
                                    "p (g c) -> p g c", c=64)[:, 0:3:2, :],
                                pm[:, 256:384].rearrange("p (g c) -> p g c", c=64)),
                                r=(pmk,), w=(f"v_pair{sl}.{tt}",))

                            def projtm_b(bufs=bufs, tt=tt):
                                for qi, qb, qbk, dst, dkey in bufs:
                                    em.pe(lambda e, qb=qb, qi=qi, tt=tt: e.transpose(
                                        ptr[:, qi * 256 + (tt % 2) * 128:qi * 256 + (tt % 2) * 128 + 128],
                                        qb[:], c_id[:]),
                                        r=(qbk, "c_id"), w=("ptr",), tag="trQ")
                                if tt % 2 == 1:
                                    for qi, qb, qbk, dst, dkey in bufs:
                                        em.act(lambda e, qi=qi, tt=tt, dst=dst: e.activation(
                                            dst[:, (tt - 1) * 128:(tt + 1) * 128],
                                            ptr[:, qi * 256:qi * 256 + 256], AF.Copy),
                                            r=("ptr",), w=(dkey,))
                            pend_b[0] = projtm_b

                        def projtm_ab(tt=tt, pend_b=pend_b, projtm_a=projtm_a):
                            bprev = pend_b[0]
                            projtm_a()
                            if bprev is not None:
                                bprev()
                        cl.append(projtm_ab)

                    def projtm_flush(pend_b=pend_b):
                        if pend_b[0] is not None:
                            pend_b[0]()
                            pend_b[0] = None
                    cl.append(projtm_flush)
                if branch == "attn":
                    for tt in range(NT):
                        def projv(tt=tt, wsl=wv_sl, vp=vp):
                            pm, pmk = fpm()
                            xnk = (f"xnfm.{tt}.0", f"xnfm.{tt}.1")
                            for c in range(NCH):
                                em.pe(lambda e, c=c, tt=tt, wsl=wsl, xn_fm=xn_fm, pm=pm: e.matmul(
                                    pm[:, 0:128], xn_fm[:, c * T + tt * 128:c * T + (tt + 1) * 128], wsl(c),
                                    start=(c == 0), stop=(c == NCH - 1)),
                                    r=(wbkey,) + xnk, w=(pmk,), tag="prj.v")
                            em.dve(lambda e, tt=tt, vp=vp, pm=pm: e.tensor_copy(
                                vp[:, tt * 192:(tt + 1) * 192].rearrange(
                                    "p (g c) -> p g c", c=64)[:, 0:3:2, :],
                                pm[:, 0:128].rearrange("p (g c) -> p g c", c=64)),
                                r=(pmk,), w=(f"v_pair{sl}.{tt}",))
                        cl.append(projv)

                def mk_msq(src_t, skey):
                    def msq(src_t=src_t, skey=skey):
                        em.act(lambda e, src_t=src_t, qs=qs: e.activation(qs[:], src_t[:], AF.Square),
                               r=(skey,), w=(qsk,))
                    return msq

                def mk_mred(col):
                    def mred(col=col):
                        for ns in range(4):
                            pm, pmk = fpm()
                            em.pe(lambda e, ns=ns, qs=qs, pm=pm: e.matmul(
                                pm[0:2, 0:512], c_bo[:], qs[:, ns * 512:(ns + 1) * 512],
                                start=True, stop=True),
                                r=("c_bo", qsk), w=(pmk,))
                            em.dve(lambda e, ns=ns, col=col, pm=pm: e.tensor_reduce(
                                nrm_s[0:2, col + ns:col + ns + 1], pm[0:2, 0:512], axis=AXX, op=ALU.max),
                                r=(pmk,), w=(f"nrm{col + ns}",))
                    return mred

                def mnorm_fin():
                    em.dve(lambda e: e.tensor_reduce(vtr_a[0:2, 2:3], nrm_s[0:2, 0:4], axis=AXX, op=ALU.max),
                           r=("nrm0", "nrm1", "nrm2", "nrm3"), w=("nq",))
                    em.dve(lambda e: e.tensor_reduce(vtr_a[0:2, 3:4], nrm_s[0:2, 4:8], axis=AXX, op=ALU.max),
                           r=("nrm4", "nrm5", "nrm6", "nrm7"), w=("nk",))
                    em.act(lambda e: e.activation(vtr_a[0:2, 4:5], vtr_a[0:2, 2:3], AF.Ln),
                           r=("nq",), w=("lnq",))
                    em.act(lambda e: e.activation(vtr_a[0:2, 5:6], vtr_a[0:2, 3:4], AF.Ln),
                           r=("nk",), w=("lnk",))
                    em.dve(lambda e: e.tensor_tensor(vtr_a[0:2, 6:7], vtr_a[0:2, 4:5], vtr_a[0:2, 5:6], op=ALU.add),
                           r=("lnq", "lnk"), w=("lnsum",))
                    em.act(lambda e: e.activation(vtr_a[0:2, 0:1], vtr_a[0:2, 6:7], AF.Exp, bias=0.0, scale=0.5),
                           r=("lnsum",), w=("vtra",))
                    em.dve(lambda e: e.tensor_scalar_mul(vtr_a[0:2, 0:1], vtr_a[0:2, 0:1], -0.125),
                           r=("vtra",), w=("vtra",))
                    em.dve(lambda e: e.transpose(vtr_b[:], vtr_a[:]), r=("vtra",), w=("vtrb",))
                    em.dve(lambda e: e.tensor_reduce(vtr_b[0:1, 2:3], vtr_b[0:1, 0:2],
                                                     axis=AXX, op=ALU.min),
                           r=("vtrb",), w=("vtrb",))

                def mnorm_bc():
                    em.pe(lambda e: e.matmul(pm1[:, 0:1], c_or[:], vtr_b[0:1, 2:3],
                                             start=True, stop=True),
                          r=("vtrb", "c_or"), w=("pm1",), tag="bcM")
                    em.act(lambda e: e.activation(bias_hd[sl][0][:], pm1[:, 0:1], AF.Copy),
                           r=("pm1",), w=(f"bias{sl}.0",))
                cl.append(mk_msq(qp, qk))
                cl.append(mk_mred(0))
                cl.append(mk_msq(kp_, kk))
                cl.append(mk_mred(4))
                cl.append(mnorm_fin)
                cl.append(mnorm_bc)
                return cl

            seq = []
            if en_attn:
                seq += [("attn", p, av_attn[p], f"ava{p}") for p in range(8)]
            if en_dense:
                seq += [("dense", p, av_dense[p], f"avd{p}") for p in range(4)]
            if seq:
                fq.extend(front_closures(seq[0][0], seq[0][1], 0))

            for tt in range(NT):
                xb, xkey = xt[tt % 2], f"xt{tt % 2}"
                xnb, xnkey = xnt[tt % 2], f"xnt{tt % 2}"
                src = x_in[b, tt * 128:(tt + 1) * 128, :]
                em.dma_sp(lambda e, d=xb, s=src: e.dma_start(d[:], s), r=(), w=(xkey,),
                          sem=f"dx{tt % 2}", tag=f"ldx{b}.{tt}")
                o = (tt % 2) * 8
                pk = str(tt % 2)
                em.dve(lambda e, xb=xb, o=o: e.tensor_reduce(ln_s[:, o:o + 1], xb[:], axis=AXX, op=ALU.add),
                       r=(xkey,), w=("ln0" + pk,))
                em.act(lambda e, xb=xb, o=o, tt=tt: e.activation(
                    junk2[:, (tt % 2) * H:(tt % 2) * H + H], xb[:], AF.Square, accum_out=ln_s[:, o + 1:o + 2]),
                    r=(xkey,), w=("ln1" + pk, "junk2." + pk))
                em.dve(lambda e, o=o: e.tensor_scalar_mul(ln_s[:, o + 2:o + 3], ln_s[:, o:o + 1], 1.0 / H),
                       r=("ln0" + pk,), w=("ln2" + pk,))
                em.dve(lambda e, o=o: e.tensor_tensor(ln_s[:, o + 3:o + 4], ln_s[:, o + 2:o + 3],
                                                      ln_s[:, o + 2:o + 3], op=ALU.mult),
                       r=("ln2" + pk,), w=("ln3" + pk,))
                em.dve(lambda e, o=o: e.scalar_tensor_tensor(ln_s[:, o + 4:o + 5], ln_s[:, o + 1:o + 2],
                                                             1.0 / H, ln_s[:, o + 3:o + 4],
                                                             op0=ALU.mult, op1=ALU.subtract),
                       r=("ln1" + pk, "ln3" + pk), w=("ln4" + pk,))
                em.act(lambda e, o=o: e.activation(ln_s[:, o + 5:o + 6], ln_s[:, o + 4:o + 5],
                                                   AF.Ln, bias=z_eps[:], scale=1.0),
                       r=("ln4" + pk, "z_eps"), w=("ln5" + pk,))
                em.act(lambda e, o=o: e.activation(ln_s[:, o + 6:o + 7], ln_s[:, o + 5:o + 6],
                                                   AF.Exp, bias=0.0, scale=-0.5),
                       r=("ln5" + pk,), w=("ln6" + pk,))
                em.dve(lambda e, xb=xb, xnb=xnb, o=o: e.scalar_tensor_tensor(
                    xnb[:], xb[:], ln_s[:, o + 2:o + 3], ln_s[:, o + 6:o + 7].to_broadcast([128, H]),
                    op0=ALU.subtract, op1=ALU.mult),
                    r=(xkey, "ln2" + pk, "ln6" + pk), w=(xnkey,))
                em.dma_sp(lambda e, xnb=xnb, d=xn_scr[b, tt * 128:(tt + 1) * 128, :]: e.dma_start(d, xnb[:]),
                          r=(xnkey,), w=(xn_scr_key + f".{tt}",), sem="dxn", tag=f"stxn{b}.{tt}")
                for g in range(2):
                    for c4 in range(4):
                        c = g * 4 + c4
                        em.pe(lambda e, xnb=xnb, c=c, c4=c4: e.transpose(
                            ptr[:, c4 * 128:(c4 + 1) * 128], xnb[:, c * 128:(c + 1) * 128], c_id[:]),
                            r=(xnkey, "c_id"), w=("ptr",))
                    em.dve(lambda e, g=g, tt=tt, xn_fm=xn_fm: e.tensor_copy(
                        xn_fm[:].rearrange("p (c t) -> p c t", c=NCH)[:, g * 4:(g + 1) * 4, tt * 128:(tt + 1) * 128],
                        ptr[:].rearrange("p (c t) -> p c t", c=4)),
                        r=("ptr",), w=(f"xnfm.{tt}.{g}",))
                if tt == 0:
                    drain_front(1)
                elif tt % 4 == 3:
                    drain_front(2)
            xn_keys = tuple(f"xnfm.{tt}.{g}" for tt in range(NT) for g in range(2))

            if en_mosa:
                for ns in range(4):
                    for c in range(NCH):
                        em.pe(lambda e, ns=ns, c=c, xn_fm=xn_fm: e.matmul(
                            pm1[0:8, 0:512], c_router[:, c * 8:(c + 1) * 8],
                            xn_fm[:, c * T + ns * 512:c * T + (ns + 1) * 512],
                            start=(c == 0), stop=(c == NCH - 1)),
                            r=("c_router",) + xn_keys, w=("pm1",), tag="router")
                    em.act(lambda e, ns=ns: e.activation(
                        r_scores[:, ns * 512:(ns + 1) * 512], pm1[0:8, 0:512], AF.Copy),
                        r=("pm1",), w=("r_scores",), tag="rsc")

                def bis_init():
                    em.dve(lambda e: e.tensor_reduce(b_hi[0][:], r_scores[:], axis=AXX, op=ALU.max),
                           r=("r_scores",), w=("b_hi0",))
                    em.dve(lambda e: e.tensor_reduce(b_lo[0][:], r_scores[:], axis=AXX, op=ALU.min),
                           r=("b_lo0", "r_scores"), w=("b_lo0",))
                    em.dve(lambda e: e.tensor_scalar_add(b_lo[0][:], b_lo[0][:], -1.0),
                           r=("b_lo0",), w=("b_lo0",))
                    em.dve(lambda e: e.tensor_scalar(b_mid[:], b_lo[0][:], b_hi[0][:], 0.5,
                                                     op0=ALU.add, op1=ALU.mult),
                           r=("b_lo0", "b_hi0"), w=("b_mid",))
                bg_queue.append(bis_init)

                def mk_iter(i):
                    cur, nxt = i % 2, (i + 1) % 2
                    cls = []
                    for cc in range(4):
                        cls.append(lambda cc=cc: em.dve(lambda e, cc=cc: e.tensor_scalar(
                            junk2[0:8, cc * 512:(cc + 1) * 512], r_scores[:, cc * 512:(cc + 1) * 512],
                            b_mid[:], 0.0, op0=ALU.is_gt, op1=ALU.add,
                            accum_out=b_c4[:, cc:cc + 1]),
                            r=("r_scores", "b_mid"), w=("junk2.0", "junk2.1", "b_c4")))

                    def tail():
                        em.dve(lambda e: e.tensor_reduce(b_cnt[:], b_c4[:], axis=AXX, op=ALU.add),
                               r=("b_c4",), w=("b_cnt",))
                        em.dve(lambda e: e.tensor_single_scalar(b_cond[:], b_cnt[:], float(KSEL), op=ALU.is_ge),
                               r=("b_cnt",), w=("b_cond",))
                        em.dve(lambda e: e.select(b_lo[nxt][:], b_cond[:], b_mid[:], b_lo[cur][:]),
                               r=("b_cond", "b_mid", f"b_lo{cur}"), w=(f"b_lo{nxt}",))
                        em.dve(lambda e: e.select(b_hi[nxt][:], b_cond[:], b_hi[cur][:], b_mid[:]),
                               r=("b_cond", "b_mid", f"b_hi{cur}"), w=(f"b_hi{nxt}",))
                        em.dve(lambda e: e.tensor_scalar(b_mid[:], b_lo[nxt][:], b_hi[nxt][:], 0.5,
                                                         op0=ALU.add, op1=ALU.mult),
                               r=(f"b_lo{nxt}", f"b_hi{nxt}"), w=("b_mid",))
                    cls.append(tail)
                    return cls
                for i in range(N_BISECT):
                    bg_queue.extend(mk_iter(i))
                lo_fin = f"b_lo{N_BISECT % 2}"

                def bis_post(r_mask=r_mask, r_grid=r_grid):
                    em.dve(lambda e: e.tensor_single_scalar(r_mask[:], r_scores[:],
                                                            b_lo[N_BISECT % 2][:], op=ALU.is_gt),
                           r=("r_scores", lo_fin), w=("r_mask",))
                    em.dve(lambda e: e.tensor_tensor_scan(r_grid[:], r_mask[:],
                                                          z_one[0:8, :].to_broadcast([8, T]),
                                                          0.0, op0=ALU.add, op1=ALU.mult),
                           r=("r_mask", "z_one"), w=("r_grid",))
                    em.dve(lambda e: e.scalar_tensor_tensor(r_grid[:], r_grid[:], 1.0, r_mask[:],
                                                            op0=ALU.mult, op1=ALU.mult),
                           r=("r_grid", "r_mask"), w=("r_grid",))
                bg_queue.append(bis_post)


            def emit_attention(branch, p, sl, av_tile, avkey):
                qp, qk = q_pair[sl], f"q_pair{sl}"
                kp_, kk = k_pair[sl], f"k_pair{sl}"
                vp = v_pair[sl]
                sct = [0]
                for w in range(4):
                    w0 = 512 * w
                    w1 = w0 + 512
                    kts = [kt for kt in range(NT) if kt * 128 < w1]
                    klast = kts[-1]

                    def emit_S_pair(idx, kt, w0=w0, w1=w1):
                        qstart = max(kt * 128, w0)
                        cw = w1 - qstart
                        slab, slk = pt3[idx % 4], f"pt{idx % 4}"
                        pss = []
                        for hb in (0, 64):
                            bi = sct[0] % 3
                            ps, psk = pS2[bi], f"pS{bi}"
                            sct[0] += 1
                            pss.append((hb, bi, ps, psk))
                            em.pe(lambda e, ps=ps, cw=cw, kt=kt, hb=hb, qa=qstart: e.matmul(
                                ps[:, 0:cw],
                                kp_[hb:hb + 64, kt * 128:(kt + 1) * 128],
                                qp[hb:hb + 64, qa:qa + cw],
                                start=True, stop=True, tile_position=(hb, 0)),
                                r=(kk, qk), w=(psk,), tag="S")
                        for hb, bi, ps, psk in pss:
                            so = (hb // 64) * 512
                            em.act(lambda e, slab=slab, ps=ps, cw=cw, so=so: e.activation(
                                slab[:, so:so + cw], ps[:, 0:cw], AF.Exp,
                                bias=bias_hd[sl][0][:], scale=0.125),
                                r=(psk, f"bias{sl}.0"),
                                w=(slk + ("a" if hb == 0 else "b"),), tag="exp")
                        if kt * 128 >= w0:
                            for hb in (0, 64):
                                so = (hb // 64) * 512
                                sk2 = slk + ("a" if hb == 0 else "b")
                                em.dve(lambda e, slab=slab, so=so: e.tensor_tensor(
                                    slab[:, so:so + 128], slab[:, so:so + 128],
                                    c_um[:], op=ALU.mult),
                                    r=(sk2, "c_um"), w=(sk2,), tag="dmask")

                    def emit_AV_pair(idx, kt, w0=w0, klast=klast):
                        qstart = max(kt * 128, w0)
                        cw = 512 - (qstart - w0)
                        co = qstart - w0
                        slab, slk = pt3[idx % 4], f"pt{idx % 4}"
                        for hb in (0, 64):
                            so = (hb // 64) * 512
                            vsl = vp[:, kt * 192 + hb:kt * 192 + hb + 128]
                            em.pe(lambda e, vsl=vsl, slab=slab, kt=kt, cw=cw, co=co,
                                  so=so, hb=hb: e.matmul(
                                pav[0:128, (hb // 64) * 512 + co:(hb // 64) * 512 + co + cw],
                                vsl, slab[:, so:so + cw],
                                start=(kt == 0), stop=(kt == klast)),
                                r=(slk + ("a" if hb == 0 else "b"),
                                   f"v_pair{sl}.{kt}", f"v_ones{sl}"),
                                w=(f"pavb{hb // 64}",), tag="AV")

                    def push_evac(hb, w0=w0):
                        pk = f"pavb{hb // 64}"
                        co = (hb // 64) * 512
                        pr = ev_par[0] % 2
                        ev_par[0] += 1
                        rbb, rbk = rb_sb[pr], f"rbsb{pr}"
                        lb = 64 - hb
                        evq.append(lambda: em.act(lambda e: e.activation(
                            rbb[:, 0:512], pav[lb:lb + 64, co:co + 512], AF.Ln),
                            r=(pk,), w=(rbk,), tag="lln"))
                        evq.append(lambda: em.act(lambda e: e.activation(
                            rbb[:, 0:512], rbb[:, 0:512], AF.Exp, bias=0.0, scale=-1.0),
                            r=(rbk,), w=(rbk,), tag="lexp"))

                        def ev_rest():
                            em.dve(lambda e: e.tensor_tensor(
                                av_tile[hb:hb + 64, w0:w0 + 512],
                                pav[hb:hb + 64, co:co + 512], rbb[:, 0:512], op=ALU.mult),
                                r=(pk, rbk), w=(avkey,), tag="avn")
                        evq.append(ev_rest)

                    SK = 3
                    for idx, kt in enumerate(kts):
                        emit_S_pair(idx, kt)
                        drain_ev(1)
                        drain_front(1)
                        drain_bg(1)
                        if idx >= SK:
                            emit_AV_pair(idx - SK, kts[idx - SK])
                            drain_ev(1)
                            drain_front(1)
                    for idx in range(max(0, len(kts) - SK), len(kts)):
                        emit_AV_pair(idx, kts[idx])
                        drain_ev(1)
                        drain_front(1)
                        drain_bg(1)
                    for hb in (0, 64):
                        push_evac(hb)
                    while evq:
                        drain_ev(1)
                        drain_front(1)

            for i_, (br, p, avt, avk) in enumerate(seq):
                sl = i_ % 2
                if i_ == 0:
                    drain_front(10 ** 9)
                if i_ + 1 < len(seq):
                    fq.extend(front_closures(seq[i_ + 1][0], seq[i_ + 1][1], (i_ + 1) % 2))
                emit_attention(br, p, sl, avt, avk)
                drain_front(10 ** 9)
            drain_bg(10 ** 9)

            if en_mosa:
                for tt in range(NT):
                    em.pe(lambda e, tt=tt, r_grid=r_grid: e.transpose(
                        pav[:, tt * 8:tt * 8 + 8], r_grid[:, tt * 128:(tt + 1) * 128], c_idf[0:8, 0:8]),
                        r=("r_grid", "c_idf"), w=("pavb0",), tag="trG")
                em.dve(lambda e: e.tensor_copy(g_tm[:], pav[:, 0:128]),
                       r=("pavb0",), w=("g_tm",))
                for tt in range(NT):
                    em.pe(lambda e, tt=tt: e.transpose(
                        pm1[:, tt * 8:tt * 8 + 8], r_scores[:, tt * 128:(tt + 1) * 128], c_idf[0:8, 0:8]),
                        r=("r_scores", "c_idf"), w=("pm1",), tag="trS")
                em.dve(lambda e: e.tensor_copy(
                    sc_tm[:].rearrange("p (k n) -> p k n", n=9)[:, :, 1:9],
                    pm1[:, 0:128].rearrange("p (k n) -> p k n", n=8)),
                    r=("pm1",), w=("sc_tm",))
                em.pool(lambda e: e.tensor_copy(
                    sc_tm[:].rearrange("p (k n) -> p k n", n=9)[:, :, 0:1],
                    c_tok[:].rearrange("p (k one) -> p k one", one=1)),
                    r=("c_tok",), w=("sc_tm",))

            em.barrier()
            scA.close()

            scB = ExitStack()
            c_iotar = S(f"c_iotar{b}", (128, T), F32, scB)
            av_mosa = [S(f"av_mosa{b}_{p}", (128, T), BF16, scB) for p in range(4)]
            wo_all = S(f"wo_all{b}", (128, 8192), BF16, scB)
            dwo_all = S(f"dwo_all{b}", (128, 4096), BF16, scB)
            mwo_all = S(f"mwo_all{b}", (128, 4096), BF16, scB)
            em.dma_sp(lambda e, dd=c_iotar[:], ss=d_iota_row[:, :]: e.dma_start(dd, ss),
                      r=(), w=("c_iotar",), sem="dwo", tag="ld:c_iotar")
            for (dst_, src_, k_) in ((wo_all, d_woT, "wo_all"), (dwo_all, d_dwo, "dwo_all"),
                                     (mwo_all, d_mwo, "mwo_all")):
                em.dma_pool(lambda e, dd=dst_[:], ss=src_[:, :]: e.dma_start(dd, ss),
                            r=(), w=(k_,), sem="dwo", tag="ld:" + k_)
            em.fence_keys(("c_iotar", "wo_all", "dwo_all", "mwo_all"))
            if en_mosa:
                wbufm = [S(f"wbm{b}_{i}", (128, 1536), BF16, scB) for i in range(2)]
                oh_t = [S(f"oh_t{b}_{i}", (128, KSEL), F32, scB) for i in range(2)]
                idxsc8 = S(f"idxsc8{b}", (128, 32), F32, scB)
                cmT = S(f"cmT{b}", (2, 256), F32, scB)
                idx_u = [S(f"idx_u{b}_{i}", (128, 2), U32, scB) for i in range(2)]
                gate_c = S(f"gate_c{b}", (128, 2), F32, scB)
                gate_e = S(f"gate_e{b}", (128, 2), F32, scB)
                xg = [[S(f"xg{b}_{pp}_{i}", (128, H), BF16, scB) for i in range(2)]
                      for pp in range(2)]
                xg_fm = S(f"xg_fm{b}", (128, NCH * KSEL), BF16, scB)
                mq8 = S(f"mq8{b}", (64, 8 * KSEL), BF16, scB)
                mk8 = S(f"mk8{b}", (64, 8 * KSEL), BF16, scB)
                mv8 = S(f"mv8{b}", (128, 8 * 130), BF16, scB)
                msq = S(f"msq{b}", (64, KSEL), BF16, scB)
                mpt = S(f"mpt{b}", (128, 3 * 128), BF16, scB)
                mavtm = S(f"mavtm{b}", (128, 2 * D), BF16, scB)
                bias_m = S(f"bias_m{b}", (128, 1), F32, scB)
                oh_rt = [S(f"oh_rt{b}_{i}", (128, 512), BF16, scB) for i in range(2)]

                em.pool(lambda e, mv8=mv8: e.memset(
                    mv8[:].rearrange("p (h s) -> p h s", s=130)[:, :, 64:130:65], 1.0),
                    w=("mv_ones",), tag="mvones")

                def m_stage1(hh):
                    par = hh % 2
                    wb, wbkey = wbufm[par], f"wbm{par}"
                    ap = d_mwqkv[:, :].rearrange("p (c e) -> p c e", c=8)[:, :, hh * 192:(hh + 1) * 192]
                    em.dma_pool(lambda e, ap=ap, wb=wb: e.dma_start(
                        wb[:, 0:1536].rearrange("p (c e) -> p c e", c=8), ap),
                        r=(), w=(wbkey,), sem=f"dw{par}", tag=f"ldmw{hh}")
                    for tt in range(NT):
                        oh, ohk = oh_t[tt % 2], f"oh{tt % 2}"
                        em.dve(lambda e, oh=oh, tt=tt, hh=hh: e.tensor_single_scalar(
                            oh[:], c_iota1[:], g_tm[:, tt * 8 + hh:tt * 8 + hh + 1], op=ALU.is_equal),
                            r=("c_iota1", "g_tm"), w=(ohk,))
                        em.pe(lambda e, oh=oh, tt=tt, hh=hh: e.matmul(
                            pm1[0:2, 0:256],
                            sc_tm[:, tt * 9:tt * 9 + hh + 2:hh + 1],
                            oh[:, 0:256],
                            start=(tt == 0), stop=(tt == NT - 1)),
                            r=(ohk, "sc_tm"), w=("pm1",), tag="cmm")
                    em.act(lambda e: e.activation(cmT[:, 0:256], pm1[0:2, 0:256], AF.Copy),
                           r=("pm1",), w=("cmT",))
                    for half in range(2):
                        em.pe(lambda e, half=half: e.transpose(
                            pm2[0:128, half * 2:half * 2 + 2],
                            cmT[:, half * 128:(half + 1) * 128], c_idf[0:2, 0:2]),
                            r=("cmT", "c_idf"), w=("pm2",), tag="cmtr")
                    em.act(lambda e, hh=hh, idxsc8=idxsc8: e.activation(
                        idxsc8[:, hh * 4:hh * 4 + 4], pm2[0:128, 0:4], AF.Copy),
                        r=("pm2",), w=(f"idxsc.{hh}",))
                    em.dve(lambda e, par=par, hh=hh, idxsc8=idxsc8: e.tensor_copy(
                        idx_u[par][:], idxsc8[:, hh * 4:hh * 4 + 4:2]),
                        r=(f"idxsc.{hh}",), w=(f"idx_u{par}",))
                    for rs in range(2):
                        em.dma_pool(lambda e, rs=rs, b=b, par=par: e.indirect_dma_start(
                            xg[par][rs][:], None,
                            xn_scr[:, :, :].rearrange("b t h -> (b t) h"),
                            bass.IndirectOffsetOnAxis(ap=idx_u[par][:, rs:rs + 1], axis=0),
                            element_offset=b * T * H),
                            r=(f"idx_u{par}",) + tuple(xn_scr_key + f".{tt}" for tt in range(NT)),
                            w=(f"xg{par}{rs}",), sem=f"dg{rs}", tag=f"gather{hh}.{rs}")

                def m_stage2(hh):
                    par = hh % 2
                    wb, wbkey = wbufm[par], f"wbm{par}"
                    mpb = [0]

                    def mpav():
                        mpb[0] += 1
                        return ((pav[:, 0:64], "pavb0") if mpb[0] % 2
                                else (pav[:, 512:576], "pavb1"))
                    for rs in range(2):
                        for cg in range(2):
                            for c4 in range(4):
                                c = cg * 4 + c4
                                em.pe(lambda e, rs=rs, c=c, c4=c4, par=par: e.transpose(
                                    ptr[:, c4 * 128:(c4 + 1) * 128], xg[par][rs][:, c * 128:(c + 1) * 128], c_id[:]),
                                    r=(f"xg{par}{rs}", "c_id"), w=("ptr",), tag="trXg")
                            em.dve(lambda e, rs=rs, cg=cg, xg_fm=xg_fm: e.tensor_copy(
                                xg_fm[:].rearrange("p (c r) -> p c r", c=NCH)[:, cg * 4:(cg + 1) * 4, rs * 128:(rs + 1) * 128],
                                ptr[:].rearrange("p (c t) -> p c t", c=4)),
                                r=("ptr",), w=("xg_fm",))
                    rope_tables_tm(mcos_tm, msin_tm,
                                   lambda ti, hh=hh, idxsc8=idxsc8: idxsc8[:, hh * 4 + ti * 2:hh * 4 + ti * 2 + 1],
                                   (f"idxsc.{hh}",), 2, "mrtk")
                    for (dst8, dk8, woff, ck, sk) in (
                            (mq8, f"mq.{hh}", 0, "mrtkcos", "mrtksin"),
                            (mk8, f"mk.{hh}", 64, "mrtkcos", "mrtksin")):
                        for rs in range(2):
                            qb, qbk = qtmb[rs % 2], f"qtmb{rs % 2}"
                            mpv, mpk = mpav()
                            for c in range(NCH):
                                em.pe(lambda e, c=c, rs=rs, wb=wb, xg_fm=xg_fm, woff=woff, mpv=mpv: e.matmul(
                                    mpv[:, 0:64], xg_fm[:, c * KSEL + rs * 128:c * KSEL + (rs + 1) * 128],
                                    wb[:, c * 192 + woff:c * 192 + woff + 64],
                                    start=(c == 0), stop=(c == NCH - 1)),
                                    r=(wbkey, "xg_fm"), w=(mpk,), tag="mqk")
                            em.act(lambda e, qb=qb, mpv=mpv: e.activation(qb[:, 0:64], mpv[:, 0:64], AF.Copy),
                                   r=(mpk,), w=(qbk,))
                            rope_apply_tm(qb, qbk,
                                          mcos_tm[:, rs * 32:(rs + 1) * 32],
                                          msin_tm[:, rs * 32:(rs + 1) * 32], ck, sk, 1)
                            em.pe(lambda e, qb=qb, rs=rs: e.transpose(
                                ptr[0:64, rs * 128:(rs + 1) * 128], qb[:, 0:64], c_id[:]),
                                r=(qbk, "c_id"), w=("ptr",), tag="trmQ")
                        em.act(lambda e, dst8=dst8, hh=hh: e.activation(
                            dst8[:, hh * KSEL:(hh + 1) * KSEL], ptr[0:64, 0:KSEL], AF.Copy),
                            r=("ptr",), w=(dk8,))
                    for rs in range(2):
                        mpv, mpk = mpav()
                        for c in range(NCH):
                            em.pe(lambda e, c=c, rs=rs, wb=wb, xg_fm=xg_fm, mpv=mpv: e.matmul(
                                mpv[:, 0:64], xg_fm[:, c * KSEL + rs * 128:c * KSEL + (rs + 1) * 128],
                                wb[:, c * 192 + 128:c * 192 + 192],
                                start=(c == 0), stop=(c == NCH - 1)),
                                r=(wbkey, "xg_fm"), w=(mpk,), tag="mv")
                        em.act(lambda e, rs=rs, hh=hh, mv8=mv8, mpv=mpv: e.activation(
                            mv8[:, hh * 130 + rs * 65:hh * 130 + rs * 65 + 64], mpv[:, 0:64], AF.Copy),
                            r=(mpk,), w=(f"mv.{hh}.{rs}",))

                m_stage1(0)
                for hh in range(8):
                    if hh + 1 < 8:
                        m_stage1(hh + 1)
                    m_stage2(hh)

                for hh in range(8):
                    mqv = mq8[:, hh * KSEL:(hh + 1) * KSEL]
                    mkv = mk8[:, hh * KSEL:(hh + 1) * KSEL]
                    em.act(lambda e, mqv=mqv, msq=msq: e.activation(msq[:], mqv, AF.Square),
                           r=(f"mq.{hh}",), w=("msq",))
                    em.pe(lambda e, msq=msq: e.matmul(pm1[0:1, 0:KSEL], c_bo[0:64, 0:1], msq[:],
                                                      start=True, stop=True),
                          r=("c_bo", "msq"), w=("pm1",), tag="mnq")
                    em.dve(lambda e: e.tensor_reduce(nrm_s[0:1, 0:1], pm1[0:1, 0:KSEL], axis=AXX, op=ALU.max),
                           r=("pm1",), w=("nq",))
                    em.act(lambda e, mkv=mkv, msq=msq: e.activation(msq[:], mkv, AF.Square),
                           r=(f"mk.{hh}",), w=("msq",))
                    em.pe(lambda e, msq=msq: e.matmul(pm1[0:1, 0:KSEL], c_bo[0:64, 0:1], msq[:],
                                                      start=True, stop=True),
                          r=("c_bo", "msq"), w=("pm1",), tag="mnk")
                    em.dve(lambda e: e.tensor_reduce(nrm_s[0:1, 4:5], pm1[0:1, 0:KSEL], axis=AXX, op=ALU.max),
                           r=("pm1",), w=("nk",))
                    em.act(lambda e: e.activation(nrm_s[0:1, 1:2], nrm_s[0:1, 0:1], AF.Ln),
                           r=("nq",), w=("lnq",))
                    em.act(lambda e: e.activation(nrm_s[0:1, 5:6], nrm_s[0:1, 4:5], AF.Ln),
                           r=("nk",), w=("lnk",))
                    em.dve(lambda e: e.tensor_tensor(nrm_s[0:1, 2:3], nrm_s[0:1, 1:2], nrm_s[0:1, 5:6], op=ALU.add),
                           r=("lnq", "lnk"), w=("lnsum",))
                    em.act(lambda e: e.activation(nrm_s[0:1, 3:4], nrm_s[0:1, 2:3], AF.Exp, bias=0.0, scale=0.5),
                           r=("lnsum",), w=("Mt",))
                    em.dve(lambda e: e.tensor_scalar_mul(nrm_s[0:1, 3:4], nrm_s[0:1, 3:4], -0.125),
                           r=("Mt",), w=("Mt",))
                    em.pe(lambda e: e.matmul(pm1[:, 0:1], c_or[:], nrm_s[0:1, 3:4], start=True, stop=True),
                          r=("Mt", "c_or"), w=("pm1",), tag="mbc")
                    em.act(lambda e, bias_m=bias_m: e.activation(bias_m[:], pm1[:, 0:1], AF.Copy),
                           r=("pm1",), w=("bias_m",))

                    tiles = [(0, 0), (0, 1), (1, 1)]
                    for ti, (kt, qt) in enumerate(tiles):
                        em.pe(lambda e, ti=ti, kt=kt, qt=qt, hh=hh, mq8=mq8, mk8=mk8: e.matmul(
                            pS2[0][:, ti * 128:(ti + 1) * 128],
                            mk8[:, hh * KSEL + kt * 128:hh * KSEL + (kt + 1) * 128],
                            mq8[:, hh * KSEL + qt * 128:hh * KSEL + (qt + 1) * 128],
                            start=True, stop=(kt != qt)),
                            r=(f"mk.{hh}", f"mq.{hh}"), w=("pS0",), tag="mS")
                        if kt == qt:
                            em.pe(lambda e, ti=ti: e.matmul(
                                pS2[0][:, ti * 128:(ti + 1) * 128], c_id[:], c_cm[:],
                                start=False, stop=True),
                                r=("c_id", "c_cm"), w=("pS0",), tag="mmask")
                    em.act(lambda e, mpt=mpt, bias_m=bias_m: e.activation(
                        mpt[:, 0:384], pS2[0][:, 0:384], AF.Exp, bias=bias_m[:], scale=0.125),
                        r=("pS0", "bias_m"), w=("mpt",), tag="mexp")
                    for ti, (kt, qt) in enumerate(tiles):
                        em.pe(lambda e, ti=ti, kt=kt, qt=qt, hh=hh, mpt=mpt, mv8=mv8: e.matmul(
                            pav[:, qt * 65:qt * 65 + 65], mpt[:, ti * 128:(ti + 1) * 128],
                            mv8[:, hh * 130 + kt * 65:hh * 130 + kt * 65 + 65],
                            start=(kt == 0), stop=(kt == qt)),
                            r=("mpt", f"mv.{hh}.{kt}", "mv_ones"), w=("pavb0",), tag="mAV")
                    em.act(lambda e, hh=hh, gate_e=gate_e, idxsc8=idxsc8: e.activation(
                        gate_e[:], idxsc8[:, hh * 4 + 1:hh * 4 + 4:2], AF.Exp, bias=0.0, scale=-1.0),
                        r=(f"idxsc.{hh}",), w=("gate_e",))
                    em.dve(lambda e, gate_e=gate_e: e.tensor_scalar_add(gate_e[:], gate_e[:], 1.0),
                           r=("gate_e",), w=("gate_e",))
                    em.dve(lambda e, gate_c=gate_c, gate_e=gate_e: e.reciprocal(gate_c[:], gate_e[:]),
                           r=("gate_e",), w=("gate_c",))
                    em.dve(lambda e: e.reciprocal(
                        nrm_s[:, 0:2], pav[:, 64:130:65]),
                        r=("pavb0",), w=("mrcp",))
                    for rs in range(2):
                        em.dve(lambda e, rs=rs, mavtm=mavtm, gate_c=gate_c: e.tensor_scalar(
                            mavtm[:, rs * 64:(rs + 1) * 64], pav[:, rs * 65:rs * 65 + 64],
                            nrm_s[:, rs:rs + 1], gate_c[:, rs:rs + 1], op0=ALU.mult, op1=ALU.mult),
                            r=("pavb0", "mrcp", "gate_c"), w=("mavtm",))
                    hb2 = (hh % 2) * 64
                    for ns in range(4):
                        for rs in range(2):
                            em.dve(lambda e, rs=rs, ns=ns, hh=hh, oh_rt=oh_rt, idxsc8=idxsc8: e.tensor_single_scalar(
                                oh_rt[rs][:], c_iotar[:, ns * 512:(ns + 1) * 512],
                                idxsc8[:, hh * 4 + rs * 2:hh * 4 + rs * 2 + 1], op=ALU.is_equal),
                                r=("c_iotar", f"idxsc.{hh}"), w=(f"ohrt{rs}",))
                        em.pe(lambda e, ns=ns, mavtm=mavtm, oh_rt=oh_rt: e.matmul(
                            pm1[0:64, 0:512], mavtm[:, 0:64], oh_rt[0][:],
                            start=True, stop=False),
                            r=("mavtm", "ohrt0"), w=("pm1",), tag="msc")
                        em.pe(lambda e, ns=ns, mavtm=mavtm, oh_rt=oh_rt: e.matmul(
                            pm1[0:64, 0:512], mavtm[:, 64:128], oh_rt[1][:],
                            start=False, stop=True),
                            r=("mavtm", "ohrt1"), w=("pm1",), tag="msc2")
                        em.act(lambda e, ns=ns, hh=hh, hb2=hb2: e.activation(
                            av_mosa[hh // 2][hb2:hb2 + 64, ns * 512:(ns + 1) * 512],
                            pm1[0:64, 0:512], AF.Copy),
                            r=("pm1",), w=(f"avm{hh // 2}",))

            srcs = []
            if en_attn:
                srcs += [(av_attn[p], wo_all, p, (f"ava{p}",), "wo_all") for p in range(8)]
            if en_dense:
                srcs += [(av_dense[p], dwo_all, p, (f"avd{p}",), "dwo_all") for p in range(4)]
            if en_mosa:
                srcs += [(av_mosa[p], mwo_all, p, (f"avm{p}",), "mwo_all") for p in range(4)]
            obanks = [(pS2[0], "pS0"), (pS2[1], "pS1"), (pS2[2], "pS2"),
                      (pav[:, 0:512], "pavb0"), (pav[:, 512:1024], "pavb1")]
            obc = [0]

            def ldx(tt):
                xb, xkey = xt[tt % 2], f"xt{tt % 2}"
                src = x_in[b, tt * 128:(tt + 1) * 128, :]
                em.dma_sp(lambda e, d=xb, s=src: e.dma_start(d[:], s), r=(), w=(xkey,),
                          sem=f"dx{tt % 2}", tag=f"rldx{b}.{tt}")
            ldx(0)
            ldx(1)
            for tt in range(NT):
                xb, xkey = xt[tt % 2], f"xt{tt % 2}"
                ob, obkey = out_sb[tt % 2], f"osb{tt % 2}"
                for ns in range(2):
                    pb, pbk = obanks[obc[0] % 5]
                    obc[0] += 1
                    if srcs:
                        for ki, (av, wsb, p, avkeys, wkey) in enumerate(srcs):
                            em.pe(lambda e, pb=pb, av=av, wsb=wsb, p=p, tt=tt, ns=ns,
                                  ki=ki, nsr=len(srcs): e.matmul(
                                pb[:, 0:512], av[:, tt * 128:(tt + 1) * 128],
                                wsb[:, p * 1024 + ns * 512:p * 1024 + (ns + 1) * 512],
                                start=(ki == 0), stop=(ki == nsr - 1)),
                                r=avkeys + (wkey,), w=(pbk,), tag="oprj")
                        em.dve(lambda e, pb=pb, ob=ob, ns=ns, xb=xb: e.tensor_tensor(
                            ob[:, ns * 512:(ns + 1) * 512], pb[:, 0:512],
                            xb[:, ns * 512:(ns + 1) * 512], op=ALU.add),
                            r=(pbk, xkey), w=(obkey,), tag="oadd")
                    else:
                        em.dve(lambda e, ob=ob, ns=ns, xb=xb: e.tensor_copy(
                            ob[:, ns * 512:(ns + 1) * 512], xb[:, ns * 512:(ns + 1) * 512]),
                            r=(xkey,), w=(obkey,), tag="ocp")
                em.dma_sp(lambda e, ob=ob, d=out_t[b, tt * 128:(tt + 1) * 128, :]: e.dma_start(d, ob[:]),
                          r=(obkey,), w=(), sem="dout", tag=f"stout{b}.{tt}")
                if tt + 2 < NT:
                    ldx(tt + 2)
            em.barrier()
            scB.close()

        if tap_outs:
            for nm, (dram, sb) in tap_outs.items():
                em.dma_sp(lambda e, dram=dram, sb=sb: e.dma_start(dram[:, :], sb[:]),
                          r=(), w=(), sem="dtap", tag="tap:" + nm)
        em.barrier()
        em.finalize_into(nc, sems, dma_sems)
    return nc


_CACHE = {}


def kernel(**inputs):
    x = np.asarray(inputs["L0_attn_x"])
    host = prep_host(inputs)
    consts = host_consts()
    if "full" not in _CACHE:
        _CACHE["full"] = build_nc()
    nc = _CACHE["full"]
    in_maps = []
    for core in range(8):
        m = {"x": np.ascontiguousarray(x[core * NB:(core + 1) * NB])}
        m.update(host)
        m.update(consts)
        in_maps.append(m)
    res = run_bass_kernel_spmd(nc, in_maps, core_ids=list(range(8)))
    out = np.concatenate([np.asarray(r["out"]) for r in res.results], axis=0)
    return out.astype(np.float32)


# version 21 (final)
# speedup vs baseline: 1.1829x; 1.0051x over previous
import numpy as np
import ml_dtypes
import concourse.bass as bass
import concourse.mybir as mybir
from concourse.bass_utils import run_bass_kernel_spmd
from contextlib import ExitStack

F32 = mybir.dt.float32
BF16 = mybir.dt.bfloat16
U32 = mybir.dt.uint32
U8 = mybir.dt.uint8
I32 = mybir.dt.int32
AF = mybir.ActivationFunctionType
ALU = mybir.AluOpType
AXX = mybir.AxisListType.X

T = 2048
H = 1024
D = 64
NB = 2
NT = T // 128
NCH = H // 128
KSEL = 256
NEG = -1e30
N_BISECT = 60

TPI = 2.0 * np.pi
TPI_HI = float(np.float32(TPI))
TPI_LO = float(np.float32(TPI - np.float64(np.float32(TPI))))
PI_F = float(np.float32(np.pi))
HALFPI = float(np.float32(np.pi / 2))


class EM:
    ENGS = ("PE", "ACT", "DVE", "POOL", "SP")

    DMA_SEMS = ("dx0", "dx1", "dxn", "dout", "dpos", "dcon", "dconp",
                "dw0", "dw1", "dwo", "dg0", "dg1", "dtap")

    def __init__(self):
        self.ops = []
        self.last_w = {}
        self.readers = {}
        self.floor = None
        self.dma_idx = {s: 0 for s in self.DMA_SEMS}

    def op(self, eng, fn, r=(), w=(), dma=None, tag=""):
        o = {"eng": eng, "fn": fn, "dma": dma, "deps": [], "inc": False,
             "tag": tag, "i": len(self.ops)}
        if dma is not None:
            self.dma_idx[dma] += 1
            o["dma_i"] = self.dma_idx[dma]
        deps = {}
        if self.floor is not None:
            deps[self.floor["i"]] = self.floor
        for k in r:
            d = self.last_w.get(k)
            if d is not None:
                deps[d["i"]] = d
        for k in w:
            d = self.last_w.get(k)
            if d is not None:
                deps[d["i"]] = d
            for rd in self.readers.get(k, ()):
                deps[rd["i"]] = rd
        for d in deps.values():
            if d is o:
                continue
            if d["dma"] is None and d["eng"] == "PE" and eng == "PE" and o["dma"] is None:
                continue
            d["inc"] = True
            o["deps"].append(d)
        for k in w:
            self.last_w[k] = o
            self.readers[k] = []
        wset = set(w)
        for k in r:
            if k not in wset:
                self.readers.setdefault(k, []).append(o)
        self.ops.append(o)
        return o

    def pe(self, fn, r=(), w=(), tag=""):
        return self.op("PE", fn, r, w, tag=tag)

    def act(self, fn, r=(), w=(), tag=""):
        return self.op("ACT", fn, r, w, tag=tag)

    def dve(self, fn, r=(), w=(), tag=""):
        return self.op("DVE", fn, r, w, tag=tag)

    def pool(self, fn, r=(), w=(), tag=""):
        return self.op("POOL", fn, r, w, tag=tag)

    def dma_sp(self, fn, r=(), w=(), sem="dcon", tag=""):
        return self.op("SP", fn, r, w, dma=sem, tag=tag)

    def dma_pool(self, fn, r=(), w=(), sem="dconp", tag=""):
        return self.op("POOL", fn, r, w, dma=sem, tag=tag)

    def fence_keys(self, keys, eng="POOL"):
        return self.op(eng, lambda e: e.nop(), r=tuple(keys), w=tuple(keys), tag="kfence")

    def barrier(self):
        o = self.op("SP", lambda e: e.nop(), (), (), tag="fence")
        deps = {}
        for d in self.last_w.values():
            deps[d["i"]] = d
        for rl in self.readers.values():
            for d in rl:
                deps[d["i"]] = d
        o["deps"] = []
        for d in deps.values():
            if d is o:
                continue
            d["inc"] = True
            o["deps"].append(d)
        o["inc"] = True
        self.last_w = {k: o for k in self.last_w}
        self.readers = {}
        self.floor = o
        return o

    def finalize_into(self, nc, sems, dma_sems):
        cnt = {e: 0 for e in self.ENGS}
        for o in self.ops:
            if o["dma"] is None and o["inc"]:
                cnt[o["eng"]] += 1
                o["inc_no"] = cnt[o["eng"]]
        for o in self.ops:
            waits = {}
            for d in o["deps"]:
                if d["dma"] is not None:
                    s, v = d["dma"], 16 * d["dma_i"]
                else:
                    s, v = d["eng"], d["inc_no"]
                if waits.get(s, 0) < v:
                    waits[s] = v
            o["waits"] = waits
        allsems = dict(sems)
        allsems.update(dma_sems)

        def emit_for(ename, eng):
            wm = {}
            for o in self.ops:
                if o["eng"] != ename:
                    continue
                for s, v in sorted(o["waits"].items()):
                    if wm.get(s, 0) >= v:
                        continue
                    eng.wait_ge(allsems[s], v)
                    wm[s] = v
                inst = o["fn"](eng)
                if o["dma"] is not None:
                    inst.then_inc(allsems[o["dma"]], 16)
                elif o["inc"]:
                    inst.then_inc(allsems[ename], 1)

        with nc.Block() as block:
            @block.tensor
            def _(e):
                emit_for("PE", e)

            @block.scalar
            def _(e):
                emit_for("ACT", e)

            @block.vector
            def _(e):
                emit_for("DVE", e)

            @block.gpsimd
            def _(e):
                emit_for("POOL", e)

            @block.sync
            def _(e):
                emit_for("SP", e)


def _chunk_major(a, nchunk):
    rows, C = a.shape
    assert rows == nchunk * 128
    return np.ascontiguousarray(
        a.reshape(nchunk, 128, C).transpose(1, 0, 2).reshape(128, nchunk * C))


def prep_host(inp):
    h = {}
    h["wqT"] = _chunk_major(np.ascontiguousarray(np.asarray(inp["L0_attn_wq"]).T), 8)
    h["wkT"] = _chunk_major(np.ascontiguousarray(np.asarray(inp["L0_attn_wk"]).T), 8)
    h["wvT"] = _chunk_major(np.ascontiguousarray(np.asarray(inp["L0_attn_wv"]).T), 8)
    h["woT"] = _chunk_major(np.ascontiguousarray(np.asarray(inp["L0_attn_wo"]).T), 8)
    dw = np.asarray(inp["L0_ffn_dense_wqkv"])
    blocks = []
    for p in range(4):
        blocks.append(np.concatenate(
            [dw[:, 2 * p, 0:64], dw[:, 2 * p + 1, 0:64],
             dw[:, 2 * p, 64:128], dw[:, 2 * p + 1, 64:128],
             dw[:, 2 * p, 128:192], dw[:, 2 * p + 1, 128:192]], axis=1))
    h["dwqkv"] = _chunk_major(np.concatenate(blocks, axis=1), 8)
    mw = np.asarray(inp["L0_ffn_mosa_wqkv"])
    blocks = []
    for hh in range(8):
        blocks.append(np.concatenate(
            [mw[:, hh, 0:64], mw[:, hh, 64:128], mw[:, hh, 128:192]], axis=1))
    h["mwqkv"] = _chunk_major(np.concatenate(blocks, axis=1), 8)
    h["dwo"] = _chunk_major(np.asarray(inp["L0_ffn_dense_wo"]), 4)
    h["mwo"] = _chunk_major(np.asarray(inp["L0_ffn_mosa_wo"]), 4)
    h["router"] = _chunk_major(np.asarray(inp["L0_ffn_mosa_router_w"]), 8)
    return h


def host_consts():
    c = {}
    ident = np.eye(128, dtype=np.float32)
    c["identb"] = ident.astype(ml_dtypes.bfloat16)
    c["identf"] = ident
    c["cmaskb"] = (np.tril(np.ones((128, 128), dtype=np.float32), -1) * NEG).astype(ml_dtypes.bfloat16)
    c["umaskb"] = np.triu(np.ones((128, 128), dtype=np.float32)).astype(ml_dtypes.bfloat16)
    bo = np.zeros((128, 2), dtype=np.float32)
    bo[0:64, 0] = 1.0
    bo[64:128, 1] = 1.0
    c["blockones"] = bo.astype(ml_dtypes.bfloat16)
    c["onesrow"] = np.ones((1, 128), dtype=np.float32)
    j = np.arange(16, dtype=np.float64)
    v = (10000.0 ** (-j / 16.0)) / (2.0 * np.pi)
    hi = v.astype(np.float32)
    lo = (v - hi.astype(np.float64)).astype(np.float32)
    hi2 = np.concatenate([hi, hi])
    lo2 = np.concatenate([lo, lo])
    c["invfh"] = np.ascontiguousarray(np.broadcast_to(hi2, (128, 32)))
    c["invfl"] = np.ascontiguousarray(np.broadcast_to(lo2, (128, 32)))
    c["iota_row"] = np.ascontiguousarray(np.broadcast_to(np.arange(T, dtype=np.float32), (128, T)))
    c["iota1"] = np.ascontiguousarray(np.broadcast_to(np.arange(1, KSEL + 1, dtype=np.float32), (128, KSEL)))
    tok = (np.arange(NT)[None, :] * 128 + np.arange(128)[:, None]).astype(np.float32)
    c["tokidx"] = np.ascontiguousarray(tok)
    return c


def build_nc(en_attn=True, en_dense=True, en_mosa=True, taps=()):
    nc = bass.Bass("TRN2", target_bir_lowering=False)

    def din(name, shape, dtype=F32):
        return nc.dram_tensor(name, list(shape), dtype, kind="ExternalInput")

    x_in = din("x", (NB, T, H))
    d_wqT = din("wqT", (128, 8192))
    d_wkT = din("wkT", (128, 8192))
    d_wvT = din("wvT", (128, 8192))
    d_woT = din("woT", (128, 8192))
    d_dwqkv = din("dwqkv", (128, 8 * 1536))
    d_mwqkv = din("mwqkv", (128, 8 * 1536))
    d_dwo = din("dwo", (128, 4096))
    d_mwo = din("mwo", (128, 4096))
    d_router = din("router", (128, 64))
    d_identb = din("identb", (128, 128), BF16)
    d_identf = din("identf", (128, 128), F32)
    d_cmaskb = din("cmaskb", (128, 128), BF16)
    d_umaskb = din("umaskb", (128, 128), BF16)
    d_blockones = din("blockones", (128, 2), BF16)
    d_onesrow = din("onesrow", (1, 128))
    d_invfh = din("invfh", (128, 32))
    d_invfl = din("invfl", (128, 32))
    d_iota_row = din("iota_row", (128, T))
    d_iota1 = din("iota1", (128, KSEL))
    d_tokidx = din("tokidx", (128, NT))

    out_t = nc.dram_tensor("out", [NB, T, H], F32, kind="ExternalOutput")
    xn_scr = nc.dram_tensor("xn_scr", [NB, T, H], BF16)

    tap_outs = {}
    tap_req = list(taps)
    ctx = ExitStack()
    with ctx:
        def S(name, shape, dtype, stk=None):
            return (stk or ctx).enter_context(nc.sbuf_tensor(name, list(shape), dtype))

        def P(name, shape, dtype):
            return ctx.enter_context(nc.psum_tensor(name, list(shape), dtype))

        def tap(nm, tsb):
            if nm in tap_req:
                tap_outs[nm] = (nc.dram_tensor("dbg_" + nm, [tsb.shape[0], tsb.shape[1]],
                                               tsb.dtype, kind="ExternalOutput"), tsb)

        def tapdump(nm, tsb, rkeys):
            if nm not in tap_req:
                return
            dram = nc.dram_tensor("dbg_" + nm, [tsb.shape[0], tsb.shape[1]],
                                  tsb.dtype, kind="ExternalOutput")
            em.dma_sp(lambda e, dram=dram, tsb=tsb: e.dma_start(dram[:, :], tsb[:]),
                      r=tuple(rkeys), w=(), sem="dtap", tag="tapd:" + nm)

        c_id = S("c_id", (128, 128), BF16)
        c_idf = S("c_idf", (128, 128), F32)
        c_cm = S("c_cm", (128, 128), BF16)
        c_um = S("c_um", (128, 128), BF16)
        c_bo = S("c_bo", (128, 2), BF16)
        c_or = S("c_or", (1, 128), F32)
        c_invfh = S("c_invfh", (128, 32), F32)
        c_invfl = S("c_invfl", (128, 32), F32)
        c_iota1 = S("c_iota1", (128, KSEL), F32)
        c_tok = S("c_tok", (128, NT), F32)
        c_router = S("c_router", (128, 64), BF16)
        z_eps = S("z_eps", (128, 1), F32)
        z_one = S("z_one", (128, 1), F32)

        xt = [S(f"xt{i}", (128, H), F32) for i in range(2)]
        xnt = [S(f"xnt{i}", (128, H), BF16) for i in range(2)]
        junk2 = S("junk2", (128, T), BF16)
        ln_s = S("ln_s", (128, 16), F32)

        r_scores = S("r_scores", (8, T), F32)
        b_lo = [S(f"b_lo{i}", (8, 1), F32) for i in range(2)]
        b_hi = [S(f"b_hi{i}", (8, 1), F32) for i in range(2)]
        b_mid = S("b_mid", (8, 1), F32)
        b_cnt = S("b_cnt", (8, 1), F32)
        b_cond = S("b_cond", (8, 1), U8)
        b_c4 = S("b_c4", (8, 4), F32)

        av_attn = [S(f"av_attn{p}", (128, T), BF16) for p in range(8)]
        av_dense = [S(f"av_dense{p}", (128, T), BF16) for p in range(4)]

        g_tm = S("g_tm", (128, 128), F32)
        sc_tm = S("sc_tm", (128, NT * 9), F32)

        nrm_s = S("nrm_s", (128, 8), F32)
        rb_sb = [S(f"rb_sb{i}", (64, 512), F32) for i in range(2)]
        vtr_a = S("vtr_a", (32, 32), F32)
        vtr_b = S("vtr_b", (32, 32), F32)
        bias_hd = [[S(f"bias_hd{s}_{i}", (128, 1), F32) for i in range(2)] for s in range(2)]
        out_sb = [S(f"out_sb{i}", (128, H), F32) for i in range(2)]
        rts = [S(f"rts{i}", (128, 32), F32) for i in range(3)]
        rti = S("rti", (128, 32), I32)
        rp3 = [S(f"rp3{i}", (128, 32), BF16) for i in range(3)]
        pos2b = S("pos2b", (2, 128), F32)
        cos_tm = S("cos_tm", (128, NT * 32), BF16)
        sin_tm = S("sin_tm", (128, NT * 32), BF16)
        mcos_tm = S("mcos_tm", (128, 2 * 32), BF16)
        msin_tm = S("msin_tm", (128, 2 * 32), BF16)
        qtmb = [S(f"qtmb{i}", (128, 128), BF16) for i in range(4)]

        pSx = P("pSx", (128, 1536), F32)
        pS2 = [pSx[:, i * 512:(i + 1) * 512] for i in range(3)]
        pav = P("pav", (128, 1024), F32)
        ptr = P("ptr", (128, 512), BF16)
        pm1 = P("pm1", (128, 512), F32)
        pm2 = P("pm2", (128, 512), F32)

        sems = {e: ctx.enter_context(nc.semaphore("sem_" + e)) for e in EM.ENGS}
        dma_sems = {s: ctx.enter_context(nc.semaphore("sem_" + s)) for s in EM.DMA_SEMS}

        em = EM()

        tap("r_scores", r_scores)
        tap("g_tm", g_tm)
        tap("sc_tm", sc_tm)

        def ldc(dst, src, key, pool=False):
            f = (lambda e, dd=dst, ss=src: e.dma_start(dd, ss))
            (em.dma_pool if pool else em.dma_sp)(f, r=(), w=(key,), tag="ld:" + key)

        ldc(c_id[:], d_identb[:, :], "c_id")
        ldc(c_idf[:], d_identf[:, :], "c_idf")
        ldc(c_cm[:], d_cmaskb[:, :], "c_cm")
        ldc(c_um[:], d_umaskb[:, :], "c_um")
        ldc(c_bo[:], d_blockones[:, :], "c_bo")
        ldc(c_or[:], d_onesrow[:, :], "c_or")
        ldc(c_invfh[:], d_invfh[:, :], "c_invfh")
        ldc(c_invfl[:], d_invfl[:, :], "c_invfl")
        ldc(c_iota1[:], d_iota1[:, :], "c_iota1")
        ldc(c_tok[:], d_tokidx[:, :], "c_tok")
        ldc(c_router[:], d_router[:, :], "c_router", pool=True)
        em.dve(lambda e: e.memset(z_eps[:], 1e-5), w=("z_eps",))
        em.dve(lambda e: e.memset(z_one[:], 1.0), w=("z_one",))
        em.barrier()

        def rope_tables_tm(cos_tm, sin_tm, pos_col, pos_keys, ntile, kp):
            for ti in range(ntile):
                pap = pos_col(ti)
                em.dve(lambda e, pap=pap: e.tensor_scalar_mul(rts[1][:], c_invfl[:], pap),
                       r=("c_invfl",) + pos_keys, w=(kp + "1",))
                em.dve(lambda e, pap=pap: e.scalar_tensor_tensor(
                    rts[0][:], c_invfh[:], pap, rts[1][:], op0=ALU.mult, op1=ALU.add),
                    r=("c_invfh", kp + "1") + pos_keys, w=(kp + "0",))
                em.dve(lambda e: e.tensor_copy(rti[:], rts[0][:]),
                       r=(kp + "0",), w=(kp + "i",))
                em.dve(lambda e: e.tensor_copy(rts[2][:], rti[:]),
                       r=(kp + "i",), w=(kp + "2",))
                em.dve(lambda e: e.tensor_tensor(rts[1][:], rts[0][:], rts[2][:], op=ALU.subtract),
                       r=(kp + "0", kp + "2"), w=(kp + "1",))
                em.dve(lambda e: e.tensor_scalar_mul(rts[2][:], rts[1][:], TPI_LO),
                       r=(kp + "1",), w=(kp + "2",))
                em.dve(lambda e: e.scalar_tensor_tensor(
                    rts[0][:], rts[1][:], TPI_HI, rts[2][:], op0=ALU.mult, op1=ALU.add),
                    r=(kp + "1", kp + "2"), w=(kp + "0",))
                em.act(lambda e, ti=ti: e.activation(sin_tm[:, ti * 32:(ti + 1) * 32], rts[0][:], AF.Sin),
                       r=(kp + "0",), w=(kp + "sin",))
                em.dve(lambda e: e.tensor_scalar_add(rts[1][:], rts[0][:], HALFPI),
                       r=(kp + "0",), w=(kp + "1",))
                em.dve(lambda e: e.tensor_single_scalar(rts[2][:], rts[1][:], PI_F, op=ALU.is_gt),
                       r=(kp + "1",), w=(kp + "2",))
                em.dve(lambda e: e.tensor_scalar_mul(rts[2][:], rts[2][:], TPI),
                       r=(kp + "2",), w=(kp + "2",))
                em.dve(lambda e: e.tensor_tensor(rts[1][:], rts[1][:], rts[2][:], op=ALU.subtract),
                       r=(kp + "1", kp + "2"), w=(kp + "1",))
                em.act(lambda e, ti=ti: e.activation(cos_tm[:, ti * 32:(ti + 1) * 32], rts[1][:], AF.Sin),
                       r=(kp + "1",), w=(kp + "cos",))

        def rope_apply_tm(tile, tkey, cos_sl, sin_sl, ck, sk, nh):
            tv = tile[:].rearrange("p (g c) -> p g c", c=16)
            x1 = tv[:, 0:4 * nh:4, :]
            x2 = tv[:, 1:4 * nh:4, :]
            cs = cos_sl.rearrange("p (g c) -> p g c", c=16)[:, 0:nh, :]
            sn = sin_sl.rearrange("p (g c) -> p g c", c=16)[:, 0:nh, :]
            rp = [rp3[i][:].rearrange("p (g c) -> p g c", c=16)[:, 0:nh, :]
                  for i in range(3)]
            em.dve(lambda e: e.tensor_tensor(rp[0], x1, cs, op=ALU.mult),
                   r=(tkey, ck), w=("rp0",))
            em.dve(lambda e: e.tensor_tensor(rp[1], x2, sn, op=ALU.mult),
                   r=(tkey, sk), w=("rp1",))
            em.dve(lambda e: e.tensor_tensor(rp[2], x1, sn, op=ALU.mult),
                   r=(tkey, sk), w=("rp2",))
            em.dve(lambda e: e.tensor_tensor(x1, rp[0], rp[1], op=ALU.subtract),
                   r=("rp0", "rp1"), w=(tkey,))
            em.dve(lambda e: e.tensor_tensor(rp[1], x2, cs, op=ALU.mult),
                   r=(tkey, ck), w=("rp1",))
            em.dve(lambda e: e.tensor_tensor(x2, rp[2], rp[1], op=ALU.add),
                   r=("rp2", "rp1"), w=(tkey,))

        bg_queue = []

        def drain_bg(k):
            for _ in range(min(k, len(bg_queue))):
                bg_queue.pop(0)()

        rope_done = [False]

        for b in range(NB):
            xn_scr_key = f"xnscr{b}"
            scA = ExitStack()
            xn_fm = S(f"xn_fm{b}", (128, NCH * T), BF16, scA)
            r_mask = S(f"r_mask{b}", (8, T), BF16, scA)
            r_grid = S(f"r_grid{b}", (8, T), F32, scA)
            wbuf = [S(f"wbuf{b}_{i}", (128, 3072), BF16, scA) for i in range(2)]
            q_pair = [S(f"q_pair{b}_{i}", (128, T), BF16, scA) for i in range(2)]
            k_pair = [S(f"k_pair{b}_{i}", (128, T), BF16, scA) for i in range(2)]
            v_pair = [S(f"v_pair{b}_{i}", (128, NT * 192), BF16, scA) for i in range(2)]
            qsq = [S(f"qsq{b}_{i}", (128, T), BF16, scA) for i in range(2)]
            pt3 = [S(f"pt{b}_{i}", (128, 1024), BF16, scA) for i in range(4)]

            for sl_ in range(2):
                em.pool(lambda e, vp=v_pair[sl_]: e.memset(
                    vp[:].rearrange("p (k s) -> p k s", s=192)[:, :, 64:128], 1.0),
                    w=(f"v_ones{sl_}",), tag="vones")

            if en_dense and not rope_done[0]:
                rope_tables_tm(cos_tm, sin_tm, lambda ti: c_tok[:, ti:ti + 1],
                               ("c_tok",), NT, "rtk")
                rope_done[0] = True

            fq = []

            def drain_front(k):
                for _ in range(min(k, len(fq))):
                    fq.pop(0)()

            evq = []
            ev_par = [0]

            def drain_ev(k):
                for _ in range(min(k, len(evq))):
                    evq.pop(0)()

            fr_ctr = [0]

            def fpm():
                fr_ctr[0] += 1
                return (pm1, "pm1") if fr_ctr[0] % 2 == 0 else (pm2, "pm2")

            def front_closures(branch, p, sl):
                cl = []
                wb, wbkey = wbuf[sl], f"wbuf{sl}"
                qp, qk = q_pair[sl], f"q_pair{sl}"
                kp_, kk = k_pair[sl], f"k_pair{sl}"
                vp = v_pair[sl]
                qs, qsk = qsq[sl], f"qsq{sl}"

                if branch == "attn":
                    def add_w():
                        for wi, (dsrc, off) in enumerate(((d_wqT, 0), (d_wkT, 1024), (d_wvT, 2048))):
                            ap = dsrc[:, :].rearrange("p (c e) -> p c e", c=8)[:, :, p * 128:(p + 1) * 128]
                            em.dma_pool(lambda e, ap=ap, off=off, wb=wb: e.dma_start(
                                wb[:, off:off + 1024].rearrange("p (c e) -> p c e", c=8), ap),
                                r=(), w=(wbkey,), sem=f"dw{sl}", tag=f"ldw.a{p}.{wi}")
                    cl.append(add_w)

                    def wq_sl(c, wb=wb):
                        return wb[:, c * 128:(c + 1) * 128]

                    def wk_sl(c, wb=wb):
                        return wb[:, 1024 + c * 128:1024 + (c + 1) * 128]

                    def wv_sl(c, wb=wb):
                        return wb[:, 2048 + c * 128:2048 + (c + 1) * 128]
                else:
                    def add_w():
                        ap = d_dwqkv[:, :].rearrange("p (c e) -> p c e", c=8)[:, :, p * 384:(p + 1) * 384]
                        em.dma_pool(lambda e, ap=ap, wb=wb: e.dma_start(
                            wb[:, 0:3072].rearrange("p (c e) -> p c e", c=8), ap),
                            r=(), w=(wbkey,), sem=f"dw{sl}", tag=f"ldw.d{p}")
                    cl.append(add_w)

                    def wq_sl(c, wb=wb):
                        return wb[:, c * 384:c * 384 + 128]

                    def wk_sl(c, wb=wb):
                        return wb[:, c * 384 + 128:c * 384 + 256]

                    def wv_sl(c, wb=wb):
                        return wb[:, c * 384 + 256:c * 384 + 384]

                if branch == "attn":
                    for ns in range(4):
                        for (dst, dkey, wsl) in ((qp, qk, wq_sl), (kp_, kk, wk_sl)):
                            def projqk(dst=dst, dkey=dkey, wsl=wsl, ns=ns):
                                pm, pmk = fpm()
                                xnk = tuple(f"xnfm.{t4}.{g}" for t4 in range(ns * 4, ns * 4 + 4)
                                            for g in range(2))
                                for c in range(NCH):
                                    em.pe(lambda e, c=c, ns=ns, wsl=wsl, xn_fm=xn_fm, pm=pm: e.matmul(
                                        pm[:, 0:512], wsl(c), xn_fm[:, c * T + ns * 512:c * T + (ns + 1) * 512],
                                        start=(c == 0), stop=(c == NCH - 1)),
                                        r=(wbkey,) + xnk, w=(pmk,), tag="prj")
                                em.act(lambda e, ns=ns, dst=dst, pm=pm: e.activation(
                                    dst[:, ns * 512:(ns + 1) * 512], pm[:, 0:512], AF.Copy),
                                    r=(pmk,), w=(dkey,))
                            cl.append(projqk)
                else:
                    pend_b = [None]
                    for tt in range(NT):
                        def projtm_a(tt=tt, pend_b=pend_b):
                            pm, pmk = fpm()
                            xnk = (f"xnfm.{tt}.0", f"xnfm.{tt}.1")
                            for c in range(NCH):
                                em.pe(lambda e, c=c, tt=tt, wb=wb, xn_fm=xn_fm, pm=pm: e.matmul(
                                    pm[:, 0:384], xn_fm[:, c * T + tt * 128:c * T + (tt + 1) * 128],
                                    wb[:, c * 384:(c + 1) * 384],
                                    start=(c == 0), stop=(c == NCH - 1)),
                                    r=(wbkey,) + xnk, w=(pmk,), tag="prjtm")
                            bufs = []
                            for qi, (dst, dkey) in enumerate(((qp, qk), (kp_, kk))):
                                qb, qbk = qtmb[qi * 2 + tt % 2], f"qtmb{qi * 2 + tt % 2}"
                                em.act(lambda e, qb=qb, pm=pm, qi=qi: e.activation(
                                    qb[:], pm[:, qi * 128:(qi + 1) * 128], AF.Copy),
                                    r=(pmk,), w=(qbk,))
                                rope_apply_tm(qb, qbk,
                                              cos_tm[:, tt * 32:(tt + 1) * 32],
                                              sin_tm[:, tt * 32:(tt + 1) * 32],
                                              "rtkcos", "rtksin", 2)
                                bufs.append((qi, qb, qbk, dst, dkey))
                            em.dve(lambda e, tt=tt, vp=vp, pm=pm: e.tensor_copy(
                                vp[:, tt * 192:(tt + 1) * 192].rearrange(
                                    "p (g c) -> p g c", c=64)[:, 0:3:2, :],
                                pm[:, 256:384].rearrange("p (g c) -> p g c", c=64)),
                                r=(pmk,), w=(f"v_pair{sl}.{tt}",))

                            def projtm_b(bufs=bufs, tt=tt):
                                for qi, qb, qbk, dst, dkey in bufs:
                                    em.pe(lambda e, qb=qb, qi=qi, tt=tt: e.transpose(
                                        ptr[:, qi * 256 + (tt % 2) * 128:qi * 256 + (tt % 2) * 128 + 128],
                                        qb[:], c_id[:]),
                                        r=(qbk, "c_id"), w=("ptr",), tag="trQ")
                                if tt % 2 == 1:
                                    for qi, qb, qbk, dst, dkey in bufs:
                                        em.act(lambda e, qi=qi, tt=tt, dst=dst: e.activation(
                                            dst[:, (tt - 1) * 128:(tt + 1) * 128],
                                            ptr[:, qi * 256:qi * 256 + 256], AF.Copy),
                                            r=("ptr",), w=(dkey,))
                            pend_b[0] = projtm_b

                        def projtm_ab(tt=tt, pend_b=pend_b, projtm_a=projtm_a):
                            bprev = pend_b[0]
                            projtm_a()
                            if bprev is not None:
                                bprev()
                        cl.append(projtm_ab)

                    def projtm_flush(pend_b=pend_b):
                        if pend_b[0] is not None:
                            pend_b[0]()
                            pend_b[0] = None
                    cl.append(projtm_flush)
                if branch == "attn":
                    for tt in range(NT):
                        def projv(tt=tt, wsl=wv_sl, vp=vp):
                            pm, pmk = fpm()
                            xnk = (f"xnfm.{tt}.0", f"xnfm.{tt}.1")
                            for c in range(NCH):
                                em.pe(lambda e, c=c, tt=tt, wsl=wsl, xn_fm=xn_fm, pm=pm: e.matmul(
                                    pm[:, 0:128], xn_fm[:, c * T + tt * 128:c * T + (tt + 1) * 128], wsl(c),
                                    start=(c == 0), stop=(c == NCH - 1)),
                                    r=(wbkey,) + xnk, w=(pmk,), tag="prj.v")
                            em.dve(lambda e, tt=tt, vp=vp, pm=pm: e.tensor_copy(
                                vp[:, tt * 192:(tt + 1) * 192].rearrange(
                                    "p (g c) -> p g c", c=64)[:, 0:3:2, :],
                                pm[:, 0:128].rearrange("p (g c) -> p g c", c=64)),
                                r=(pmk,), w=(f"v_pair{sl}.{tt}",))
                        cl.append(projv)

                def mk_msq(src_t, skey):
                    def msq(src_t=src_t, skey=skey):
                        em.act(lambda e, src_t=src_t, qs=qs: e.activation(qs[:], src_t[:], AF.Square),
                               r=(skey,), w=(qsk,))
                    return msq

                def mk_mred(col):
                    def mred(col=col):
                        for ns in range(4):
                            pm, pmk = fpm()
                            em.pe(lambda e, ns=ns, qs=qs, pm=pm: e.matmul(
                                pm[0:2, 0:512], c_bo[:], qs[:, ns * 512:(ns + 1) * 512],
                                start=True, stop=True),
                                r=("c_bo", qsk), w=(pmk,))
                            em.dve(lambda e, ns=ns, col=col, pm=pm: e.tensor_reduce(
                                nrm_s[0:2, col + ns:col + ns + 1], pm[0:2, 0:512], axis=AXX, op=ALU.max),
                                r=(pmk,), w=(f"nrm{col + ns}",))
                    return mred

                def mnorm_fin():
                    em.dve(lambda e: e.tensor_reduce(vtr_a[0:2, 2:3], nrm_s[0:2, 0:4], axis=AXX, op=ALU.max),
                           r=("nrm0", "nrm1", "nrm2", "nrm3"), w=("nq",))
                    em.dve(lambda e: e.tensor_reduce(vtr_a[0:2, 3:4], nrm_s[0:2, 4:8], axis=AXX, op=ALU.max),
                           r=("nrm4", "nrm5", "nrm6", "nrm7"), w=("nk",))
                    em.act(lambda e: e.activation(vtr_a[0:2, 4:5], vtr_a[0:2, 2:3], AF.Ln),
                           r=("nq",), w=("lnq",))
                    em.act(lambda e: e.activation(vtr_a[0:2, 5:6], vtr_a[0:2, 3:4], AF.Ln),
                           r=("nk",), w=("lnk",))
                    em.dve(lambda e: e.tensor_tensor(vtr_a[0:2, 6:7], vtr_a[0:2, 4:5], vtr_a[0:2, 5:6], op=ALU.add),
                           r=("lnq", "lnk"), w=("lnsum",))
                    em.act(lambda e: e.activation(vtr_a[0:2, 0:1], vtr_a[0:2, 6:7], AF.Exp, bias=0.0, scale=0.5),
                           r=("lnsum",), w=("vtra",))
                    em.dve(lambda e: e.tensor_scalar_mul(vtr_a[0:2, 0:1], vtr_a[0:2, 0:1], -0.125),
                           r=("vtra",), w=("vtra",))
                    em.dve(lambda e: e.transpose(vtr_b[:], vtr_a[:]), r=("vtra",), w=("vtrb",))
                    em.dve(lambda e: e.tensor_reduce(vtr_b[0:1, 2:3], vtr_b[0:1, 0:2],
                                                     axis=AXX, op=ALU.min),
                           r=("vtrb",), w=("vtrb",))

                def mnorm_bc():
                    em.pe(lambda e: e.matmul(pm1[:, 0:1], c_or[:], vtr_b[0:1, 2:3],
                                             start=True, stop=True),
                          r=("vtrb", "c_or"), w=("pm1",), tag="bcM")
                    em.act(lambda e: e.activation(bias_hd[sl][0][:], pm1[:, 0:1], AF.Copy),
                           r=("pm1",), w=(f"bias{sl}.0",))
                cl.append(mk_msq(qp, qk))
                cl.append(mk_mred(0))
                cl.append(mk_msq(kp_, kk))
                cl.append(mk_mred(4))
                cl.append(mnorm_fin)
                cl.append(mnorm_bc)
                return cl

            seq = []
            if en_attn:
                seq += [("attn", p, av_attn[p], f"ava{p}") for p in range(8)]
            if en_dense:
                seq += [("dense", p, av_dense[p], f"avd{p}") for p in range(4)]
            if seq:
                fq.extend(front_closures(seq[0][0], seq[0][1], 0))

            for tt in range(NT):
                xb, xkey = xt[tt % 2], f"xt{tt % 2}"
                xnb, xnkey = xnt[tt % 2], f"xnt{tt % 2}"
                src = x_in[b, tt * 128:(tt + 1) * 128, :]
                em.dma_sp(lambda e, d=xb, s=src: e.dma_start(d[:], s), r=(), w=(xkey,),
                          sem=f"dx{tt % 2}", tag=f"ldx{b}.{tt}")
                o = (tt % 2) * 8
                pk = str(tt % 2)
                em.dve(lambda e, xb=xb, o=o: e.tensor_reduce(ln_s[:, o:o + 1], xb[:], axis=AXX, op=ALU.add),
                       r=(xkey,), w=("ln0" + pk,))
                em.act(lambda e, xb=xb, o=o, tt=tt: e.activation(
                    junk2[:, (tt % 2) * H:(tt % 2) * H + H], xb[:], AF.Square, accum_out=ln_s[:, o + 1:o + 2]),
                    r=(xkey,), w=("ln1" + pk, "junk2." + pk))
                em.dve(lambda e, o=o: e.tensor_scalar_mul(ln_s[:, o + 2:o + 3], ln_s[:, o:o + 1], 1.0 / H),
                       r=("ln0" + pk,), w=("ln2" + pk,))
                em.dve(lambda e, o=o: e.tensor_tensor(ln_s[:, o + 3:o + 4], ln_s[:, o + 2:o + 3],
                                                      ln_s[:, o + 2:o + 3], op=ALU.mult),
                       r=("ln2" + pk,), w=("ln3" + pk,))
                em.dve(lambda e, o=o: e.scalar_tensor_tensor(ln_s[:, o + 4:o + 5], ln_s[:, o + 1:o + 2],
                                                             1.0 / H, ln_s[:, o + 3:o + 4],
                                                             op0=ALU.mult, op1=ALU.subtract),
                       r=("ln1" + pk, "ln3" + pk), w=("ln4" + pk,))
                em.act(lambda e, o=o: e.activation(ln_s[:, o + 5:o + 6], ln_s[:, o + 4:o + 5],
                                                   AF.Ln, bias=z_eps[:], scale=1.0),
                       r=("ln4" + pk, "z_eps"), w=("ln5" + pk,))
                em.act(lambda e, o=o: e.activation(ln_s[:, o + 6:o + 7], ln_s[:, o + 5:o + 6],
                                                   AF.Exp, bias=0.0, scale=-0.5),
                       r=("ln5" + pk,), w=("ln6" + pk,))
                em.dve(lambda e, xb=xb, xnb=xnb, o=o: e.scalar_tensor_tensor(
                    xnb[:], xb[:], ln_s[:, o + 2:o + 3], ln_s[:, o + 6:o + 7].to_broadcast([128, H]),
                    op0=ALU.subtract, op1=ALU.mult),
                    r=(xkey, "ln2" + pk, "ln6" + pk), w=(xnkey,))
                em.dma_sp(lambda e, xnb=xnb, d=xn_scr[b, tt * 128:(tt + 1) * 128, :]: e.dma_start(d, xnb[:]),
                          r=(xnkey,), w=(xn_scr_key + f".{tt}",), sem="dxn", tag=f"stxn{b}.{tt}")
                for g in range(2):
                    for c4 in range(4):
                        c = g * 4 + c4
                        em.pe(lambda e, xnb=xnb, c=c, c4=c4: e.transpose(
                            ptr[:, c4 * 128:(c4 + 1) * 128], xnb[:, c * 128:(c + 1) * 128], c_id[:]),
                            r=(xnkey, "c_id"), w=("ptr",))
                    em.dve(lambda e, g=g, tt=tt, xn_fm=xn_fm: e.tensor_copy(
                        xn_fm[:].rearrange("p (c t) -> p c t", c=NCH)[:, g * 4:(g + 1) * 4, tt * 128:(tt + 1) * 128],
                        ptr[:].rearrange("p (c t) -> p c t", c=4)),
                        r=("ptr",), w=(f"xnfm.{tt}.{g}",))
                if tt == 0:
                    drain_front(1)
                elif tt % 4 == 3:
                    drain_front(2)
            xn_keys = tuple(f"xnfm.{tt}.{g}" for tt in range(NT) for g in range(2))

            if en_mosa:
                for ns in range(4):
                    for c in range(NCH):
                        em.pe(lambda e, ns=ns, c=c, xn_fm=xn_fm: e.matmul(
                            pm1[0:8, 0:512], c_router[:, c * 8:(c + 1) * 8],
                            xn_fm[:, c * T + ns * 512:c * T + (ns + 1) * 512],
                            start=(c == 0), stop=(c == NCH - 1)),
                            r=("c_router",) + xn_keys, w=("pm1",), tag="router")
                    em.act(lambda e, ns=ns: e.activation(
                        r_scores[:, ns * 512:(ns + 1) * 512], pm1[0:8, 0:512], AF.Copy),
                        r=("pm1",), w=("r_scores",), tag="rsc")

                def bis_init():
                    em.dve(lambda e: e.tensor_reduce(b_hi[0][:], r_scores[:], axis=AXX, op=ALU.max),
                           r=("r_scores",), w=("b_hi0",))
                    em.dve(lambda e: e.tensor_reduce(b_lo[0][:], r_scores[:], axis=AXX, op=ALU.min),
                           r=("b_lo0", "r_scores"), w=("b_lo0",))
                    em.dve(lambda e: e.tensor_scalar_add(b_lo[0][:], b_lo[0][:], -1.0),
                           r=("b_lo0",), w=("b_lo0",))
                    em.dve(lambda e: e.tensor_scalar(b_mid[:], b_lo[0][:], b_hi[0][:], 0.5,
                                                     op0=ALU.add, op1=ALU.mult),
                           r=("b_lo0", "b_hi0"), w=("b_mid",))
                bg_queue.append(bis_init)

                def mk_iter(i):
                    cur, nxt = i % 2, (i + 1) % 2
                    cls = []
                    for cc in range(4):
                        cls.append(lambda cc=cc: em.dve(lambda e, cc=cc: e.tensor_scalar(
                            junk2[0:8, cc * 512:(cc + 1) * 512], r_scores[:, cc * 512:(cc + 1) * 512],
                            b_mid[:], 0.0, op0=ALU.is_gt, op1=ALU.add,
                            accum_out=b_c4[:, cc:cc + 1]),
                            r=("r_scores", "b_mid"), w=("junk2.0", "junk2.1", "b_c4")))

                    def tail():
                        em.dve(lambda e: e.tensor_reduce(b_cnt[:], b_c4[:], axis=AXX, op=ALU.add),
                               r=("b_c4",), w=("b_cnt",))
                        em.dve(lambda e: e.tensor_single_scalar(b_cond[:], b_cnt[:], float(KSEL), op=ALU.is_ge),
                               r=("b_cnt",), w=("b_cond",))
                        em.dve(lambda e: e.select(b_lo[nxt][:], b_cond[:], b_mid[:], b_lo[cur][:]),
                               r=("b_cond", "b_mid", f"b_lo{cur}"), w=(f"b_lo{nxt}",))
                        em.dve(lambda e: e.select(b_hi[nxt][:], b_cond[:], b_hi[cur][:], b_mid[:]),
                               r=("b_cond", "b_mid", f"b_hi{cur}"), w=(f"b_hi{nxt}",))
                        em.dve(lambda e: e.tensor_scalar(b_mid[:], b_lo[nxt][:], b_hi[nxt][:], 0.5,
                                                         op0=ALU.add, op1=ALU.mult),
                               r=(f"b_lo{nxt}", f"b_hi{nxt}"), w=("b_mid",))
                    cls.append(tail)
                    return cls
                for i in range(N_BISECT):
                    bg_queue.extend(mk_iter(i))
                lo_fin = f"b_lo{N_BISECT % 2}"

                def bis_post(r_mask=r_mask, r_grid=r_grid):
                    em.dve(lambda e: e.tensor_single_scalar(r_mask[:], r_scores[:],
                                                            b_lo[N_BISECT % 2][:], op=ALU.is_gt),
                           r=("r_scores", lo_fin), w=("r_mask",))
                    em.dve(lambda e: e.tensor_tensor_scan(r_grid[:], r_mask[:],
                                                          z_one[0:8, :].to_broadcast([8, T]),
                                                          0.0, op0=ALU.add, op1=ALU.mult),
                           r=("r_mask", "z_one"), w=("r_grid",))
                    em.dve(lambda e: e.scalar_tensor_tensor(r_grid[:], r_grid[:], 1.0, r_mask[:],
                                                            op0=ALU.mult, op1=ALU.mult),
                           r=("r_grid", "r_mask"), w=("r_grid",))
                bg_queue.append(bis_post)


            def emit_attention(branch, p, sl, av_tile, avkey):
                qp, qk = q_pair[sl], f"q_pair{sl}"
                kp_, kk = k_pair[sl], f"k_pair{sl}"
                vp = v_pair[sl]
                sct = [0]
                for w in range(4):
                    w0 = 512 * w
                    w1 = w0 + 512
                    kts = [kt for kt in range(NT) if kt * 128 < w1]
                    klast = kts[-1]

                    def emit_S_pair(idx, kt, w0=w0, w1=w1):
                        qstart = max(kt * 128, w0)
                        cw = w1 - qstart
                        slab, slk = pt3[idx % 4], f"pt{idx % 4}"
                        pss = []
                        for hb in (0, 64):
                            bi = sct[0] % 3
                            ps, psk = pS2[bi], f"pS{bi}"
                            sct[0] += 1
                            pss.append((hb, bi, ps, psk))
                            em.pe(lambda e, ps=ps, cw=cw, kt=kt, hb=hb, qa=qstart: e.matmul(
                                ps[:, 0:cw],
                                kp_[hb:hb + 64, kt * 128:(kt + 1) * 128],
                                qp[hb:hb + 64, qa:qa + cw],
                                start=True, stop=True, tile_position=(hb, 0)),
                                r=(kk, qk), w=(psk,), tag="S")
                        for hb, bi, ps, psk in pss:
                            so = (hb // 64) * 512
                            em.act(lambda e, slab=slab, ps=ps, cw=cw, so=so: e.activation(
                                slab[:, so:so + cw], ps[:, 0:cw], AF.Exp,
                                bias=bias_hd[sl][0][:], scale=0.125),
                                r=(psk, f"bias{sl}.0"),
                                w=(slk + ("a" if hb == 0 else "b"),), tag="exp")
                        if kt * 128 >= w0:
                            for hb in (0, 64):
                                so = (hb // 64) * 512
                                sk2 = slk + ("a" if hb == 0 else "b")
                                em.dve(lambda e, slab=slab, so=so: e.tensor_tensor(
                                    slab[:, so:so + 128], slab[:, so:so + 128],
                                    c_um[:], op=ALU.mult),
                                    r=(sk2, "c_um"), w=(sk2,), tag="dmask")

                    def emit_AV_pair(idx, kt, w0=w0, klast=klast):
                        qstart = max(kt * 128, w0)
                        cw = 512 - (qstart - w0)
                        co = qstart - w0
                        slab, slk = pt3[idx % 4], f"pt{idx % 4}"
                        for hb in (0, 64):
                            so = (hb // 64) * 512
                            vsl = vp[:, kt * 192 + hb:kt * 192 + hb + 128]
                            em.pe(lambda e, vsl=vsl, slab=slab, kt=kt, cw=cw, co=co,
                                  so=so, hb=hb: e.matmul(
                                pav[0:128, (hb // 64) * 512 + co:(hb // 64) * 512 + co + cw],
                                vsl, slab[:, so:so + cw],
                                start=(kt == 0), stop=(kt == klast)),
                                r=(slk + ("a" if hb == 0 else "b"),
                                   f"v_pair{sl}.{kt}", f"v_ones{sl}"),
                                w=(f"pavb{hb // 64}",), tag="AV")

                    def push_evac(hb, w0=w0):
                        pk = f"pavb{hb // 64}"
                        co = (hb // 64) * 512
                        pr = ev_par[0] % 2
                        ev_par[0] += 1
                        rbb, rbk = rb_sb[pr], f"rbsb{pr}"
                        lb = 64 - hb
                        evq.append(lambda: em.act(lambda e: e.activation(
                            rbb[:, 0:512], pav[lb:lb + 64, co:co + 512], AF.Ln),
                            r=(pk,), w=(rbk,), tag="lln"))
                        evq.append(lambda: em.act(lambda e: e.activation(
                            rbb[:, 0:512], rbb[:, 0:512], AF.Exp, bias=0.0, scale=-1.0),
                            r=(rbk,), w=(rbk,), tag="lexp"))

                        def ev_rest():
                            em.dve(lambda e: e.tensor_tensor(
                                av_tile[hb:hb + 64, w0:w0 + 512],
                                pav[hb:hb + 64, co:co + 512], rbb[:, 0:512], op=ALU.mult),
                                r=(pk, rbk), w=(avkey,), tag="avn")
                        evq.append(ev_rest)

                    SK = 3
                    for idx, kt in enumerate(kts):
                        emit_S_pair(idx, kt)
                        drain_ev(1)
                        drain_front(1)
                        drain_bg(1)
                        if idx >= SK:
                            emit_AV_pair(idx - SK, kts[idx - SK])
                            drain_ev(1)
                            drain_front(1)
                    for idx in range(max(0, len(kts) - SK), len(kts)):
                        emit_AV_pair(idx, kts[idx])
                        drain_ev(1)
                        drain_front(1)
                        drain_bg(1)
                    for hb in (0, 64):
                        push_evac(hb)
                    while evq:
                        drain_ev(1)
                        drain_front(1)

            for i_, (br, p, avt, avk) in enumerate(seq):
                sl = i_ % 2
                if i_ == 0:
                    drain_front(10 ** 9)
                if i_ + 1 < len(seq):
                    fq.extend(front_closures(seq[i_ + 1][0], seq[i_ + 1][1], (i_ + 1) % 2))
                emit_attention(br, p, sl, avt, avk)
                drain_front(10 ** 9)
            drain_bg(10 ** 9)

            if en_mosa:
                for tt in range(NT):
                    em.pe(lambda e, tt=tt, r_grid=r_grid: e.transpose(
                        pav[:, tt * 8:tt * 8 + 8], r_grid[:, tt * 128:(tt + 1) * 128], c_idf[0:8, 0:8]),
                        r=("r_grid", "c_idf"), w=("pavb0",), tag="trG")
                em.dve(lambda e: e.tensor_copy(g_tm[:], pav[:, 0:128]),
                       r=("pavb0",), w=("g_tm",))
                for tt in range(NT):
                    em.pe(lambda e, tt=tt: e.transpose(
                        pm1[:, tt * 8:tt * 8 + 8], r_scores[:, tt * 128:(tt + 1) * 128], c_idf[0:8, 0:8]),
                        r=("r_scores", "c_idf"), w=("pm1",), tag="trS")
                em.dve(lambda e: e.tensor_copy(
                    sc_tm[:].rearrange("p (k n) -> p k n", n=9)[:, :, 1:9],
                    pm1[:, 0:128].rearrange("p (k n) -> p k n", n=8)),
                    r=("pm1",), w=("sc_tm",))
                em.pool(lambda e: e.tensor_copy(
                    sc_tm[:].rearrange("p (k n) -> p k n", n=9)[:, :, 0:1],
                    c_tok[:].rearrange("p (k one) -> p k one", one=1)),
                    r=("c_tok",), w=("sc_tm",))

            em.barrier()
            scA.close()

            scB = ExitStack()
            c_iotar = S(f"c_iotar{b}", (128, T), F32, scB)
            av_mosa = [S(f"av_mosa{b}_{p}", (128, T), BF16, scB) for p in range(4)]
            wo_all = S(f"wo_all{b}", (128, 8192), BF16, scB)
            dwo_all = S(f"dwo_all{b}", (128, 4096), BF16, scB)
            mwo_all = S(f"mwo_all{b}", (128, 4096), BF16, scB)
            em.dma_sp(lambda e, dd=c_iotar[:], ss=d_iota_row[:, :]: e.dma_start(dd, ss),
                      r=(), w=("c_iotar",), sem="dwo", tag="ld:c_iotar")
            for (dst_, src_, k_) in ((wo_all, d_woT, "wo_all"), (dwo_all, d_dwo, "dwo_all"),
                                     (mwo_all, d_mwo, "mwo_all")):
                em.dma_pool(lambda e, dd=dst_[:], ss=src_[:, :]: e.dma_start(dd, ss),
                            r=(), w=(k_,), sem="dwo", tag="ld:" + k_)
            em.fence_keys(("c_iotar", "wo_all", "dwo_all", "mwo_all"))
            if en_mosa:
                wbufm = [S(f"wbm{b}_{i}", (128, 1536), BF16, scB) for i in range(2)]
                oh_t = [S(f"oh_t{b}_{i}", (128, KSEL), F32, scB) for i in range(2)]
                idxsc8 = S(f"idxsc8{b}", (128, 32), F32, scB)
                cmT = S(f"cmT{b}", (2, 256), F32, scB)
                idx_u = [S(f"idx_u{b}_{i}", (128, 2), U32, scB) for i in range(2)]
                gate_c = S(f"gate_c{b}", (128, 2), F32, scB)
                gate_e = S(f"gate_e{b}", (128, 2), F32, scB)
                xg = [[S(f"xg{b}_{pp}_{i}", (128, H), BF16, scB) for i in range(2)]
                      for pp in range(2)]
                xg_fm = S(f"xg_fm{b}", (128, NCH * KSEL), BF16, scB)
                mq8 = S(f"mq8{b}", (64, 8 * KSEL), BF16, scB)
                mk8 = S(f"mk8{b}", (64, 8 * KSEL), BF16, scB)
                mv8 = S(f"mv8{b}", (128, 8 * 130), BF16, scB)
                msq = S(f"msq{b}", (64, KSEL), BF16, scB)
                mpt = S(f"mpt{b}", (128, 3 * 128), BF16, scB)
                mavtm = S(f"mavtm{b}", (128, 2 * D), BF16, scB)
                bias_m = S(f"bias_m{b}", (128, 1), F32, scB)
                oh_rt = [S(f"oh_rt{b}_{i}", (128, 512), BF16, scB) for i in range(2)]

                em.pool(lambda e, mv8=mv8: e.memset(
                    mv8[:].rearrange("p (h s) -> p h s", s=130)[:, :, 64:130:65], 1.0),
                    w=("mv_ones",), tag="mvones")

                def m_stage1(hh):
                    par = hh % 2
                    wb, wbkey = wbufm[par], f"wbm{par}"
                    ap = d_mwqkv[:, :].rearrange("p (c e) -> p c e", c=8)[:, :, hh * 192:(hh + 1) * 192]
                    em.dma_pool(lambda e, ap=ap, wb=wb: e.dma_start(
                        wb[:, 0:1536].rearrange("p (c e) -> p c e", c=8), ap),
                        r=(), w=(wbkey,), sem=f"dw{par}", tag=f"ldmw{hh}")
                    for tt in range(NT):
                        oh, ohk = oh_t[tt % 2], f"oh{tt % 2}"
                        em.dve(lambda e, oh=oh, tt=tt, hh=hh: e.tensor_single_scalar(
                            oh[:], c_iota1[:], g_tm[:, tt * 8 + hh:tt * 8 + hh + 1], op=ALU.is_equal),
                            r=("c_iota1", "g_tm"), w=(ohk,))
                        em.pe(lambda e, oh=oh, tt=tt, hh=hh: e.matmul(
                            pm1[0:2, 0:256],
                            sc_tm[:, tt * 9:tt * 9 + hh + 2:hh + 1],
                            oh[:, 0:256],
                            start=(tt == 0), stop=(tt == NT - 1)),
                            r=(ohk, "sc_tm"), w=("pm1",), tag="cmm")
                    em.act(lambda e: e.activation(cmT[:, 0:256], pm1[0:2, 0:256], AF.Copy),
                           r=("pm1",), w=("cmT",))
                    for half in range(2):
                        em.pe(lambda e, half=half: e.transpose(
                            pm2[0:128, half * 2:half * 2 + 2],
                            cmT[:, half * 128:(half + 1) * 128], c_idf[0:2, 0:2]),
                            r=("cmT", "c_idf"), w=("pm2",), tag="cmtr")
                    em.act(lambda e, hh=hh, idxsc8=idxsc8: e.activation(
                        idxsc8[:, hh * 4:hh * 4 + 4], pm2[0:128, 0:4], AF.Copy),
                        r=("pm2",), w=(f"idxsc.{hh}",))
                    em.dve(lambda e, par=par, hh=hh, idxsc8=idxsc8: e.tensor_copy(
                        idx_u[par][:], idxsc8[:, hh * 4:hh * 4 + 4:2]),
                        r=(f"idxsc.{hh}",), w=(f"idx_u{par}",))
                    for rs in range(2):
                        em.dma_pool(lambda e, rs=rs, b=b, par=par: e.indirect_dma_start(
                            xg[par][rs][:], None,
                            xn_scr[:, :, :].rearrange("b t h -> (b t) h"),
                            bass.IndirectOffsetOnAxis(ap=idx_u[par][:, rs:rs + 1], axis=0),
                            element_offset=b * T * H),
                            r=(f"idx_u{par}",) + tuple(xn_scr_key + f".{tt}" for tt in range(NT)),
                            w=(f"xg{par}{rs}",), sem=f"dg{rs}", tag=f"gather{hh}.{rs}")

                def m_stage2(hh):
                    par = hh % 2
                    wb, wbkey = wbufm[par], f"wbm{par}"
                    mpb = [0]

                    def mpav():
                        mpb[0] += 1
                        return ((pav[:, 0:64], "pavb0") if mpb[0] % 2
                                else (pav[:, 512:576], "pavb1"))
                    for rs in range(2):
                        for cg in range(2):
                            for c4 in range(4):
                                c = cg * 4 + c4
                                em.pe(lambda e, rs=rs, c=c, c4=c4, par=par: e.transpose(
                                    ptr[:, c4 * 128:(c4 + 1) * 128], xg[par][rs][:, c * 128:(c + 1) * 128], c_id[:]),
                                    r=(f"xg{par}{rs}", "c_id"), w=("ptr",), tag="trXg")
                            em.dve(lambda e, rs=rs, cg=cg, xg_fm=xg_fm: e.tensor_copy(
                                xg_fm[:].rearrange("p (c r) -> p c r", c=NCH)[:, cg * 4:(cg + 1) * 4, rs * 128:(rs + 1) * 128],
                                ptr[:].rearrange("p (c t) -> p c t", c=4)),
                                r=("ptr",), w=("xg_fm",))
                    rope_tables_tm(mcos_tm, msin_tm,
                                   lambda ti, hh=hh, idxsc8=idxsc8: idxsc8[:, hh * 4 + ti * 2:hh * 4 + ti * 2 + 1],
                                   (f"idxsc.{hh}",), 2, "mrtk")
                    for (dst8, dk8, woff, ck, sk) in (
                            (mq8, f"mq.{hh}", 0, "mrtkcos", "mrtksin"),
                            (mk8, f"mk.{hh}", 64, "mrtkcos", "mrtksin")):
                        for rs in range(2):
                            qb, qbk = qtmb[rs % 2], f"qtmb{rs % 2}"
                            mpv, mpk = mpav()
                            for c in range(NCH):
                                em.pe(lambda e, c=c, rs=rs, wb=wb, xg_fm=xg_fm, woff=woff, mpv=mpv: e.matmul(
                                    mpv[:, 0:64], xg_fm[:, c * KSEL + rs * 128:c * KSEL + (rs + 1) * 128],
                                    wb[:, c * 192 + woff:c * 192 + woff + 64],
                                    start=(c == 0), stop=(c == NCH - 1)),
                                    r=(wbkey, "xg_fm"), w=(mpk,), tag="mqk")
                            em.act(lambda e, qb=qb, mpv=mpv: e.activation(qb[:, 0:64], mpv[:, 0:64], AF.Copy),
                                   r=(mpk,), w=(qbk,))
                            rope_apply_tm(qb, qbk,
                                          mcos_tm[:, rs * 32:(rs + 1) * 32],
                                          msin_tm[:, rs * 32:(rs + 1) * 32], ck, sk, 1)
                            em.pe(lambda e, qb=qb, rs=rs: e.transpose(
                                ptr[0:64, rs * 128:(rs + 1) * 128], qb[:, 0:64], c_id[:]),
                                r=(qbk, "c_id"), w=("ptr",), tag="trmQ")
                        em.act(lambda e, dst8=dst8, hh=hh: e.activation(
                            dst8[:, hh * KSEL:(hh + 1) * KSEL], ptr[0:64, 0:KSEL], AF.Copy),
                            r=("ptr",), w=(dk8,))
                    for rs in range(2):
                        mpv, mpk = mpav()
                        for c in range(NCH):
                            em.pe(lambda e, c=c, rs=rs, wb=wb, xg_fm=xg_fm, mpv=mpv: e.matmul(
                                mpv[:, 0:64], xg_fm[:, c * KSEL + rs * 128:c * KSEL + (rs + 1) * 128],
                                wb[:, c * 192 + 128:c * 192 + 192],
                                start=(c == 0), stop=(c == NCH - 1)),
                                r=(wbkey, "xg_fm"), w=(mpk,), tag="mv")
                        em.act(lambda e, rs=rs, hh=hh, mv8=mv8, mpv=mpv: e.activation(
                            mv8[:, hh * 130 + rs * 65:hh * 130 + rs * 65 + 64], mpv[:, 0:64], AF.Copy),
                            r=(mpk,), w=(f"mv.{hh}.{rs}",))

                def m_stage3(hh):
                    mqv = mq8[:, hh * KSEL:(hh + 1) * KSEL]
                    mkv = mk8[:, hh * KSEL:(hh + 1) * KSEL]
                    em.act(lambda e, mqv=mqv, msq=msq: e.activation(msq[:], mqv, AF.Square),
                           r=(f"mq.{hh}",), w=("msq",))
                    em.pe(lambda e, msq=msq: e.matmul(pS2[2][0:1, 0:KSEL], c_bo[0:64, 0:1], msq[:],
                                                      start=True, stop=True),
                          r=("c_bo", "msq"), w=("pS2",), tag="mnq")
                    em.dve(lambda e: e.tensor_reduce(nrm_s[0:1, 0:1], pS2[2][0:1, 0:KSEL], axis=AXX, op=ALU.max),
                           r=("pS2",), w=("nq",))
                    em.act(lambda e, mkv=mkv, msq=msq: e.activation(msq[:], mkv, AF.Square),
                           r=(f"mk.{hh}",), w=("msq",))
                    em.pe(lambda e, msq=msq: e.matmul(pS2[2][0:1, 0:KSEL], c_bo[0:64, 0:1], msq[:],
                                                      start=True, stop=True),
                          r=("c_bo", "msq"), w=("pS2",), tag="mnk")
                    em.dve(lambda e: e.tensor_reduce(nrm_s[0:1, 4:5], pS2[2][0:1, 0:KSEL], axis=AXX, op=ALU.max),
                           r=("pS2",), w=("nk",))
                    em.act(lambda e: e.activation(nrm_s[0:1, 1:2], nrm_s[0:1, 0:1], AF.Ln),
                           r=("nq",), w=("lnq",))
                    em.act(lambda e: e.activation(nrm_s[0:1, 5:6], nrm_s[0:1, 4:5], AF.Ln),
                           r=("nk",), w=("lnk",))
                    em.dve(lambda e: e.tensor_tensor(nrm_s[0:1, 2:3], nrm_s[0:1, 1:2], nrm_s[0:1, 5:6], op=ALU.add),
                           r=("lnq", "lnk"), w=("lnsum",))
                    em.act(lambda e: e.activation(nrm_s[0:1, 3:4], nrm_s[0:1, 2:3], AF.Exp, bias=0.0, scale=0.5),
                           r=("lnsum",), w=("Mt",))
                    em.dve(lambda e: e.tensor_scalar_mul(nrm_s[0:1, 3:4], nrm_s[0:1, 3:4], -0.125),
                           r=("Mt",), w=("Mt",))
                    em.pe(lambda e: e.matmul(pS2[2][:, 0:1], c_or[:], nrm_s[0:1, 3:4], start=True, stop=True),
                          r=("Mt", "c_or"), w=("pS2",), tag="mbc")
                    em.act(lambda e, bias_m=bias_m: e.activation(bias_m[:], pS2[2][:, 0:1], AF.Copy),
                           r=("pS2",), w=("bias_m",))

                    tiles = [(0, 0), (0, 1), (1, 1)]
                    for ti, (kt, qt) in enumerate(tiles):
                        em.pe(lambda e, ti=ti, kt=kt, qt=qt, hh=hh, mq8=mq8, mk8=mk8: e.matmul(
                            pS2[0][:, ti * 128:(ti + 1) * 128],
                            mk8[:, hh * KSEL + kt * 128:hh * KSEL + (kt + 1) * 128],
                            mq8[:, hh * KSEL + qt * 128:hh * KSEL + (qt + 1) * 128],
                            start=True, stop=(kt != qt)),
                            r=(f"mk.{hh}", f"mq.{hh}"), w=("pS0",), tag="mS")
                        if kt == qt:
                            em.pe(lambda e, ti=ti: e.matmul(
                                pS2[0][:, ti * 128:(ti + 1) * 128], c_id[:], c_cm[:],
                                start=False, stop=True),
                                r=("c_id", "c_cm"), w=("pS0",), tag="mmask")
                    em.act(lambda e, mpt=mpt, bias_m=bias_m: e.activation(
                        mpt[:, 0:384], pS2[0][:, 0:384], AF.Exp, bias=bias_m[:], scale=0.125),
                        r=("pS0", "bias_m"), w=("mpt",), tag="mexp")
                    for ti, (kt, qt) in enumerate(tiles):
                        em.pe(lambda e, ti=ti, kt=kt, qt=qt, hh=hh, mpt=mpt, mv8=mv8: e.matmul(
                            pS2[1][:, qt * 65:qt * 65 + 65], mpt[:, ti * 128:(ti + 1) * 128],
                            mv8[:, hh * 130 + kt * 65:hh * 130 + kt * 65 + 65],
                            start=(kt == 0), stop=(kt == qt)),
                            r=("mpt", f"mv.{hh}.{kt}", "mv_ones"), w=("pS1",), tag="mAV")
                    em.act(lambda e, hh=hh, gate_e=gate_e, idxsc8=idxsc8: e.activation(
                        gate_e[:], idxsc8[:, hh * 4 + 1:hh * 4 + 4:2], AF.Exp, bias=0.0, scale=-1.0),
                        r=(f"idxsc.{hh}",), w=("gate_e",))
                    em.dve(lambda e, gate_e=gate_e: e.tensor_scalar_add(gate_e[:], gate_e[:], 1.0),
                           r=("gate_e",), w=("gate_e",))
                    em.dve(lambda e, gate_c=gate_c, gate_e=gate_e: e.reciprocal(gate_c[:], gate_e[:]),
                           r=("gate_e",), w=("gate_c",))
                    em.dve(lambda e: e.reciprocal(
                        nrm_s[:, 0:2], pS2[1][:, 64:130:65]),
                        r=("pS1",), w=("mrcp",))
                    for rs in range(2):
                        em.dve(lambda e, rs=rs, mavtm=mavtm, gate_c=gate_c: e.tensor_scalar(
                            mavtm[:, rs * 64:(rs + 1) * 64], pS2[1][:, rs * 65:rs * 65 + 64],
                            nrm_s[:, rs:rs + 1], gate_c[:, rs:rs + 1], op0=ALU.mult, op1=ALU.mult),
                            r=("pS1", "mrcp", "gate_c"), w=("mavtm",))
                    hb2 = (hh % 2) * 64
                    for ns in range(4):
                        for rs in range(2):
                            em.dve(lambda e, rs=rs, ns=ns, hh=hh, oh_rt=oh_rt, idxsc8=idxsc8: e.tensor_single_scalar(
                                oh_rt[rs][:], c_iotar[:, ns * 512:(ns + 1) * 512],
                                idxsc8[:, hh * 4 + rs * 2:hh * 4 + rs * 2 + 1], op=ALU.is_equal),
                                r=("c_iotar", f"idxsc.{hh}"), w=(f"ohrt{rs}",))
                        em.pe(lambda e, ns=ns, mavtm=mavtm, oh_rt=oh_rt: e.matmul(
                            pS2[2][0:64, 0:512], mavtm[:, 0:64], oh_rt[0][:],
                            start=True, stop=False),
                            r=("mavtm", "ohrt0"), w=("pS2",), tag="msc")
                        em.pe(lambda e, ns=ns, mavtm=mavtm, oh_rt=oh_rt: e.matmul(
                            pS2[2][0:64, 0:512], mavtm[:, 64:128], oh_rt[1][:],
                            start=False, stop=True),
                            r=("mavtm", "ohrt1"), w=("pS2",), tag="msc2")
                        em.act(lambda e, ns=ns, hh=hh, hb2=hb2: e.activation(
                            av_mosa[hh // 2][hb2:hb2 + 64, ns * 512:(ns + 1) * 512],
                            pS2[2][0:64, 0:512], AF.Copy),
                            r=("pS2",), w=(f"avm{hh // 2}",))

                m_stage1(0)
                for hh in range(8):
                    if hh + 1 < 8:
                        m_stage1(hh + 1)
                    if hh >= 1:
                        m_stage3(hh - 1)
                    m_stage2(hh)
                m_stage3(7)

            srcs = []
            if en_attn:
                srcs += [(av_attn[p], wo_all, p, (f"ava{p}",), "wo_all") for p in range(8)]
            if en_dense:
                srcs += [(av_dense[p], dwo_all, p, (f"avd{p}",), "dwo_all") for p in range(4)]
            if en_mosa:
                srcs += [(av_mosa[p], mwo_all, p, (f"avm{p}",), "mwo_all") for p in range(4)]
            obanks = [(pS2[0], "pS0"), (pS2[1], "pS1"), (pS2[2], "pS2"),
                      (pav[:, 0:512], "pavb0"), (pav[:, 512:1024], "pavb1")]
            obc = [0]

            def ldx(tt):
                xb, xkey = xt[tt % 2], f"xt{tt % 2}"
                src = x_in[b, tt * 128:(tt + 1) * 128, :]
                em.dma_sp(lambda e, d=xb, s=src: e.dma_start(d[:], s), r=(), w=(xkey,),
                          sem=f"dx{tt % 2}", tag=f"rldx{b}.{tt}")
            ldx(0)
            ldx(1)
            for tt in range(NT):
                xb, xkey = xt[tt % 2], f"xt{tt % 2}"
                ob, obkey = out_sb[tt % 2], f"osb{tt % 2}"
                for ns in range(2):
                    pb, pbk = obanks[obc[0] % 5]
                    obc[0] += 1
                    if srcs:
                        for ki, (av, wsb, p, avkeys, wkey) in enumerate(srcs):
                            em.pe(lambda e, pb=pb, av=av, wsb=wsb, p=p, tt=tt, ns=ns,
                                  ki=ki, nsr=len(srcs): e.matmul(
                                pb[:, 0:512], av[:, tt * 128:(tt + 1) * 128],
                                wsb[:, p * 1024 + ns * 512:p * 1024 + (ns + 1) * 512],
                                start=(ki == 0), stop=(ki == nsr - 1)),
                                r=avkeys + (wkey,), w=(pbk,), tag="oprj")
                        em.dve(lambda e, pb=pb, ob=ob, ns=ns, xb=xb: e.tensor_tensor(
                            ob[:, ns * 512:(ns + 1) * 512], pb[:, 0:512],
                            xb[:, ns * 512:(ns + 1) * 512], op=ALU.add),
                            r=(pbk, xkey), w=(obkey,), tag="oadd")
                    else:
                        em.dve(lambda e, ob=ob, ns=ns, xb=xb: e.tensor_copy(
                            ob[:, ns * 512:(ns + 1) * 512], xb[:, ns * 512:(ns + 1) * 512]),
                            r=(xkey,), w=(obkey,), tag="ocp")
                em.dma_sp(lambda e, ob=ob, d=out_t[b, tt * 128:(tt + 1) * 128, :]: e.dma_start(d, ob[:]),
                          r=(obkey,), w=(), sem="dout", tag=f"stout{b}.{tt}")
                if tt + 2 < NT:
                    ldx(tt + 2)
            em.barrier()
            scB.close()

        if tap_outs:
            for nm, (dram, sb) in tap_outs.items():
                em.dma_sp(lambda e, dram=dram, sb=sb: e.dma_start(dram[:, :], sb[:]),
                          r=(), w=(), sem="dtap", tag="tap:" + nm)
        em.barrier()
        em.finalize_into(nc, sems, dma_sems)
    return nc


_CACHE = {}


def kernel(**inputs):
    x = np.asarray(inputs["L0_attn_x"])
    host = prep_host(inputs)
    consts = host_consts()
    if "full" not in _CACHE:
        _CACHE["full"] = build_nc()
    nc = _CACHE["full"]
    in_maps = []
    for core in range(8):
        m = {"x": np.ascontiguousarray(x[core * NB:(core + 1) * NB])}
        m.update(host)
        m.update(consts)
        in_maps.append(m)
    res = run_bass_kernel_spmd(nc, in_maps, core_ids=list(range(8)))
    out = np.concatenate([np.asarray(r["out"]) for r in res.results], axis=0)
    return out.astype(np.float32)
